# version 21
# baseline (speedup 1.0000x reference)
import numpy as np
from contextlib import ExitStack

import concourse.bass as bass
import concourse.mybir as mybir
from concourse.bass_utils import run_bass_kernel_spmd

F32 = mybir.dt.float32
BF16 = mybir.dt.bfloat16
I32 = mybir.dt.int32
AF = mybir.ActivationFunctionType
ALU = mybir.AluOpType
AX = mybir.AxisListType

NCORES = 8
B, N, D = 16, 2048, 1024
H, DH, M = 16, 64, 64
DFF, HID = 4096, 3072
TPC = B * N // NCORES
NCH = D // 128
NHC = HID // 128

EPS_K = 1e-6
EPS_LN = 1e-6
EPS_OUT = 1e-5

_DN32 = np.float32(1.0) / np.sqrt(np.sqrt(np.float32(64.0)))
DN = float(_DN32)
CDIAG = float((_DN32 * _DN32) / np.float32(2.0))
_S32 = np.float32(1.0 / 127.0)
SCALE_Q = float(_S32)
SCALE_Q2 = float(_S32 * _S32)

ENGS = ("sync", "gpsimd", "tensor", "vector", "scalar")
NCHAN = 10
CHANS = tuple("c%d" % i for i in range(NCHAN))


class Sched:

    def __init__(self, nc, ctx):
        self.nc = nc
        self.sem = {e: ctx.enter_context(nc.semaphore("sem_" + e)) for e in ENGS}
        for ch in CHANS:
            self.sem[ch] = ctx.enter_context(nc.semaphore("sem_" + ch))
        self.cnt = {k: 0 for k in self.sem}
        self.steps = {e: [] for e in ENGS}
        self.pending = {e: [] for e in ENGS}

    def do(self, eng, emit, deps=(), dma=False, n_dma=1, chan=None,
           n_inst=1, inc_last=False):
        waits = {}
        for d in list(deps) + self.pending[eng]:
            if d is None:
                continue
            de, dv = d
            assert not (de == eng and not dma and dv > self.cnt[eng])
            if waits.get(de, 0) < dv:
                waits[de] = dv
        self.pending[eng] = []
        if dma:
            assert chan in CHANS, chan
            self.cnt[chan] += 16 * n_dma
            tok = (chan, self.cnt[chan])
        else:
            self.cnt[eng] += 1 if inc_last else n_inst
            tok = (eng, self.cnt[eng])
        self.steps[eng].append((waits, emit, dma, n_dma, chan, n_inst,
                                inc_last))
        return tok

    def wait_only(self, eng, deps):
        waits = {}
        for d in deps:
            if d is None:
                continue
            de, dv = d
            if waits.get(de, 0) < dv:
                waits[de] = dv
        self.steps[eng].append((waits, None, False, 0, None, 0, False))

    def barrier_tokens(self):
        return [(k, self.cnt[k]) for k in self.sem]

    def phase_start(self):
        toks = self.barrier_tokens()
        for e in ENGS:
            self.pending[e] = list(toks)

    def flush(self, block):
        for e in ENGS:
            steps = self.steps[e]
            if not steps:
                continue
            sems = self.sem

            def run(eng, _steps=steps, _e=e):
                for waits, emit, dma, n_dma, chan, n_inst, inc_last in _steps:
                    for we, wv in waits.items():
                        eng.wait_ge(sems[we], wv)
                    if emit is None:
                        continue
                    insts = emit(eng)
                    if not isinstance(insts, (list, tuple)):
                        insts = [insts]
                    if dma:
                        assert len(insts) == n_dma, (len(insts), n_dma)
                        for i in insts:
                            i.then_inc(sems[chan], 16)
                    elif inc_last:
                        insts[-1].then_inc(sems[_e], 1)
                    else:
                        assert len(insts) == n_inst, (len(insts), n_inst)
                        for i in insts:
                            i.then_inc(sems[_e], 1)

            getattr(block, e)(run)
            self.steps[e] = []


class RW:

    def __init__(self):
        self.w = []
        self.r = []

    def wdep(self):
        return self.r + self.w

    def rdep(self):
        return list(self.w)

    def wrote(self, tok, reset=True):
        if reset:
            self.w = [tok]
            self.r = []
        else:
            self.w.append(tok)

    def read(self, tok):
        self.r.append(tok)


def build_program(debug=(), upto="C2", parts="lkdvzqwt", nt=None, use_cc=True):
    nc = bass.Bass("TRN2", target_bir_lowering=False, debug=False)
    ctx = ExitStack()

    def dram(name, shape, dtype, kind="Internal", **kw):
        if name in debug and kind == "Internal":
            kind = "ExternalOutput"
        return nc.dram_tensor(name, shape, dtype, kind=kind, **kw).ap()

    xT = dram("xT", [D, TPC], F32, kind="ExternalInput")
    wq = dram("wq", [D, D], F32, kind="ExternalInput")
    wk = dram("wk", [D, D], F32, kind="ExternalInput")
    wv = dram("wv", [D, D], F32, kind="ExternalInput")
    wo = dram("wo", [D, D], F32, kind="ExternalInput")
    projbd = dram("projbd", [128, 128], F32, kind="ExternalInput")
    w1 = dram("w1", [D, DFF], F32, kind="ExternalInput")
    w2 = dram("w2", [DFF, D], F32, kind="ExternalInput")
    b1c = dram("b1c", [128, DFF // 128], F32, kind="ExternalInput")
    b2c = dram("b2c", [128, NCH], F32, kind="ExternalInput")
    g1c = dram("g1c", [128, NCH], F32, kind="ExternalInput")
    b1lc = dram("b1lc", [128, NCH], F32, kind="ExternalInput")
    g2c = dram("g2c", [128, NCH], F32, kind="ExternalInput")
    b2lc = dram("b2lc", [128, NCH], F32, kind="ExternalInput")
    wgT = dram("wgT", [D, HID], I32, kind="ExternalInput")
    wuT = dram("wuT", [D, HID], I32, kind="ExternalInput")
    wdT = dram("wdT", [HID, D], I32, kind="ExternalInput")
    eye = dram("eye", [128, 128], F32, kind="ExternalInput")
    ones1r = dram("ones1r", [1, 128], F32, kind="ExternalInput")
    avgc = dram("avgc", [128, 1], F32, kind="ExternalInput")
    blockneg = dram("blockneg", [128, 128], F32, kind="ExternalInput")
    maskeo = dram("maskeo", [1, 256], F32, kind="ExternalInput")

    yT = dram("yT", [D, TPC], F32, kind="ExternalOutput")

    xnh_h = dram("xnh_h", [D, TPC], BF16)
    xnl_h = dram("xnl_h", [D, TPC], BF16)
    o1h_h = dram("o1h_h", [D, TPC], BF16)
    o1l_h = dram("o1l_h", [D, TPC], BF16)
    aout_h = dram("aout_h", [D, TPC], F32)
    hh_h = dram("hh_h", [DFF, TPC], BF16)
    hl_h = dram("hl_h", [DFF, TPC], BF16)
    gate_h = dram("gate_h", [HID, TPC], F32)
    hsh_h = dram("hsh_h", [HID, TPC], BF16)
    hsl_h = dram("hsl_h", [HID, TPC], BF16)
    ccin = dram("ccin", [1, 16], F32)
    ccout = dram("ccout", [1, 16], F32, addr_space="Shared")
    dbg_kv = dram("dbg_kv", [128, 32 * 65], F32)
    dbg_mx = dram("dbg_mx", [1, 16], F32)
    dbg_qf = dram("dbg_qf", [128, 4 * H * 65], F32)
    dbg_qft = dram("dbg_qft", [128, 512], F32)
    dbg_den = dram("dbg_den", [1, 2048], F32)
    dbg_lmx = dram("dbg_lmx", [1, 144], F32)
    dbg_aend = dram("dbg_aend", [128, 8448], F32)
    dbg_kd = dram("dbg_kd", [128, 148], F32)
    dbg_gmax = dram("dbg_gmax", [128, 16], F32)
    dbg_attn = dram("dbg_attn", [128, NCH * 512], F32)

    S = Sched(nc, ctx)
    block = ctx.enter_context(nc.Block())

    PS = [ctx.enter_context(nc.psum_tensor("ps%d" % i, [128, 512], F32))
          for i in range(8)]
    PB = [RW() for _ in range(8)]

    def sb(name, shape, dtype=F32, stack=ctx):
        return stack.enter_context(nc.sbuf_tensor(name, shape, dtype))

    eye_t = sb("eye_t", [128, 128])
    ones1_t = sb("ones1_t", [1, 128])
    avgc_t = sb("avgc_t", [128, 1])
    blockneg_t = sb("blockneg_t", [128, 128])
    maskeo_t = sb("maskeo_t", [1, 256])
    projbd_t = sb("projbd_t", [128, 128])
    kv_acc = sb("kv_acc", [128, 32 * 65])
    gmax = sb("gmax", [128, 16])
    maxscr = sb("maxscr", [128, 4])
    mxrow = sb("mxrow", [1, 144])
    mx_sb = sb("mx_sb", [1, 16])
    scale_col = sb("scale_col", [128, 1])
    stat_sb = sb("stat_sb", [1, 4 * 512])
    g1_t = sb("g1_t", [128, NCH])
    b1l_t = sb("b1l_t", [128, NCH])
    g2_t = sb("g2_t", [128, NCH])
    b2l_t = sb("b2l_t", [128, NCH])
    b2_t = sb("b2_t", [128, NCH])
    b1_t = sb("b1_t", [128, DFF // 128])
    nb1_t = sb("nb1_t", [128, DFF // 128])

    r_stat = RW()
    r_kv = RW()
    r_gmax = RW()
    r_maxscr = RW()
    r_scol = RW()
    r_mx = RW()
    r_mxrow = RW()

    tok_consts = S.do("sync", lambda eng: [
        eng.dma_start(out=eye_t[:], in_=eye[:]),
        eng.dma_start(out=ones1_t[:], in_=ones1r[:]),
        eng.dma_start(out=avgc_t[:], in_=avgc[:]),
        eng.dma_start(out=blockneg_t[:], in_=blockneg[:]),
        eng.dma_start(out=maskeo_t[:], in_=maskeo[:]),
        eng.dma_start(out=projbd_t[:], in_=projbd[:]),
        eng.dma_start(out=g1_t[:], in_=g1c[:]),
        eng.dma_start(out=b1l_t[:], in_=b1lc[:]),
        eng.dma_start(out=g2_t[:], in_=g2c[:]),
        eng.dma_start(out=b2l_t[:], in_=b2lc[:]),
        eng.dma_start(out=b2_t[:], in_=b2c[:]),
        eng.dma_start(out=b1_t[:], in_=b1c[:]),
    ], dma=True, n_dma=12, chan="c0")

    tok_init = S.do("vector", lambda eng: [
        eng.tensor_scalar(out=projbd_t[:], in0=projbd_t[:], scalar1=DN,
                          scalar2=None, op0=ALU.mult),
        eng.memset(kv_acc[:], 0.0),
        eng.memset(gmax[:], -3.0e38),
        eng.tensor_scalar(out=nb1_t[:], in0=b1_t[:], scalar1=-1.0,
                          scalar2=None, op0=ALU.mult),
    ], deps=[tok_consts], n_inst=4)
    r_kv.wrote(tok_init)
    r_gmax.wrote(tok_init)

    if "dbg_init" in debug:
        t = S.do("gpsimd", lambda eng: eng.dma_start(
            out=dbg_gmax[:], in_=gmax[:]), deps=[tok_init], dma=True,
            chan="c9")
        r_gmax.read(t)

    def ln_stats(tile_buf, tile_rw, sq_bufs, sq_rws, T, pmean_i, pmsq_i,
                 pmu_i, prs_i, chunk_toks, eps):
        mm_last = None
        for chh in range(NCH):
            sqi = chh % 2
            tok_sq = S.do("scalar", lambda eng, _c=chh, _s=sqi: eng.activation(
                out=sq_bufs[_s][:, 0:T],
                in_=tile_buf[:, _c * T:(_c + 1) * T], func=AF.Square),
                deps=[chunk_toks[chh]] + sq_rws[sqi].wdep())
            sq_rws[sqi].wrote(tok_sq)
            tile_rw.read(tok_sq)

            tok_mm = S.do("tensor", lambda eng, _c=chh, _s=sqi: [
                eng.matmul(PS[pmean_i][0:1, 0:T], avgc_t[:],
                           tile_buf[:, _c * T:(_c + 1) * T],
                           start=(_c == 0), stop=(_c == NCH - 1)),
                eng.matmul(PS[pmsq_i][0:1, 0:T], avgc_t[:],
                           sq_bufs[_s][:, 0:T], start=(_c == 0),
                           stop=(_c == NCH - 1)),
            ], deps=[chunk_toks[chh], tok_sq, tok_consts] +
                (PB[pmean_i].wdep() + PB[pmsq_i].wdep() if chh == 0 else []),
                n_inst=2)
            sq_rws[sqi].read(tok_mm)
            tile_rw.read(tok_mm)
            mm_last = tok_mm
        PB[pmean_i].wrote(mm_last)
        PB[pmsq_i].wrote(mm_last)

        t1 = S.do("vector", lambda eng: [
            eng.tensor_copy(out=stat_sb[0:1, 0:T], in_=PS[pmean_i][0:1, 0:T]),
            eng.tensor_copy(out=stat_sb[0:1, 512:512 + T],
                            in_=PS[pmsq_i][0:1, 0:T]),
        ], deps=[mm_last] + r_stat.wdep(), n_inst=2)
        r_stat.wrote(t1)
        PB[pmean_i].read(t1)
        PB[pmsq_i].read(t1)
        t2 = S.do("vector", lambda eng: eng.tensor_mul(
            stat_sb[0:1, 1024:1024 + T], stat_sb[0:1, 0:T],
            stat_sb[0:1, 0:T]), deps=[t1])
        r_stat.wrote(t2, reset=False)
        t3 = S.do("vector", lambda eng: eng.tensor_sub(
            stat_sb[0:1, 512:512 + T], stat_sb[0:1, 512:512 + T],
            stat_sb[0:1, 1024:1024 + T]), deps=[t2])
        r_stat.wrote(t3, reset=False)
        t4 = S.do("vector", lambda eng, _e=eps: eng.tensor_scalar(
            out=stat_sb[0:1, 512:512 + T], in0=stat_sb[0:1, 512:512 + T],
            scalar1=_e, scalar2=None, op0=ALU.add), deps=[t3])
        r_stat.wrote(t4, reset=False)
        t5 = S.do("scalar", lambda eng: eng.activation(
            out=stat_sb[0:1, 1024:1024 + T], in_=stat_sb[0:1, 512:512 + T],
            func=AF.Sqrt), deps=[t4])
        r_stat.wrote(t5, reset=False)
        t6 = S.do("vector", lambda eng: eng.reciprocal(
            out=stat_sb[0:1, 1536:1536 + T], in_=stat_sb[0:1, 1024:1024 + T]),
            deps=[t5])
        r_stat.wrote(t6, reset=False)

        tok_bmu = S.do("tensor", lambda eng: eng.matmul(
            PS[pmu_i][:, 0:T], ones1_t[:], stat_sb[0:1, 0:T],
            start=True, stop=True), deps=[t6] + PB[pmu_i].wdep())
        PB[pmu_i].wrote(tok_bmu)
        r_stat.read(tok_bmu)
        tok_brs = S.do("tensor", lambda eng: eng.matmul(
            PS[prs_i][:, 0:T], ones1_t[:], stat_sb[0:1, 1536:1536 + T],
            start=True, stop=True), deps=[t6] + PB[prs_i].wdep())
        PB[prs_i].wrote(tok_brs)
        r_stat.read(tok_brs)
        return tok_bmu, tok_brs

    def ln_norm_chunk(out_buf, out_rw, out_off, z_buf, z_rw, z_off, T,
                      pmu_i, prs_i, tok_bmu, tok_brs, gcol, bcol, first):
        n1 = S.do("vector", lambda eng: eng.tensor_sub(
            out_buf[:, out_off:out_off + T], z_buf[:, z_off:z_off + T],
            PS[pmu_i][:, 0:T]),
            deps=[tok_bmu, tok_brs] + z_rw.rdep() +
            (out_rw.wdep() if first else []))
        out_rw.wrote(n1, reset=first)
        z_rw.read(n1)
        PB[pmu_i].read(n1)
        n2 = S.do("vector", lambda eng: eng.tensor_mul(
            out_buf[:, out_off:out_off + T], out_buf[:, out_off:out_off + T],
            PS[prs_i][:, 0:T]), deps=[n1])
        out_rw.wrote(n2, reset=False)
        PB[prs_i].read(n2)
        last = n2
        if gcol is not None:
            n3 = S.do("scalar", lambda eng: eng.activation(
                out=out_buf[:, out_off:out_off + T],
                in_=out_buf[:, out_off:out_off + T],
                func=AF.Identity, bias=bcol, scale=gcol), deps=[n2])
            out_rw.wrote(n3, reset=False)
            last = n3
        return last

    def ld3(eng, sbuf_t, hbm_ap, t0, T):
        dst = sbuf_t[:].rearrange("p (c t) -> p c t", t=T)
        srcr = hbm_ap.rearrange("(c p) n -> p c n", p=128)[:, :, t0:t0 + T]
        return eng.dma_start(out=dst, in_=srcr)

    def st3(eng, hbm_ap, sbuf_t, t0, T):
        dstr = hbm_ap.rearrange("(c p) n -> p c n", p=128)[:, :, t0:t0 + T]
        srcs = sbuf_t[:].rearrange("p (c t) -> p c t", t=T)
        return eng.dma_start(out=dstr, in_=srcs)

    def split3s(srcap, hap, lap, tmpap, deps):
        t1 = S.do("scalar", lambda eng: eng.activation(
            out=hap, in_=srcap, func=AF.Copy), deps=deps)
        t2 = S.do("vector", lambda eng: eng.tensor_sub(tmpap, srcap, hap),
                  deps=[t1])
        t3 = S.do("vector", lambda eng: eng.tensor_copy(out=lap, in_=tmpap),
                  deps=[t2])
        return t3

    TA = 512
    NT = nt if nt is not None else TPC // TA

    phA = ExitStack()
    wkh = sb("wkh", [128, NCH * D], BF16, stack=phA)
    wkl = sb("wkl", [128, NCH * D], BF16, stack=phA)
    wvh = sb("wvh", [128, NCH * D], BF16, stack=phA)
    wvl = sb("wvl", [128, NCH * D], BF16, stack=phA)
    wstg = [sb("wstg%d" % i, [128, D // 2], stack=phA) for i in range(2)]
    wtmp = [sb("wtmp%d" % i, [128, D // 2], stack=phA) for i in range(2)]
    xb = [sb("xb0", [128, NCH * TA], stack=phA)]
    xnh = [sb("xnh%d" % i, [128, NCH * TA], BF16, stack=phA) for i in range(2)]
    xnl = [sb("xnl%d" % i, [128, NCH * TA], BF16, stack=phA) for i in range(2)]
    xtc = [sb("xtc%d" % i, [128, TA], stack=phA) for i in range(2)]
    kfm = sb("kfm", [128, NCH * TA], stack=phA)
    sqb = [sb("sqb%d" % i, [128, TA], stack=phA) for i in range(2)]
    kdneg = sb("kdneg", [16, TA], stack=phA)
    kdT = sb("kdT", [128, 4 * 16], stack=phA)
    ek_ext = sb("ek_ext", [128, 4 * H * 65], stack=phA)
    v_ext = sb("v_ext", [128, 4 * H * 65], stack=phA)
    kdtm = [sb("kdtm%d" % i, [128, 128], stack=phA) for i in range(2)]

    r_xb = [RW()]
    r_xnh = [RW(), RW()]
    r_xnl = [RW(), RW()]
    r_xtc = [RW(), RW()]
    r_wstg = [RW(), RW()]
    r_wtmp = [RW(), RW()]
    r_wk = RW()
    r_wv = RW()
    r_kfm = RW()
    r_sq = [RW(), RW()]
    r_kdneg = RW()
    r_kdT = RW()
    r_ek = RW()
    r_v = RW()
    r_kdtm = [RW(), RW()]

    def ld_split_w(srct, dsth, dstl, chan, rw):
        last = None
        HDW = D // 2
        for i in range(2 * NCH):
            dd, hv = i // 2, i % 2
            sg = i % 2
            tok_l = S.do("sync", lambda eng, _d=dd, _h=hv, _s=sg:
                         eng.dma_start(
                out=wstg[_s][:],
                in_=srct[_d * 128:(_d + 1) * 128, _h * HDW:(_h + 1) * HDW]),
                deps=r_wstg[sg].wdep(), dma=True, chan=chan)
            r_wstg[sg].wrote(tok_l)
            tok_s = split3s(
                wstg[sg][:],
                dsth[:, dd * D + hv * HDW:dd * D + (hv + 1) * HDW],
                dstl[:, dd * D + hv * HDW:dd * D + (hv + 1) * HDW],
                wtmp[sg][:],
                [tok_l] + r_wtmp[sg].wdep() + (rw.wdep() if i == 0 else []))
            r_wstg[sg].read(tok_s)
            r_wtmp[sg].wrote(tok_s)
            rw.wrote(tok_s, reset=(i == 0))
            last = tok_s
        return last

    tok_wk = ld_split_w(wk, wkh, wkl, "c1", r_wk)
    tok_wv = ld_split_w(wv, wvh, wvl, "c2", r_wv)

    tok_ones = S.do("vector", lambda eng: [
        eng.memset(ek_ext.rearrange("p (a c) -> p a c", c=65)[:, :, 64:65], 1.0),
        eng.memset(v_ext.rearrange("p (a c) -> p a c", c=65)[:, :, 64:65], 1.0),
    ], n_inst=2)
    r_ek.wrote(tok_ones)
    r_v.wrote(tok_ones)

    for it in range(NT):
        bf = it % 2
        t0 = it * TA
        slot0 = (it // 4) * 16

        tok_x = S.do("sync", lambda eng, _t0=t0: ld3(
            eng, xb[0], xT, _t0, TA), deps=r_xb[0].wdep(), dma=True, n_dma=1,
            chan="c3" if bf == 0 else "c4")
        r_xb[0].wrote(tok_x)
        xtoks = [tok_x] * NCH

        tok_bmu, tok_brs = ln_stats(xb[0], r_xb[0], sqb, r_sq, TA,
                                    6, 7, 4, 5, xtoks, EPS_OUT)

        xn_toks = []
        for chh in range(NCH):
            tb = chh % 2
            t = ln_norm_chunk(xtc[tb], r_xtc[tb], 0, xb[0], r_xb[0],
                              chh * TA, TA, 4, 5, tok_bmu, tok_brs,
                              None, None, first=True)
            tsp = split3s(xtc[tb][:, 0:TA],
                          xnh[bf][:, chh * TA:(chh + 1) * TA],
                          xnl[bf][:, chh * TA:(chh + 1) * TA],
                          sqb[tb][:, 0:TA],
                          [t] + r_sq[tb].wdep() +
                          (r_xnh[bf].wdep() + r_xnl[bf].wdep()
                           if chh == 0 else []))
            r_xtc[tb].read(tsp)
            r_sq[tb].wrote(tsp)
            r_xnh[bf].wrote(tsp, reset=(chh == 0))
            r_xnl[bf].wrote(tsp, reset=(chh == 0))
            xn_toks.append(tsp)

        tok_sp = S.do("gpsimd", lambda eng, _t0=t0, _b=bf: [
            st3(eng, xnh_h, xnh[_b], _t0, TA),
            st3(eng, xnl_h, xnl[_b], _t0, TA)],
            deps=xn_toks, dma=True, n_dma=2,
            chan="c5" if bf == 0 else "c6")
        r_xnh[bf].read(tok_sp)
        r_xnl[bf].read(tok_sp)

        kev_toks = []
        kd_last = None
        for kch in (range(NCH) if "k" in parts else []):
            pi = kch % 2
            tok_k = S.do("tensor", lambda eng, _k=kch, _b=bf: [
                eng.matmul(PS[_k % 2][:, 0:TA],
                           wop[:, dd * D + _k * 128: dd * D + (_k + 1) * 128],
                           xop[:, dd * TA:(dd + 1) * TA],
                           start=(pp == 0 and dd == 0),
                           stop=(pp == 2 and dd == NCH - 1))
                for pp, (wop, xop) in enumerate(
                    ((wkh, xnh[_b]), (wkh, xnl[_b]), (wkl, xnh[_b])))
                for dd in range(NCH)],
                deps=[tok_wk] + xn_toks + PB[pi].wdep(), n_inst=24)
            PB[pi].wrote(tok_k)
            r_xnh[bf].read(tok_k)
            r_xnl[bf].read(tok_k)
            r_wk.read(tok_k)

            tok_kev = S.do("vector", lambda eng, _k=kch: eng.tensor_copy(
                out=kfm[:, _k * TA:(_k + 1) * TA], in_=PS[_k % 2][:, 0:TA]),
                deps=[tok_k] + (r_kfm.wdep() if kch == 0 else []))
            PB[pi].read(tok_kev)
            kev_toks.append(tok_kev)

            if "q" not in parts:
                kd_last = tok_kev
                continue
            tok_ksq = S.do("scalar", lambda eng, _k=kch: eng.activation(
                out=sqb[_k % 2][:, 0:TA], in_=PS[_k % 2][:, 0:TA],
                func=AF.Square), deps=[tok_k, tok_kev] + r_sq[kch % 2].wdep())
            r_sq[kch % 2].wrote(tok_ksq)
            PB[pi].read(tok_ksq)

            if "w" not in parts:
                kd_last = tok_ksq
                continue
            tok_kdmm = S.do("tensor", lambda eng, _k=kch: eng.matmul(
                PS[6][0:16, 0:TA], blockneg_t[:, _k * 16:(_k + 1) * 16],
                sqb[_k % 2][:, 0:TA], start=(_k == 0), stop=(_k == NCH - 1)),
                deps=[tok_ksq] + (PB[6].wdep() if kch == 0 else []))
            r_sq[kch % 2].read(tok_kdmm)
            kd_last = tok_kdmm
        if "k" not in parts:
            continue
        r_kfm.wrote(kev_toks[0])
        for t in kev_toks[1:]:
            r_kfm.wrote(t, reset=False)
        if "q" not in parts or "w" not in parts:
            continue
        PB[6].wrote(kd_last)

        tok_kdev = S.do("vector", lambda eng: eng.tensor_copy(
            out=kdneg[:], in_=PS[6][0:16, 0:TA]),
            deps=[kd_last] + r_kdneg.wdep())
        r_kdneg.wrote(tok_kdev)
        PB[6].read(tok_kdev)

        if "t" not in parts:
            continue
        tok_ktr = S.do("tensor", lambda eng: [
            eng.transpose(PS[7][:, s * 16:(s + 1) * 16],
                          kdneg[:, s * 128:(s + 1) * 128], eye_t[0:16, 0:16])
            for s in range(4)], deps=[tok_kdev, tok_consts] + PB[7].wdep(),
            n_inst=4)
        PB[7].wrote(tok_ktr)
        r_kdneg.read(tok_ktr)
        tok_ktev = S.do("vector", lambda eng: eng.tensor_copy(
            out=kdT[:], in_=PS[7][:, 0:64]), deps=[tok_ktr] + r_kdT.wdep())
        r_kdT.wrote(tok_ktev)
        PB[7].read(tok_ktev)

        ek_toks = {}
        first_ek = True
        for j in (range(8) if "d" in parts else []):
            for s in range(4):
                pi = 2 + ((j * 4 + s) % 2)
                tok_kd = S.do("tensor", lambda eng, _j=j, _s=s, _p=pi: eng.matmul(
                    PS[_p][:, 0:128],
                    kfm[:, _j * TA + _s * 128: _j * TA + (_s + 1) * 128],
                    projbd_t[:], start=True, stop=True),
                    deps=[kev_toks[j], tok_init] + PB[pi].wdep())
                PB[pi].wrote(tok_kd)
                r_kfm.read(tok_kd)

                ti = (j * 4 + s) % 2
                tok_cp = S.do("vector", lambda eng, _p=pi, _t=ti:
                              eng.tensor_copy(out=kdtm[_t][:],
                                              in_=PS[_p][:, 0:128]),
                    deps=[tok_kd] + r_kdtm[ti].wdep())
                r_kdtm[ti].wrote(tok_cp)
                PB[pi].read(tok_cp)

                tok_red = S.do("vector", lambda eng, _t=ti: [
                    eng.reduce_max(out=maxscr[:, 0:1], in_=kdtm[_t][:, 0:64],
                                   axis=AX.X),
                    eng.reduce_max(out=maxscr[:, 1:2], in_=kdtm[_t][:, 64:128],
                                   axis=AX.X),
                ], deps=[tok_cp] + r_maxscr.wdep(), n_inst=2)
                r_maxscr.wrote(tok_red)
                r_kdtm[ti].read(tok_red)

                tok_gm = S.do("vector", lambda eng, _j=j: [
                    eng.tensor_max(gmax[:, 2 * _j:2 * _j + 1],
                                   gmax[:, 2 * _j:2 * _j + 1],
                                   maxscr[:, 0:1]),
                    eng.tensor_max(gmax[:, 2 * _j + 1:2 * _j + 2],
                                   gmax[:, 2 * _j + 1:2 * _j + 2],
                                   maxscr[:, 1:2]),
                ], deps=[tok_red] + r_gmax.wdep(), n_inst=2)
                r_gmax.wrote(tok_gm, reset=False)
                r_maxscr.read(tok_gm)


                tok_ek = S.do("scalar", lambda eng, _j=j, _s=s, _t=ti: [
                    eng.activation(
                        out=ek_ext[:, _s * (H * 65) + (2 * _j + hh) * 65:
                                   _s * (H * 65) + (2 * _j + hh) * 65 + 64],
                        in_=kdtm[_t][:, hh * 64:(hh + 1) * 64],
                        func=AF.Exp,
                        bias=kdT[:, _s * 16 + 2 * _j + hh:
                                 _s * 16 + 2 * _j + hh + 1],
                        scale=1.0)
                    for hh in range(2)],
                    deps=[tok_cp, tok_ktev] +
                    (r_ek.wdep() if first_ek else []), n_inst=2)
                first_ek = False
                r_kdT.read(tok_ek)
                r_kdtm[ti].read(tok_ek)
                ek_toks[(j, s)] = tok_ek
        if "d" in parts:
            r_ek.wrote(ek_toks[(0, 0)])
            for key, t in ek_toks.items():
                if key != (0, 0):
                    r_ek.wrote(t, reset=False)

        v_toks = []
        firstv = True
        for s in (range(4) if "v" in parts else []):
            for half in range(2):
                pi = 4 + ((s * 2 + half) % 2)
                tok_v = S.do("tensor", lambda eng, _s=s, _h=half, _b=bf,
                             _p=pi: [
                    eng.matmul(PS[_p][:, 0:512],
                               xop[:, dd * TA + _s * 128:
                                   dd * TA + (_s + 1) * 128],
                               wop[:, dd * D + _h * 512:
                                    dd * D + (_h + 1) * 512],
                               start=(pp == 0 and dd == 0),
                               stop=(pp == 2 and dd == NCH - 1))
                    for pp, (xop, wop) in enumerate(
                        ((xnh[_b], wvh), (xnh[_b], wvl), (xnl[_b], wvh)))
                    for dd in range(NCH)],
                    deps=[tok_wv] + xn_toks + PB[pi].wdep(), n_inst=24)
                PB[pi].wrote(tok_v)
                r_xnh[bf].read(tok_v)
                r_xnl[bf].read(tok_v)
                r_wv.read(tok_v)

                tok_vev = S.do("vector", lambda eng, _s=s, _h=half, _p=pi:
                               eng.tensor_copy(
                    out=v_ext.rearrange("p (s h c) -> p s h c", s=4, c=65)[
                        :, _s, _h * 8:(_h + 1) * 8, 0:64],
                    in_=PS[_p][:, 0:512].rearrange("p (h c) -> p h c", c=64)),
                    deps=[tok_v] + (r_v.wdep() if firstv else []))
                firstv = False
                PB[pi].read(tok_vev)
                v_toks.append(tok_vev)
        if "v" in parts:
            r_v.wrote(v_toks[0])
            for t in v_toks[1:]:
                r_v.wrote(t, reset=False)

        for h in (range(H) if "z" in parts else []):
            pi = 6 + (h % 2)
            tok_kv = S.do("tensor", lambda eng, _h=h, _p=pi: [
                eng.matmul(PS[_p][0:65, 0:65],
                           ek_ext[:, s * (H * 65) + _h * 65:
                                  s * (H * 65) + _h * 65 + 65],
                           v_ext[:, s * (H * 65) + _h * 65:
                                 s * (H * 65) + _h * 65 + 65],
                           start=(s == 0), stop=(s == 3))
                for s in range(4)],
                deps=[ek_toks[(h // 2, sx)] for sx in range(4)] + v_toks +
                PB[pi].wdep() + [tok_ones], n_inst=4)
            PB[pi].wrote(tok_kv)
            r_ek.read(tok_kv)
            r_v.read(tok_kv)

            tok_acc = S.do("vector", lambda eng, _h=h, _p=pi, _sl=slot0:
                           eng.tensor_add(
                kv_acc[0:65, (_sl + _h) * 65:(_sl + _h) * 65 + 65],
                kv_acc[0:65, (_sl + _h) * 65:(_sl + _h) * 65 + 65],
                PS[_p][0:65, 0:65]), deps=[tok_kv] + r_kv.wdep())
            r_kv.wrote(tok_acc, reset=False)
            PB[pi].read(tok_acc)

    if upto == "A0":
        S.wait_only("gpsimd", S.barrier_tokens())
        S.flush(block)
        phA.close()
        ctx.close()
        return nc

    tok_gred = S.do("vector", lambda eng: eng.reduce_max(
        out=maxscr[:, 2:3], in_=gmax[:], axis=AX.X),
        deps=r_gmax.wdep() + r_maxscr.wdep())
    r_maxscr.wrote(tok_gred)
    r_gmax.read(tok_gred)

    tok_gtr = S.do("tensor", lambda eng: eng.transpose(
        PS[7][0:1, 0:128], maxscr[:, 2:3], eye_t[:]),
        deps=[tok_gred] + PB[7].wdep())
    PB[7].wrote(tok_gtr)
    r_maxscr.read(tok_gtr)

    g1tok = S.do("vector", lambda eng: eng.tensor_copy(
        out=mxrow[0:1, 0:128], in_=PS[7][0:1, 0:128]),
        deps=[tok_gtr] + r_mxrow.wdep())
    r_mxrow.wrote(g1tok)
    PB[7].read(g1tok)
    g2tok = S.do("vector", lambda eng: eng.reduce_max(
        out=mxrow[0:1, 128:129], in_=mxrow[0:1, 0:128], axis=AX.X),
        deps=[g1tok])
    r_mxrow.wrote(g2tok, reset=False)
    g3tok = S.do("vector", lambda eng: eng.memset(mx_sb[0:1, 0:16], 1.0),
                 deps=r_mx.wdep())
    r_mx.wrote(g3tok)
    g4tok = S.do("vector", lambda eng: eng.tensor_scalar(
        out=mx_sb[0:1, 0:16], in0=mx_sb[0:1, 0:16],
        scalar1=mxrow[0:1, 128:129], scalar2=None, op0=ALU.mult),
        deps=[g2tok, g3tok])
    r_mx.wrote(g4tok, reset=False)
    r_mxrow.read(g4tok)

    tok_ccs = S.do("gpsimd", lambda eng: eng.dma_start(
        out=ccin[:], in_=mx_sb[0:1, 0:16]), deps=[g4tok], dma=True,
        chan="c7")
    r_mx.read(tok_ccs)

    if use_cc:
        tok_cc = S.do("gpsimd", lambda eng: eng.collective_compute(
            "AllReduce", ALU.max, replica_groups=[[0, 1, 2, 3, 4, 5, 6, 7]],
            ins=[ccin[:]], outs=[ccout[:]]), deps=[tok_ccs], inc_last=True)
    else:
        tok_cc = tok_ccs

    tok_mxl = S.do("sync", lambda eng: eng.dma_start(
        out=mx_sb[0:1, 0:16], in_=(ccout[:] if use_cc else ccin[:])),
        deps=[tok_cc] + r_mx.wdep(), dma=True, chan="c8")
    r_mx.wrote(tok_mxl)

    if "dbg_mx" in debug:
        t = S.do("gpsimd", lambda eng: eng.dma_start(
            out=dbg_mx[:], in_=mx_sb[0:1, 0:16]), deps=[tok_mxl], dma=True,
            chan="c9")
        r_mx.read(t)
    if "dbg_aend" in debug:
        t = S.do("gpsimd", lambda eng: eng.dma_start(
            out=dbg_aend[:, 0:NCH * 512], in_=kfm[:]), deps=S.barrier_tokens()[:0] or [tok_gred], dma=True, chan="c9")
        r_kfm.read(t)
        t = S.do("gpsimd", lambda eng: eng.dma_start(
            out=dbg_aend[:, 4096:4096 + 128], in_=projbd_t[:]), deps=[tok_gred], dma=True, chan="c9")
        t = S.do("gpsimd", lambda eng: eng.dma_start(
            out=dbg_aend[:, 4224:4224 + 64], in_=kdT[:]), deps=[tok_gred], dma=True, chan="c9")
        r_kdT.read(t)
        t = S.do("gpsimd", lambda eng: eng.dma_start(
            out=dbg_aend[:, 4288:4288 + 4160], in_=ek_ext[:]), deps=[tok_gred], dma=True, chan="c9")
        r_ek.read(t)
    if "dbg_kdtm" in debug:
        t = S.do("gpsimd", lambda eng: eng.dma_start(
            out=dbg_kd[:, 0:128], in_=kdtm[1][:]), deps=[tok_gred], dma=True,
            chan="c9")
        r_kdtm[1].read(t)
        t = S.do("gpsimd", lambda eng: eng.dma_start(
            out=dbg_kd[:, 128:132], in_=maxscr[:]), deps=[tok_gred], dma=True,
            chan="c9")
        r_maxscr.read(t)
    if "dbg_lmx" in debug:
        t = S.do("gpsimd", lambda eng: eng.dma_start(
            out=dbg_lmx[:], in_=mxrow[0:1, 0:144]), deps=[g4tok], dma=True,
            chan="c9")
        r_mxrow.read(t)
        t = S.do("gpsimd", lambda eng: eng.dma_start(
            out=dbg_gmax[:], in_=gmax[:]), deps=[tok_gred], dma=True,
            chan="c9")
        r_gmax.read(t)

    tok_neg = S.do("vector", lambda eng: eng.tensor_scalar(
        out=mx_sb[0:1, 1:2], in0=mx_sb[0:1, 0:1], scalar1=-1.0, scalar2=None,
        op0=ALU.mult), deps=[tok_mxl])
    r_mx.wrote(tok_neg, reset=False)
    tok_c = S.do("scalar", lambda eng: eng.activation(
        out=mx_sb[0:1, 2:3], in_=mx_sb[0:1, 1:2], func=AF.Exp),
        deps=[tok_neg])
    r_mx.wrote(tok_c, reset=False)

    tok_bcc = S.do("tensor", lambda eng: eng.matmul(
        PS[7][:, 0:1], ones1_t[:], mx_sb[0:1, 2:3], start=True, stop=True),
        deps=[tok_c] + PB[7].wdep())
    PB[7].wrote(tok_bcc)
    r_mx.read(tok_bcc)

    sc1 = S.do("vector", lambda eng: eng.tensor_copy(
        out=scale_col[:], in_=PS[7][:, 0:1]), deps=[tok_bcc] + r_scol.wdep())
    r_scol.wrote(sc1)
    PB[7].read(sc1)
    sc2 = S.do("vector", lambda eng: eng.memset(scale_col[64:65, 0:1], EPS_K),
               deps=[sc1])
    r_scol.wrote(sc2, reset=False)

    tok_mp = S.do("vector", lambda eng: [
        eng.tensor_scalar(out=kv_acc[0:65, sl * 65:sl * 65 + 65],
                          in0=kv_acc[0:65, sl * 65:sl * 65 + 65],
                          scalar1=scale_col[0:65, 0:1], scalar2=None,
                          op0=ALU.mult)
        for sl in range(32)], deps=[sc2] + r_kv.wdep(), n_inst=32)
    r_kv.wrote(tok_mp)
    r_scol.read(tok_mp)

    if "dbg_kv" in debug:
        t = S.do("gpsimd", lambda eng: eng.dma_start(
            out=dbg_kv[:], in_=kv_acc[:]), deps=[tok_mp], dma=True, chan="c9")
        r_kv.read(t)

    S.flush(block)
    phA.close()
    S.phase_start()

    if upto == "A":
        S.wait_only("gpsimd", S.barrier_tokens())
        S.flush(block)
        ctx.close()
        return nc

    phB1 = ExitStack()
    wqh = sb("wqh", [128, NCH * D], BF16, stack=phB1)
    wql = sb("wql", [128, NCH * D], BF16, stack=phB1)
    woh = sb("woh", [128, NCH * D], BF16, stack=phB1)
    wol = sb("wol", [128, NCH * D], BF16, stack=phB1)
    wstg1 = [sb("wstg1_0", [128, D // 2], stack=phB1)]
    wtmp1 = [sb("wtmp1_0", [128, D // 2], stack=phB1)]
    xn1h = sb("xn1h", [128, NCH * TA], BF16, stack=phB1)
    xn1l = sb("xn1l", [128, NCH * TA], BF16, stack=phB1)
    attnh = sb("attnh", [128, NCH * TA], BF16, stack=phB1)
    attnl = sb("attnl", [128, NCH * TA], BF16, stack=phB1)
    tmpb1 = sb("tmpb1", [128, TA], stack=phB1)
    qfm = sb("qfm", [128, NCH * TA], stack=phB1)
    sqb1 = [sb("sqb1_%d" % i, [128, TA], stack=phB1) for i in range(2)]
    qdneg = sb("qdneg", [16, TA], stack=phB1)
    qdT = sb("qdT", [128, 4 * 16], stack=phB1)
    qf_ext = sb("qf_ext", [128, 4 * H * 65], stack=phB1)
    qfTe = sb("qfTe", [128, 512], stack=phB1)
    qfTo = sb("qfTo", [128, 512], stack=phB1)
    den_sb = sb("den_sb", [1, 2048], stack=phB1)
    biascol = sb("biascol", [128, 4], stack=phB1)
    attn_t = sb("attn_t", [128, NCH * TA], stack=phB1)
    z1 = sb("z1", [128, NCH * TA], stack=phB1)
    qdtm = [sb("qdtm%d" % i, [128, 128], stack=phB1) for i in range(2)]

    r_xn1h = RW()
    r_xn1l = RW()
    r_attnh = RW()
    r_attnl = RW()
    r_tmpb1 = RW()
    r_wstg1 = [RW(), RW()]
    r_wtmp1 = [RW(), RW()]
    r_wq = RW()
    r_wo = RW()
    r_qfm = RW()
    r_sq1 = [RW(), RW()]
    r_qdneg = RW()
    r_qdT = RW()
    r_qf = RW()
    r_qfTe = RW()
    r_qfTo = RW()
    r_den = RW()
    r_bias = RW()
    r_attn = RW()
    r_z1 = RW()
    r_qdtm = [RW(), RW()]

    def ld_split_w1(srct, dsth, dstl, chan, rw):
        last = None
        HD = D // 2
        for i in range(2 * NCH):
            dd, hv = i // 2, i % 2
            sg = 0
            tok_l = S.do("sync", lambda eng, _d=dd, _h=hv, _s=sg:
                         eng.dma_start(
                out=wstg1[_s][:],
                in_=srct[_d * 128:(_d + 1) * 128, _h * HD:(_h + 1) * HD]),
                deps=r_wstg1[sg].wdep(), dma=True, chan=chan)
            r_wstg1[sg].wrote(tok_l)
            tok_s = split3s(
                wstg1[sg][:],
                dsth[:, dd * D + hv * HD:dd * D + (hv + 1) * HD],
                dstl[:, dd * D + hv * HD:dd * D + (hv + 1) * HD],
                wtmp1[sg][:],
                [tok_l] + r_wtmp1[sg].wdep() +
                (rw.wdep() if i == 0 else []))
            r_wstg1[sg].read(tok_s)
            r_wtmp1[sg].wrote(tok_s)
            rw.wrote(tok_s, reset=(i == 0))
            last = tok_s
        return last

    tok_wq = ld_split_w1(wq, wqh, wql, "c1", r_wq)
    tok_wo = ld_split_w1(wo, woh, wol, "c2", r_wo)

    for it in range(NT):
        bf = it % 2
        t0 = it * TA
        slot0 = (it // 4) * 16

        tok_xn = S.do("sync", lambda eng, _t0=t0: [
            ld3(eng, xn1h, xnh_h, _t0, TA),
            ld3(eng, xn1l, xnl_h, _t0, TA)],
            deps=r_xn1h.wdep() + r_xn1l.wdep(), dma=True, n_dma=2,
            chan="c3")
        r_xn1h.wrote(tok_xn)
        r_xn1l.wrote(tok_xn)

        zpre_toks = []
        for chh in range(NCH):
            zp1 = S.do("vector", lambda eng, _c=chh: eng.tensor_add(
                z1[:, _c * TA:(_c + 1) * TA],
                xn1h[:, _c * TA:(_c + 1) * TA],
                xn1l[:, _c * TA:(_c + 1) * TA]),
                deps=[tok_xn] + (r_z1.wdep() if chh == 0 else []))
            r_z1.wrote(zp1, reset=(chh == 0))
            r_xn1h.read(zp1)
            r_xn1l.read(zp1)
            zpre_toks.append(zp1)

        qev_toks = []
        qd_last = None
        for kch in range(NCH):
            pi = kch % 2
            tok_q = S.do("tensor", lambda eng, _k=kch: [
                eng.matmul(PS[_k % 2][:, 0:TA],
                           wop[:, dd * D + _k * 128: dd * D + (_k + 1) * 128],
                           xop[:, dd * TA:(dd + 1) * TA],
                           start=(pp == 0 and dd == 0),
                           stop=(pp == 2 and dd == NCH - 1))
                for pp, (wop, xop) in enumerate(
                    ((wqh, xn1h), (wqh, xn1l), (wql, xn1h)))
                for dd in range(NCH)],
                deps=[tok_wq, tok_xn] + PB[pi].wdep(), n_inst=24)
            PB[pi].wrote(tok_q)
            r_xn1h.read(tok_q)
            r_xn1l.read(tok_q)
            r_wq.read(tok_q)

            tok_qev = S.do("vector", lambda eng, _k=kch: eng.tensor_copy(
                out=qfm[:, _k * TA:(_k + 1) * TA], in_=PS[_k % 2][:, 0:TA]),
                deps=[tok_q] + (r_qfm.wdep() if kch == 0 else []))
            PB[pi].read(tok_qev)
            qev_toks.append(tok_qev)

            tok_qsq = S.do("scalar", lambda eng, _k=kch: eng.activation(
                out=sqb1[_k % 2][:, 0:TA], in_=PS[_k % 2][:, 0:TA],
                func=AF.Square), deps=[tok_q, tok_qev] + r_sq1[kch % 2].wdep())
            r_sq1[kch % 2].wrote(tok_qsq)
            PB[pi].read(tok_qsq)

            tok_qdmm = S.do("tensor", lambda eng, _k=kch: eng.matmul(
                PS[6][0:16, 0:TA], blockneg_t[:, _k * 16:(_k + 1) * 16],
                sqb1[_k % 2][:, 0:TA], start=(_k == 0), stop=(_k == NCH - 1)),
                deps=[tok_qsq] + (PB[6].wdep() if kch == 0 else []))
            r_sq1[kch % 2].read(tok_qdmm)
            qd_last = tok_qdmm
        PB[6].wrote(qd_last)
        r_qfm.wrote(qev_toks[0])
        for t in qev_toks[1:]:
            r_qfm.wrote(t, reset=False)

        tok_qdev = S.do("vector", lambda eng: eng.tensor_copy(
            out=qdneg[:], in_=PS[6][0:16, 0:TA]),
            deps=[qd_last] + r_qdneg.wdep())
        r_qdneg.wrote(tok_qdev)
        PB[6].read(tok_qdev)

        tok_qtr = S.do("tensor", lambda eng: [
            eng.transpose(PS[7][:, s * 16:(s + 1) * 16],
                          qdneg[:, s * 128:(s + 1) * 128], eye_t[0:16, 0:16])
            for s in range(4)], deps=[tok_qdev] + PB[7].wdep(), n_inst=4)
        PB[7].wrote(tok_qtr)
        r_qdneg.read(tok_qtr)
        tok_qtev = S.do("vector", lambda eng: eng.tensor_copy(
            out=qdT[:], in_=PS[7][:, 0:64]), deps=[tok_qtr] + r_qdT.wdep())
        r_qdT.wrote(tok_qtev)
        PB[7].read(tok_qtev)

        qf_toks = {}
        first_qf = True
        for j in range(8):
            for s in range(4):
                pi = 2 + ((j * 4 + s) % 2)
                tok_qd = S.do("tensor", lambda eng, _j=j, _s=s, _p=pi: eng.matmul(
                    PS[_p][:, 0:128],
                    qfm[:, _j * TA + _s * 128: _j * TA + (_s + 1) * 128],
                    projbd_t[:], start=True, stop=True),
                    deps=[qev_toks[j], tok_init] + PB[pi].wdep())
                PB[pi].wrote(tok_qd)
                r_qfm.read(tok_qd)

                ti = (j * 4 + s) % 2
                tok_cp = S.do("vector", lambda eng, _p=pi, _t=ti:
                              eng.tensor_copy(out=qdtm[_t][:],
                                              in_=PS[_p][:, 0:128]),
                    deps=[tok_qd] + r_qdtm[ti].wdep())
                r_qdtm[ti].wrote(tok_cp)
                PB[pi].read(tok_cp)

                tok_red = S.do("vector", lambda eng, _t=ti: [
                    eng.reduce_max(out=maxscr[:, 0:1], in_=qdtm[_t][:, 0:64],
                                   axis=AX.X),
                    eng.reduce_max(out=maxscr[:, 1:2], in_=qdtm[_t][:, 64:128],
                                   axis=AX.X),
                ], deps=[tok_cp] + r_maxscr.wdep(), n_inst=2)
                r_maxscr.wrote(tok_red)
                r_qdtm[ti].read(tok_red)

                tok_bias = S.do("vector", lambda eng, _j=j, _s=s: [
                    eng.tensor_sub(biascol[:, 0:1],
                                   qdT[:, _s * 16 + 2 * _j:
                                       _s * 16 + 2 * _j + 1],
                                   maxscr[:, 0:1]),
                    eng.tensor_sub(biascol[:, 1:2],
                                   qdT[:, _s * 16 + 2 * _j + 1:
                                       _s * 16 + 2 * _j + 2],
                                   maxscr[:, 1:2]),
                ], deps=[tok_red, tok_qtev] + r_bias.wdep(), n_inst=2)
                r_bias.wrote(tok_bias)
                r_maxscr.read(tok_bias)
                r_qdT.read(tok_bias)

                tok_qf = S.do("scalar", lambda eng, _j=j, _s=s, _t=ti: [
                    eng.activation(
                        out=qf_ext[:, _s * (H * 65) + (2 * _j + hh) * 65:
                                   _s * (H * 65) + (2 * _j + hh) * 65 + 64],
                        in_=qdtm[_t][:, hh * 64:(hh + 1) * 64],
                        func=AF.Exp, bias=biascol[:, hh:hh + 1], scale=1.0)
                    for hh in range(2)],
                    deps=[tok_cp, tok_bias] +
                    (r_qf.wdep() if first_qf else []), n_inst=2)
                first_qf = False
                r_bias.read(tok_qf)
                r_qdtm[ti].read(tok_qf)
                qf_toks[(j, s)] = tok_qf
        r_qf.wrote(qf_toks[(0, 0)])
        for key, t in qf_toks.items():
            if key != (0, 0):
                r_qf.wrote(t, reset=False)

        geps_toks = []
        for s in range(4):
            tok_eps = S.do("gpsimd", lambda eng, _s=s: eng.tensor_scalar(
                out=qf_ext.rearrange("p (s h c) -> p s h c", s=4, c=65)[
                    :, _s, :, 0:64],
                in0=qf_ext.rearrange("p (s h c) -> p s h c", s=4, c=65)[
                    :, _s, :, 0:64],
                scalar1=EPS_K, scalar2=None, op0=ALU.add),
                deps=[qf_toks[(jj, s)] for jj in range(8)])
            r_qf.wrote(tok_eps, reset=False)
            tok_g = S.do("vector", lambda eng, _s=s: [
                eng.reduce_sum(
                    out=qf_ext[:, _s * (H * 65) + h * 65 + 64:
                               _s * (H * 65) + h * 65 + 65],
                    in_=qf_ext[:, _s * (H * 65) + h * 65:
                               _s * (H * 65) + h * 65 + 64],
                    axis=AX.X)
                for h in range(H)], deps=[tok_eps], n_inst=16)
            r_qf.wrote(tok_g, reset=False)
            geps_toks.append(tok_g)

        for j in range(8):
            he, ho = 2 * j, 2 * j + 1
            sle, slo = slot0 + he, slot0 + ho

            pT = j % 2
            tok_tre = S.do("tensor", lambda eng, _h=he, _p=pT: [
                eng.transpose(PS[_p][0:65, s * 128:(s + 1) * 128],
                              qf_ext[:, s * (H * 65) + _h * 65:
                                     s * (H * 65) + _h * 65 + 65], eye_t[:])
                for s in range(4)], deps=geps_toks + PB[pT].wdep(), n_inst=4)
            PB[pT].wrote(tok_tre)
            r_qf.read(tok_tre)
            tok_qfe = S.do("vector", lambda eng, _p=pT: eng.tensor_copy(
                out=qfTe[0:65, :], in_=PS[_p][0:65, 0:512]),
                deps=[tok_tre] + r_qfTe.wdep())
            r_qfTe.wrote(tok_qfe)
            PB[pT].read(tok_qfe)

            tok_tro = S.do("tensor", lambda eng, _h=ho, _p=pT: [
                eng.transpose(PS[5][0:65, s * 128:(s + 1) * 128],
                              qf_ext[:, s * (H * 65) + _h * 65:
                                     s * (H * 65) + _h * 65 + 65], eye_t[:])
                for s in range(4)], deps=geps_toks + PB[5].wdep(), n_inst=4)
            PB[5].wrote(tok_tro)
            r_qf.read(tok_tro)
            tok_qfo = S.do("vector", lambda eng: eng.tensor_copy(
                out=qfTo[0:65, :], in_=PS[5][0:65, 0:512]),
                deps=[tok_tro] + r_qfTo.wdep())
            r_qfTo.wrote(tok_qfo)
            PB[5].read(tok_qfo)

            tok_nume = S.do("tensor", lambda eng, _sl=sle: eng.matmul(
                PS[4][0:64, 0:512], kv_acc[0:65, _sl * 65:_sl * 65 + 64],
                qfTe[0:65, :], start=True, stop=True),
                deps=[tok_qfe, tok_mp] + PB[4].wdep())
            PB[4].wrote(tok_nume)
            r_qfTe.read(tok_nume)
            r_kv.read(tok_nume)
            tok_numo = S.do("tensor", lambda eng, _sl=slo: eng.matmul(
                PS[4][64:128, 0:512], kv_acc[0:65, _sl * 65:_sl * 65 + 64],
                qfTo[0:65, :], start=True, stop=True),
                deps=[tok_qfo, tok_mp] + PB[4].wdep())
            PB[4].wrote(tok_numo, reset=False)
            r_qfTo.read(tok_numo)
            r_kv.read(tok_numo)

            tok_dene = S.do("tensor", lambda eng, _sl=sle: eng.matmul(
                PS[2][0:1, 0:512], kv_acc[0:65, _sl * 65 + 64:_sl * 65 + 65],
                qfTe[0:65, :], start=True, stop=True),
                deps=[tok_qfe, tok_mp] + PB[2].wdep())
            PB[2].wrote(tok_dene)
            r_qfTe.read(tok_dene)
            tok_deno = S.do("tensor", lambda eng, _sl=slo: eng.matmul(
                PS[3][0:1, 0:512], kv_acc[0:65, _sl * 65 + 64:_sl * 65 + 65],
                qfTo[0:65, :], start=True, stop=True),
                deps=[tok_qfo, tok_mp] + PB[3].wdep())
            PB[3].wrote(tok_deno)
            r_qfTo.read(tok_deno)

            d1 = S.do("vector", lambda eng: [
                eng.tensor_copy(out=den_sb[0:1, 0:512], in_=PS[2][0:1, 0:512]),
                eng.tensor_copy(out=den_sb[0:1, 512:1024],
                                in_=PS[3][0:1, 0:512]),
            ], deps=[tok_dene, tok_deno] + r_den.wdep(), n_inst=2)
            r_den.wrote(d1)
            PB[2].read(d1)
            PB[3].read(d1)
            d2a = S.do("vector", lambda eng: eng.reciprocal(
                out=den_sb[0:1, 1024:1536], in_=den_sb[0:1, 0:512]),
                deps=[d1])
            r_den.wrote(d2a, reset=False)
            d2b = S.do("vector", lambda eng: eng.reciprocal(
                out=den_sb[0:1, 1536:2048], in_=den_sb[0:1, 512:1024]),
                deps=[d1])
            r_den.wrote(d2b, reset=False)

            tok_bden = S.do("tensor", lambda eng: [
                eng.matmul(PS[6][:, 0:512], maskeo_t[0:1, 0:128],
                           den_sb[0:1, 1024:1536], start=True, stop=False),
                eng.matmul(PS[6][:, 0:512], maskeo_t[0:1, 128:256],
                           den_sb[0:1, 1536:2048], start=False, stop=True),
            ], deps=[d2b, tok_consts] + PB[6].wdep(), n_inst=2)
            PB[6].wrote(tok_bden)
            r_den.read(tok_bden)

            a1 = S.do("vector", lambda eng, _j=j: eng.tensor_copy(
                out=attn_t[:, _j * TA:(_j + 1) * TA], in_=PS[4][:, 0:512]),
                deps=[tok_nume, tok_numo] +
                (r_attn.wdep() if j == 0 else []))
            r_attn.wrote(a1, reset=(j == 0))
            PB[4].read(a1)
            a2 = S.do("vector", lambda eng, _j=j: eng.tensor_mul(
                attn_t[:, _j * TA:(_j + 1) * TA],
                attn_t[:, _j * TA:(_j + 1) * TA], PS[6][:, 0:512]),
                deps=[a1, tok_bden])
            r_attn.wrote(a2, reset=False)
            PB[6].read(a2)
            a3 = split3s(attn_t[:, j * TA:(j + 1) * TA],
                         attnh[:, j * TA:(j + 1) * TA],
                         attnl[:, j * TA:(j + 1) * TA], tmpb1[:],
                         [a2] + r_tmpb1.wdep() +
                         (r_attnh.wdep() + r_attnl.wdep() if j == 0 else []))
            r_attn.read(a3)
            r_tmpb1.wrote(a3)
            r_attnh.wrote(a3, reset=(j == 0))
            r_attnl.wrote(a3, reset=(j == 0))

        attn_toks = list(r_attnh.w) + list(r_attnl.w)

        if it == 0 and "dbg_attn" in debug:
            t = S.do("gpsimd", lambda eng: eng.dma_start(
                out=dbg_attn[:], in_=attn_t[:]), deps=attn_toks, dma=True,
                chan="c9")
            r_attn.read(t)
            t = S.do("gpsimd", lambda eng: eng.dma_start(
                out=dbg_kv[:], in_=kv_acc[:]), deps=attn_toks, dma=True,
                chan="c9")
            r_kv.read(t)
            t = S.do("gpsimd", lambda eng: eng.dma_start(
                out=dbg_qf[:], in_=qf_ext[:]), deps=geps_toks, dma=True,
                chan="c9")
            r_qf.read(t)
            t = S.do("gpsimd", lambda eng: eng.dma_start(
                out=dbg_qft[:], in_=qfTo[:]), deps=r_qfTo.rdep(), dma=True,
                chan="c9")
            r_qfTo.read(t)
            t = S.do("gpsimd", lambda eng: eng.dma_start(
                out=dbg_den[:], in_=den_sb[:]), deps=r_den.rdep(), dma=True,
                chan="c9")
            r_den.read(t)

        z1_toks = []
        for o in range(NCH):
            pi = o % 2
            tok_ao = S.do("tensor", lambda eng, _o=o: [
                eng.matmul(PS[_o % 2][:, 0:TA],
                           wop[:, p * D + _o * 128: p * D + (_o + 1) * 128],
                           aop[:, p * TA:(p + 1) * TA],
                           start=(pp == 0 and p == 0),
                           stop=(pp == 2 and p == NCH - 1))
                for pp, (wop, aop) in enumerate(
                    ((woh, attnh), (woh, attnl), (wol, attnh)))
                for p in range(NCH)],
                deps=[tok_wo] + attn_toks + PB[pi].wdep(), n_inst=24)
            PB[pi].wrote(tok_ao)
            r_attnh.read(tok_ao)
            r_attnl.read(tok_ao)
            r_wo.read(tok_ao)

            tok_z1 = S.do("vector", lambda eng, _o=o: eng.tensor_add(
                z1[:, _o * TA:(_o + 1) * TA], z1[:, _o * TA:(_o + 1) * TA],
                PS[_o % 2][:, 0:TA]),
                deps=[tok_ao, zpre_toks[o]])
            r_z1.wrote(tok_z1, reset=False)
            PB[pi].read(tok_z1)
            z1_toks.append(tok_z1)

        tok_bmu, tok_brs = ln_stats(z1, r_z1, sqb1, r_sq1, TA,
                                    6, 7, 2, 3, z1_toks, EPS_LN)

        o1_toks = []
        for chh in range(NCH):
            t = ln_norm_chunk(attn_t, r_attn, chh * TA, z1, r_z1, chh * TA,
                              TA, 2, 3, tok_bmu, tok_brs,
                              g1_t[:, chh:chh + 1], b1l_t[:, chh:chh + 1],
                              first=(chh == 0))
            ts = split3s(attn_t[:, chh * TA:(chh + 1) * TA],
                         attnh[:, chh * TA:(chh + 1) * TA],
                         attnl[:, chh * TA:(chh + 1) * TA], tmpb1[:],
                         [t] + r_tmpb1.wdep() +
                         (r_attnh.wdep() + r_attnl.wdep()
                          if chh == 0 else []))
            r_attn.read(ts)
            r_tmpb1.wrote(ts)
            r_attnh.wrote(ts, reset=(chh == 0))
            r_attnl.wrote(ts, reset=(chh == 0))
            o1_toks.append(ts)

        tok_o1sp = S.do("gpsimd", lambda eng, _t0=t0: [
            st3(eng, o1h_h, attnh, _t0, TA),
            st3(eng, o1l_h, attnl, _t0, TA)],
            deps=o1_toks, dma=True, n_dma=2, chan="c5")
        r_attnh.read(tok_o1sp)
        r_attnl.read(tok_o1sp)

    S.flush(block)
    phB1.close()
    S.phase_start()

    if upto == "B1":
        S.wait_only("gpsimd", S.barrier_tokens())
        S.flush(block)
        ctx.close()
        return nc

    TB = 256
    NTB = TPC // TB

    phB2 = ExitStack()
    w1h = sb("w1h", [128, NCH * DFF], BF16, stack=phB2)
    w1l = sb("w1l", [128, NCH * DFF], BF16, stack=phB2)
    wstg2 = [sb("wstg2_%d" % i, [128, DFF // 2], stack=phB2)
             for i in range(2)]
    wtmp2 = [sb("wtmp2_%d" % i, [128, DFF // 2], stack=phB2)
             for i in range(2)]
    o1th = [sb("o1th%d" % i, [128, NCH * TB], BF16, stack=phB2)
            for i in range(2)]
    o1tl = [sb("o1tl%d" % i, [128, NCH * TB], BF16, stack=phB2)
            for i in range(2)]
    hcb = [sb("hcb%d" % i, [128, TB], stack=phB2) for i in range(2)]
    hcbh = [sb("hcbh%d" % i, [128, TB], BF16, stack=phB2) for i in range(2)]
    hcbl = [sb("hcbl%d" % i, [128, TB], BF16, stack=phB2) for i in range(2)]
    tmp1 = [sb("tmp1_%d" % i, [128, TB], stack=phB2) for i in range(2)]
    tmp2 = [sb("tmp2_%d" % i, [128, TB], stack=phB2) for i in range(2)]

    r_o1t = [RW(), RW()]
    r_hcb = [RW(), RW()]
    r_hcbh = [RW(), RW()]
    r_hcbl = [RW(), RW()]
    r_t1 = [RW(), RW()]
    r_t2 = [RW(), RW()]
    r_wstg2 = [RW(), RW()]
    r_wtmp2 = [RW(), RW()]
    r_w1 = RW()

    tok_w1 = None
    HF2 = DFF // 2
    for i in range(2 * NCH):
        dd, hv = i // 2, i % 2
        sg = i % 2
        tok_l = S.do("sync", lambda eng, _d=dd, _h=hv, _s=sg: eng.dma_start(
            out=wstg2[_s][:],
            in_=w1[_d * 128:(_d + 1) * 128, _h * HF2:(_h + 1) * HF2]),
            deps=r_wstg2[sg].wdep(), dma=True, chan="c1")
        r_wstg2[sg].wrote(tok_l)
        tok_s = split3s(
            wstg2[sg][:],
            w1h[:, dd * DFF + hv * HF2:dd * DFF + (hv + 1) * HF2],
            w1l[:, dd * DFF + hv * HF2:dd * DFF + (hv + 1) * HF2],
            wtmp2[sg][:],
            [tok_l] + r_wtmp2[sg].wdep() + (r_w1.wdep() if i == 0 else []))
        r_wstg2[sg].read(tok_s)
        r_wtmp2[sg].wrote(tok_s)
        r_w1.wrote(tok_s, reset=(i == 0))
        tok_w1 = tok_s

    for it in range(NTB):
        bf = it % 2
        t0 = it * TB

        tok_o1 = S.do("sync", lambda eng, _t0=t0, _b=bf: [
            ld3(eng, o1th[_b], o1h_h, _t0, TB),
            ld3(eng, o1tl[_b], o1l_h, _t0, TB)],
            deps=r_o1t[bf].wdep(), dma=True, n_dma=2,
            chan="c3" if bf == 0 else "c4")
        r_o1t[bf].wrote(tok_o1)

        for f in range(DFF // 128):
            pi = f % 4
            fb = f % 2
            tok_mm = S.do("tensor", lambda eng, _f=f, _b=bf, _p=pi: [
                eng.matmul(PS[_p][:, 0:TB],
                           wop[:, dd * DFF + _f * 128:
                               dd * DFF + (_f + 1) * 128],
                           xop[:, dd * TB:(dd + 1) * TB],
                           start=(pp == 0 and dd == 0),
                           stop=(pp == 2 and dd == NCH - 1))
                for pp, (wop, xop) in enumerate(
                    ((w1h, o1th[_b]), (w1h, o1tl[_b]), (w1l, o1th[_b])))
                for dd in range(NCH)],
                deps=[tok_w1, tok_o1] + PB[pi].wdep(), n_inst=24)
            PB[pi].wrote(tok_mm)
            r_o1t[bf].read(tok_mm)
            r_w1.read(tok_mm)

            e1 = S.do("scalar", lambda eng, _f=f, _p=pi, _fb=fb: [
                eng.activation(out=hcb[_fb][:], in_=PS[_p][:, 0:TB],
                               func=AF.Relu, bias=b1_t[:, _f:_f + 1],
                               scale=1.0),
                eng.activation(out=tmp1[_fb][:], in_=PS[_p][:, 0:TB],
                               func=AF.Relu, bias=nb1_t[:, _f:_f + 1],
                               scale=-1.0),
            ], deps=[tok_mm, tok_init] + r_hcb[fb].wdep() + r_t1[fb].wdep(),
                n_inst=2)
            r_hcb[fb].wrote(e1)
            r_t1[fb].wrote(e1)
            PB[pi].read(e1)
            e2 = S.do("scalar", lambda eng, _fb=fb: eng.activation(
                out=tmp2[_fb][:], in_=tmp1[_fb][:], func=AF.Exp, scale=-1.0),
                deps=[e1] + r_t2[fb].wdep())
            r_t2[fb].wrote(e2)
            r_t1[fb].read(e2)
            e3 = S.do("vector", lambda eng, _fb=fb: eng.tensor_scalar(
                out=tmp2[_fb][:], in0=tmp2[_fb][:], scalar1=-1.0,
                scalar2=None, op0=ALU.add), deps=[e2])
            r_t2[fb].wrote(e3, reset=False)
            e4 = S.do("vector", lambda eng, _fb=fb: eng.tensor_add(
                hcb[_fb][:], hcb[_fb][:], tmp2[_fb][:]), deps=[e3, e1])
            r_hcb[fb].wrote(e4, reset=False)
            r_t2[fb].read(e4)
            e5 = split3s(hcb[fb][:], hcbh[fb][:], hcbl[fb][:], tmp1[fb][:],
                         [e4] + r_t1[fb].wdep() + r_hcbh[fb].wdep() +
                         r_hcbl[fb].wdep())
            r_hcb[fb].read(e5)
            r_t1[fb].wrote(e5)
            r_hcbh[fb].wrote(e5)
            r_hcbl[fb].wrote(e5)

            tok_hsp = S.do("gpsimd", lambda eng, _f=f, _t0=t0, _fb=fb: [
                eng.dma_start(
                    out=hh_h[_f * 128:(_f + 1) * 128, _t0:_t0 + TB],
                    in_=hcbh[_fb][:]),
                eng.dma_start(
                    out=hl_h[_f * 128:(_f + 1) * 128, _t0:_t0 + TB],
                    in_=hcbl[_fb][:]),
            ], deps=[e5], dma=True, n_dma=2,
                chan="c5" if fb == 0 else "c6")
            r_hcbh[fb].read(tok_hsp)
            r_hcbl[fb].read(tok_hsp)

    S.flush(block)
    phB2.close()
    S.phase_start()

    if upto == "B2":
        S.wait_only("gpsimd", S.barrier_tokens())
        S.flush(block)
        ctx.close()
        return nc

    phB3 = ExitStack()
    w2h = sb("w2h", [128, (DFF // 128) * D], BF16, stack=phB3)
    w2l = sb("w2l", [128, (DFF // 128) * D], BF16, stack=phB3)
    wstg3 = [sb("wstg3_%d" % i, [128, D // 2], stack=phB3) for i in range(2)]
    wtmp3 = [sb("wtmp3_%d" % i, [128, D // 2], stack=phB3) for i in range(2)]
    hth = sb("hth", [128, (DFF // 128) * TB], BF16, stack=phB3)
    htl = sb("htl", [128, (DFF // 128) * TB], BF16, stack=phB3)
    o1t3h = sb("o1t3h", [128, NCH * TB], BF16, stack=phB3)
    o1t3l = sb("o1t3l", [128, NCH * TB], BF16, stack=phB3)
    z2 = sb("z2", [128, NCH * TB], stack=phB3)
    sqb3 = [sb("sqb3_%d" % i, [128, TB], stack=phB3) for i in range(2)]
    aoutb = z2

    r_ht = RW()
    r_o1t3 = RW()
    r_z2 = RW()
    r_sq3 = [RW(), RW()]
    r_ao = RW()
    r_wstg3 = [RW(), RW()]
    r_wtmp3 = [RW(), RW()]
    r_w2 = RW()

    tok_w2 = None
    HD3 = D // 2
    for i in range(2 * (DFF // 128)):
        ff, hv = i // 2, i % 2
        sg = i % 2
        tok_l = S.do("sync", lambda eng, _f=ff, _h=hv, _s=sg: eng.dma_start(
            out=wstg3[_s][:],
            in_=w2[_f * 128:(_f + 1) * 128, _h * HD3:(_h + 1) * HD3]),
            deps=r_wstg3[sg].wdep(), dma=True, chan="c1")
        r_wstg3[sg].wrote(tok_l)
        tok_s = split3s(
            wstg3[sg][:],
            w2h[:, ff * D + hv * HD3:ff * D + (hv + 1) * HD3],
            w2l[:, ff * D + hv * HD3:ff * D + (hv + 1) * HD3], wtmp3[sg][:],
            [tok_l] + r_wtmp3[sg].wdep() + (r_w2.wdep() if i == 0 else []))
        r_wstg3[sg].read(tok_s)
        r_wtmp3[sg].wrote(tok_s)
        r_w2.wrote(tok_s, reset=(i == 0))
        tok_w2 = tok_s

    for it in range(NTB):
        t0 = it * TB

        tok_h = S.do("sync", lambda eng, _t0=t0: [
            ld3(eng, hth, hh_h, _t0, TB),
            ld3(eng, htl, hl_h, _t0, TB)],
            deps=r_ht.wdep(), dma=True, n_dma=2, chan="c2")
        r_ht.wrote(tok_h)

        tok_o1 = S.do("sync", lambda eng, _t0=t0: [
            ld3(eng, o1t3h, o1h_h, _t0, TB),
            ld3(eng, o1t3l, o1l_h, _t0, TB)],
            deps=r_o1t3.wdep(), dma=True, n_dma=2, chan="c3")
        r_o1t3.wrote(tok_o1)

        z2_toks = []
        for o in range(NCH):
            pi = o % 2
            tok_mm = S.do("tensor", lambda eng, _o=o, _p=pi: [
                eng.matmul(PS[_p][:, 0:TB],
                           wop[:, ff * D + _o * 128: ff * D + (_o + 1) * 128],
                           xop[:, ff * TB:(ff + 1) * TB],
                           start=(pp == 0 and ff == 0),
                           stop=(pp == 2 and ff == DFF // 128 - 1))
                for pp, (wop, xop) in enumerate(
                    ((w2h, hth), (w2h, htl), (w2l, hth)))
                for ff in range(DFF // 128)],
                deps=[tok_w2, tok_h] + PB[pi].wdep(), n_inst=96)
            PB[pi].wrote(tok_mm)
            r_ht.read(tok_mm)
            r_w2.read(tok_mm)

            tz1 = S.do("scalar", lambda eng, _o=o, _p=pi: eng.activation(
                out=z2[:, _o * TB:(_o + 1) * TB], in_=PS[_p][:, 0:TB],
                func=AF.Identity, bias=b2_t[:, _o:_o + 1], scale=1.0),
                deps=[tok_mm] + (r_z2.wdep() if o == 0 else []))
            r_z2.wrote(tz1, reset=(o == 0))
            PB[pi].read(tz1)
            tz2a = S.do("vector", lambda eng, _o=o: eng.tensor_add(
                z2[:, _o * TB:(_o + 1) * TB], z2[:, _o * TB:(_o + 1) * TB],
                o1t3h[:, _o * TB:(_o + 1) * TB]), deps=[tz1, tok_o1])
            r_z2.wrote(tz2a, reset=False)
            r_o1t3.read(tz2a)
            tz2 = S.do("vector", lambda eng, _o=o: eng.tensor_add(
                z2[:, _o * TB:(_o + 1) * TB], z2[:, _o * TB:(_o + 1) * TB],
                o1t3l[:, _o * TB:(_o + 1) * TB]), deps=[tz2a])
            r_z2.wrote(tz2, reset=False)
            r_o1t3.read(tz2)
            z2_toks.append(tz2)

        tok_bmu, tok_brs = ln_stats(z2, r_z2, sqb3, r_sq3, TB,
                                    6, 7, 2, 3, z2_toks, EPS_LN)

        ao_toks = []
        for chh in range(NCH):
            t = ln_norm_chunk(z2, r_z2, chh * TB, z2, r_z2, chh * TB,
                              TB, 2, 3, tok_bmu, tok_brs,
                              g2_t[:, chh:chh + 1], b2l_t[:, chh:chh + 1],
                              first=False)
            ao_toks.append(t)

        tok_aosp = S.do("gpsimd", lambda eng, _t0=t0: st3(
            eng, aout_h, z2, _t0, TB),
            deps=ao_toks, dma=True, n_dma=1, chan="c5")
        r_z2.read(tok_aosp)

    S.flush(block)
    phB3.close()
    S.phase_start()

    if upto == "B3":
        S.wait_only("gpsimd", S.barrier_tokens())
        S.flush(block)
        ctx.close()
        return nc

    TC = 512
    NTC = TPC // TC

    def swiglu_proj_phase(wsrc, out_hbm, silu_mul):
        ph = ExitStack()
        wf_t = sb("wf_t_%d" % silu_mul, [128, NCH * HID], BF16, stack=ph)
        wi_t = [sb("wi_t%d_%d" % (i, silu_mul), [128, HID], I32, stack=ph)
                for i in range(2)]
        xnch = [sb("xnch%d_%d" % (i, silu_mul), [128, NCH * TC], BF16,
                   stack=ph) for i in range(2)]
        xncl = [sb("xncl%d_%d" % (i, silu_mul), [128, NCH * TC], BF16,
                   stack=ph) for i in range(2)]
        gc = [sb("gc%d_%d" % (i, silu_mul), [128, TC], stack=ph)
              for i in range(2)]
        gch = [sb("gch%d_%d" % (i, silu_mul), [128, TC], BF16, stack=ph)
               for i in range(2)]
        gcl = [sb("gcl%d_%d" % (i, silu_mul), [128, TC], BF16, stack=ph)
               for i in range(2)]
        gtmp = [sb("gtmp%d_%d" % (i, silu_mul), [128, TC], stack=ph)
                for i in range(2)]
        sgc = [sb("sgc%d_%d" % (i, silu_mul), [128, TC], stack=ph)
               for i in range(2)]
        gst = None
        if silu_mul:
            gst = [sb("gst0", [128, NHC * TC], stack=ph)]
        r_wf = RW()
        r_wi = [RW(), RW()]
        r_xnc = [RW(), RW()]
        r_gc = [RW(), RW()]
        r_gch = [RW(), RW()]
        r_gcl = [RW(), RW()]
        r_gtmp = [RW(), RW()]
        r_sgc = [RW(), RW()]
        r_gst = [RW(), RW()]

        cvt_toks = []
        for dd in range(NCH):
            sg = dd % 2
            tok_l = S.do("sync", lambda eng, _d=dd, _s=sg: eng.dma_start(
                out=wi_t[_s][:], in_=wsrc[_d * 128:(_d + 1) * 128, :]),
                deps=r_wi[sg].wdep(), dma=True, chan="c1")
            r_wi[sg].wrote(tok_l)
            tok_cv = S.do("vector", lambda eng, _d=dd, _s=sg: eng.tensor_copy(
                out=wf_t[:, _d * HID:(_d + 1) * HID], in_=wi_t[_s][:]),
                deps=[tok_l])
            r_wi[sg].read(tok_cv)
            cvt_toks.append(tok_cv)
            r_wf.wrote(tok_cv, reset=(dd == 0))

        for it in range(NTC):
            t0 = it * TC
            cb = it % 2

            tok_xn = S.do("sync", lambda eng, _t0=t0, _c=cb: [
                ld3(eng, xnch[_c], xnh_h, _t0, TC),
                ld3(eng, xncl[_c], xnl_h, _t0, TC)],
                deps=r_xnc[cb].wdep(), dma=True, n_dma=2, chan="c3")
            r_xnc[cb].wrote(tok_xn)

            tok_g = None
            if silu_mul:
                tok_g = S.do("sync", lambda eng, _t0=t0: ld3(
                    eng, gst[0], gate_h, _t0, TC), deps=r_gst[0].wdep(),
                    dma=True, n_dma=1, chan="c2")
                r_gst[0].wrote(tok_g)

            for hc in range(NHC):
                pi = hc % 4
                fb = hc % 2
                tok_mm = S.do("tensor", lambda eng, _h=hc, _p=pi, _c=cb: [
                    eng.matmul(PS[_p][:, 0:TC],
                               wf_t[:, dd * HID + _h * 128:
                                    dd * HID + (_h + 1) * 128],
                               xop[:, dd * TC:(dd + 1) * TC],
                               start=(pp == 0 and dd == 0),
                               stop=(pp == 1 and dd == NCH - 1))
                    for pp, xop in enumerate((xnch[_c], xncl[_c]))
                    for dd in range(NCH)],
                    deps=cvt_toks + [tok_xn] + PB[pi].wdep(), n_inst=16)
                PB[pi].wrote(tok_mm)
                r_xnc[cb].read(tok_mm)
                r_wf.read(tok_mm)

                if not silu_mul:
                    tok_ev = S.do("scalar", lambda eng, _p=pi, _fb=fb:
                                  eng.activation(
                        out=gc[_fb][:], in_=PS[_p][:, 0:TC], func=AF.Identity),
                        deps=[tok_mm] + r_gc[fb].wdep())
                    r_gc[fb].wrote(tok_ev)
                    PB[pi].read(tok_ev)
                    tok_spg = S.do("gpsimd", lambda eng, _h=hc, _t0=t0,
                                   _fb=fb: eng.dma_start(
                        out=out_hbm[_h * 128:(_h + 1) * 128, _t0:_t0 + TC],
                        in_=gc[_fb][:]), deps=[tok_ev], dma=True,
                        chan="c5" if fb == 0 else "c6")
                    r_gc[fb].read(tok_spg)
                else:
                    tok_si = S.do("scalar", lambda eng, _h=hc, _fb=fb:
                                  eng.activation(
                        out=sgc[_fb][:], in_=gst[0][:, _h * TC:(_h + 1) * TC],
                        func=AF.Silu, scale=SCALE_Q),
                        deps=[tok_g] + r_sgc[fb].wdep())
                    r_sgc[fb].wrote(tok_si)
                    r_gst[0].read(tok_si)
                    tok_ev = S.do("vector", lambda eng, _p=pi, _fb=fb:
                                  eng.tensor_mul(gc[_fb][:], sgc[_fb][:],
                                                 PS[_p][:, 0:TC]),
                        deps=[tok_si, tok_mm] + r_gc[fb].wdep())
                    r_gc[fb].wrote(tok_ev)
                    r_sgc[fb].read(tok_ev)
                    PB[pi].read(tok_ev)
                    tok_sp2 = split3s(gc[fb][:], gch[fb][:], gcl[fb][:],
                                      gtmp[fb][:],
                                      [tok_ev] + r_gch[fb].wdep() +
                                      r_gcl[fb].wdep() + r_gtmp[fb].wdep())
                    r_gc[fb].read(tok_sp2)
                    r_gtmp[fb].wrote(tok_sp2)
                    r_gch[fb].wrote(tok_sp2)
                    r_gcl[fb].wrote(tok_sp2)
                    tok_spg = S.do("gpsimd", lambda eng, _h=hc, _t0=t0,
                                   _fb=fb: [
                        eng.dma_start(
                            out=hsh_h[_h * 128:(_h + 1) * 128, _t0:_t0 + TC],
                            in_=gch[_fb][:]),
                        eng.dma_start(
                            out=hsl_h[_h * 128:(_h + 1) * 128, _t0:_t0 + TC],
                            in_=gcl[_fb][:]),
                    ], deps=[tok_sp2], dma=True, n_dma=2,
                        chan="c5" if fb == 0 else "c6")
                    r_gch[fb].read(tok_spg)
                    r_gcl[fb].read(tok_spg)

        S.flush(block)
        ph.close()
        S.phase_start()

    swiglu_proj_phase(wgT, gate_h, False)
    if upto == "C1":
        S.wait_only("gpsimd", S.barrier_tokens())
        S.flush(block)
        ctx.close()
        return nc
    swiglu_proj_phase(wuT, None, True)
    if upto == "C1b":
        S.wait_only("gpsimd", S.barrier_tokens())
        S.flush(block)
        ctx.close()
        return nc

    phC2 = ExitStack()
    wdf_t = sb("wdf_t", [128, NHC * D], BF16, stack=phC2)
    wdi_t = [sb("wdi_t%d" % i, [128, D], I32, stack=phC2) for i in range(2)]
    hsth = [sb("hsth%d" % i, [128, NHC * TC], BF16, stack=phC2)
            for i in range(2)]
    hstl = [sb("hstl%d" % i, [128, NHC * TC], BF16, stack=phC2)
            for i in range(2)]
    xt2 = sb("xt2", [128, NCH * TC], stack=phC2)
    aot = [sb("aot%d" % i, [128, TC], stack=phC2) for i in range(2)]
    yb = [sb("yb%d" % i, [128, TC], stack=phC2) for i in range(2)]

    r_wdf = RW()
    r_wdi = [RW(), RW()]
    r_hst = [RW(), RW()]
    r_xt2 = RW()
    r_aot = [RW(), RW()]
    r_yb = [RW(), RW()]

    cvt_toks = []
    for hcg in range(NHC):
        sg = hcg % 2
        tok_l = S.do("sync", lambda eng, _h=hcg, _s=sg: eng.dma_start(
            out=wdi_t[_s][:], in_=wdT[_h * 128:(_h + 1) * 128, :]),
            deps=r_wdi[sg].wdep(), dma=True, chan="c1")
        r_wdi[sg].wrote(tok_l)
        tok_cv = S.do("vector", lambda eng, _h=hcg, _s=sg: eng.tensor_copy(
            out=wdf_t[:, _h * D:(_h + 1) * D], in_=wdi_t[_s][:]),
            deps=[tok_l])
        r_wdi[sg].read(tok_cv)
        cvt_toks.append(tok_cv)
        r_wdf.wrote(tok_cv, reset=(hcg == 0))

    for it in range(NTC):
        t0 = it * TC
        cb = it % 2

        tok_hs = S.do("sync", lambda eng, _t0=t0, _c=cb: [
            ld3(eng, hsth[_c], hsh_h, _t0, TC),
            ld3(eng, hstl[_c], hsl_h, _t0, TC)],
            deps=r_hst[cb].wdep(), dma=True, n_dma=2, chan="c2")
        r_hst[cb].wrote(tok_hs)

        tok_x = S.do("sync", lambda eng, _t0=t0: ld3(
            eng, xt2, xT, _t0, TC), deps=r_xt2.wdep(), dma=True, n_dma=1,
            chan="c3")
        r_xt2.wrote(tok_x)



        for o in range(NCH):
            pi = o % 2
            ab = o % 2
            tok_ao = S.do("sync", lambda eng, _t0=t0, _o=o, _a=ab:
                          eng.dma_start(
                out=aot[_a][:],
                in_=aout_h[_o * 128:(_o + 1) * 128, _t0:_t0 + TC]),
                deps=r_aot[ab].wdep(), dma=True, n_dma=1, chan="c4")
            r_aot[ab].wrote(tok_ao)
            tok_mm = S.do("tensor", lambda eng, _o=o, _p=pi, _c=cb: [
                eng.matmul(PS[_p][:, 0:TC],
                           wdf_t[:, hc * D + _o * 128: hc * D + (_o + 1) * 128],
                           xop[:, hc * TC:(hc + 1) * TC],
                           start=(pp == 0 and hc == 0),
                           stop=(pp == 1 and hc == NHC - 1))
                for pp, xop in enumerate((hsth[_c], hstl[_c]))
                for hc in range(NHC)],
                deps=cvt_toks + [tok_hs] + PB[pi].wdep(), n_inst=48)
            PB[pi].wrote(tok_mm)
            r_hst[cb].read(tok_mm)
            r_wdf.read(tok_mm)

            ybi = o % 2
            y1 = S.do("vector", lambda eng, _p=pi, _y=ybi: eng.tensor_scalar(
                out=yb[_y][:], in0=PS[_p][:, 0:TC], scalar1=SCALE_Q2,
                scalar2=None, op0=ALU.mult),
                deps=[tok_mm] + r_yb[ybi].wdep())
            r_yb[ybi].wrote(y1)
            PB[pi].read(y1)
            y2 = S.do("vector", lambda eng, _o=o, _y=ybi: eng.tensor_add(
                yb[_y][:], yb[_y][:], xt2[:, _o * TC:(_o + 1) * TC]),
                deps=[y1, tok_x])
            r_yb[ybi].wrote(y2, reset=False)
            r_xt2.read(y2)
            y3 = S.do("vector", lambda eng, _o=o, _y=ybi, _a=ab:
                      eng.tensor_add(
                yb[_y][:], yb[_y][:], aot[_a][:]),
                deps=[y2, tok_ao])
            r_yb[ybi].wrote(y3, reset=False)
            r_aot[ab].read(y3)

            tok_st = S.do("gpsimd", lambda eng, _o=o, _t0=t0, _y=ybi:
                          eng.dma_start(
                out=yT[_o * 128:(_o + 1) * 128, _t0:_t0 + TC],
                in_=yb[_y][:]), deps=[y3], dma=True, chan="c5")
            r_yb[ybi].read(tok_st)

    S.wait_only("gpsimd", S.barrier_tokens())
    S.flush(block)
    phC2.close()

    ctx.close()
    return nc


def make_inputs(inputs):
    x = np.asarray(inputs["L0_attn_x"])
    Wq = np.asarray(inputs["L0_attn_Wq"]).reshape(D, D)
    Wk = np.asarray(inputs["L0_attn_Wk"]).reshape(D, D)
    Wv = np.asarray(inputs["L0_attn_Wv"]).reshape(D, D)
    Wo = np.asarray(inputs["L0_attn_Wo"]).reshape(D, D)
    proj = np.asarray(inputs["L0_attn_proj"])
    W1 = np.asarray(inputs["L0_attn_W1"])
    b1 = np.asarray(inputs["L0_attn_b1"])
    W2 = np.asarray(inputs["L0_attn_W2"])
    b2 = np.asarray(inputs["L0_attn_b2"])
    g1 = np.asarray(inputs["L0_attn_ln1_g"])
    b1l = np.asarray(inputs["L0_attn_ln1_b"])
    g2 = np.asarray(inputs["L0_attn_ln2_g"])
    b2l = np.asarray(inputs["L0_attn_ln2_b"])
    wg = np.asarray(inputs["L0_ffn_w_gate"])
    wu = np.asarray(inputs["L0_ffn_w_up"])
    wd = np.asarray(inputs["L0_ffn_w_down"])

    projbd = np.zeros((128, 128), np.float32)
    projbd[0:64, 0:64] = proj.T
    projbd[64:128, 64:128] = proj.T

    eye = np.eye(128, dtype=np.float32)
    ones1r = np.ones((1, 128), np.float32)
    avgc = np.full((128, 1), np.float32(1.0 / 1024.0), np.float32)
    blockneg = np.zeros((128, 128), np.float32)
    for chh in range(NCH):
        blockneg[0:64, chh * 16 + 2 * chh] = np.float32(-CDIAG)
        blockneg[64:128, chh * 16 + 2 * chh + 1] = np.float32(-CDIAG)
    maskeo = np.zeros((1, 256), np.float32)
    maskeo[0, 0:64] = 1.0
    maskeo[0, 192:256] = 1.0

    shared = dict(
        wq=np.ascontiguousarray(Wq), wk=np.ascontiguousarray(Wk),
        wv=np.ascontiguousarray(Wv), wo=np.ascontiguousarray(Wo),
        projbd=projbd, w1=np.ascontiguousarray(W1),
        w2=np.ascontiguousarray(W2),
        b1c=np.ascontiguousarray(b1.reshape(DFF // 128, 128).T),
        b2c=np.ascontiguousarray(b2.reshape(NCH, 128).T),
        g1c=np.ascontiguousarray(g1.reshape(NCH, 128).T),
        b1lc=np.ascontiguousarray(b1l.reshape(NCH, 128).T),
        g2c=np.ascontiguousarray(g2.reshape(NCH, 128).T),
        b2lc=np.ascontiguousarray(b2l.reshape(NCH, 128).T),
        wgT=np.ascontiguousarray(wg.T), wuT=np.ascontiguousarray(wu.T),
        wdT=np.ascontiguousarray(wd.T),
        eye=eye, ones1r=ones1r, avgc=avgc, blockneg=blockneg, maskeo=maskeo,
    )
    in_maps = []
    for c in range(NCORES):
        m = dict(shared)
        m["xT"] = np.ascontiguousarray(x[2 * c:2 * c + 2].reshape(TPC, D).T)
        in_maps.append(m)
    return in_maps


def kernel(**inputs):
    in_maps = make_inputs(inputs)
    nc = build_program()
    res = run_bass_kernel_spmd(nc, in_maps, core_ids=list(range(NCORES)))
    outs = []
    for c in range(NCORES):
        yc = np.asarray(res.results[c]["yT"])
        outs.append(np.ascontiguousarray(yc.T).reshape(2, N, D))
    return np.concatenate(outs, axis=0).astype(np.float32)


# version 22
# speedup vs baseline: 1.0228x; 1.0228x over previous
import numpy as np
from contextlib import ExitStack

import concourse.bass as bass
import concourse.mybir as mybir
from concourse.bass_utils import run_bass_kernel_spmd

F32 = mybir.dt.float32
BF16 = mybir.dt.bfloat16
I32 = mybir.dt.int32
AF = mybir.ActivationFunctionType
ALU = mybir.AluOpType
AX = mybir.AxisListType

NCORES = 8
B, N, D = 16, 2048, 1024
H, DH, M = 16, 64, 64
DFF, HID = 4096, 3072
TPC = B * N // NCORES
NCH = D // 128
NHC = HID // 128

EPS_K = 1e-6
EPS_LN = 1e-6
EPS_OUT = 1e-5

_DN32 = np.float32(1.0) / np.sqrt(np.sqrt(np.float32(64.0)))
DN = float(_DN32)
CDIAG = float((_DN32 * _DN32) / np.float32(2.0))
_S32 = np.float32(1.0 / 127.0)
SCALE_Q = float(_S32)
SCALE_Q2 = float(_S32 * _S32)

ENGS = ("sync", "gpsimd", "tensor", "vector", "scalar")
NCHAN = 12
CHANS = tuple("c%d" % i for i in range(NCHAN))


class Sched:

    def __init__(self, nc, ctx):
        self.nc = nc
        self.sem = {e: ctx.enter_context(nc.semaphore("sem_" + e)) for e in ENGS}
        for ch in CHANS:
            self.sem[ch] = ctx.enter_context(nc.semaphore("sem_" + ch))
        self.cnt = {k: 0 for k in self.sem}
        self.steps = {e: [] for e in ENGS}
        self.pending = {e: [] for e in ENGS}

    def do(self, eng, emit, deps=(), dma=False, n_dma=1, chan=None,
           n_inst=1, inc_last=False):
        waits = {}
        for d in list(deps) + self.pending[eng]:
            if d is None:
                continue
            de, dv = d
            assert not (de == eng and not dma and dv > self.cnt[eng])
            if waits.get(de, 0) < dv:
                waits[de] = dv
        self.pending[eng] = []
        if dma:
            assert chan in CHANS, chan
            self.cnt[chan] += 16 * n_dma
            tok = (chan, self.cnt[chan])
        else:
            self.cnt[eng] += 1 if inc_last else n_inst
            tok = (eng, self.cnt[eng])
        self.steps[eng].append((waits, emit, dma, n_dma, chan, n_inst,
                                inc_last))
        return tok

    def wait_only(self, eng, deps):
        waits = {}
        for d in deps:
            if d is None:
                continue
            de, dv = d
            if waits.get(de, 0) < dv:
                waits[de] = dv
        self.steps[eng].append((waits, None, False, 0, None, 0, False))

    def barrier_tokens(self):
        return [(k, self.cnt[k]) for k in self.sem]

    def phase_start(self):
        toks = self.barrier_tokens()
        for e in ENGS:
            self.pending[e] = list(toks)

    def flush(self, block):
        for e in ENGS:
            steps = self.steps[e]
            if not steps:
                continue
            sems = self.sem

            def run(eng, _steps=steps, _e=e):
                for waits, emit, dma, n_dma, chan, n_inst, inc_last in _steps:
                    for we, wv in waits.items():
                        eng.wait_ge(sems[we], wv)
                    if emit is None:
                        continue
                    insts = emit(eng)
                    if not isinstance(insts, (list, tuple)):
                        insts = [insts]
                    if dma:
                        assert len(insts) == n_dma, (len(insts), n_dma)
                        for i in insts:
                            i.then_inc(sems[chan], 16)
                    elif inc_last:
                        insts[-1].then_inc(sems[_e], 1)
                    else:
                        assert len(insts) == n_inst, (len(insts), n_inst)
                        for i in insts:
                            i.then_inc(sems[_e], 1)

            getattr(block, e)(run)
            self.steps[e] = []


class RW:

    def __init__(self):
        self.w = []
        self.r = []

    def wdep(self):
        return self.r + self.w

    def rdep(self):
        return list(self.w)

    def wrote(self, tok, reset=True):
        if reset:
            self.w = [tok]
            self.r = []
        else:
            self.w.append(tok)

    def read(self, tok):
        self.r.append(tok)


def build_program(debug=(), upto="C2", parts="lkdvzqwt", nt=None, use_cc=True):
    nc = bass.Bass("TRN2", target_bir_lowering=False, debug=False)
    ctx = ExitStack()

    def dram(name, shape, dtype, kind="Internal", **kw):
        if name in debug and kind == "Internal":
            kind = "ExternalOutput"
        return nc.dram_tensor(name, shape, dtype, kind=kind, **kw).ap()

    xT = dram("xT", [D, TPC], F32, kind="ExternalInput")
    wq = dram("wq", [D, D], F32, kind="ExternalInput")
    wk = dram("wk", [D, D], F32, kind="ExternalInput")
    wv = dram("wv", [D, D], F32, kind="ExternalInput")
    wo = dram("wo", [D, D], F32, kind="ExternalInput")
    projbd = dram("projbd", [128, 128], F32, kind="ExternalInput")
    w1 = dram("w1", [D, DFF], F32, kind="ExternalInput")
    w2 = dram("w2", [DFF, D], F32, kind="ExternalInput")
    b1c = dram("b1c", [128, DFF // 128], F32, kind="ExternalInput")
    b2c = dram("b2c", [128, NCH], F32, kind="ExternalInput")
    g1c = dram("g1c", [128, NCH], F32, kind="ExternalInput")
    b1lc = dram("b1lc", [128, NCH], F32, kind="ExternalInput")
    g2c = dram("g2c", [128, NCH], F32, kind="ExternalInput")
    b2lc = dram("b2lc", [128, NCH], F32, kind="ExternalInput")
    wgT = dram("wgT", [D, HID], I32, kind="ExternalInput")
    wuT = dram("wuT", [D, HID], I32, kind="ExternalInput")
    wdT = dram("wdT", [HID, D], I32, kind="ExternalInput")
    eye = dram("eye", [128, 128], F32, kind="ExternalInput")
    ones1r = dram("ones1r", [1, 128], F32, kind="ExternalInput")
    avgc = dram("avgc", [128, 1], F32, kind="ExternalInput")
    blockneg = dram("blockneg", [128, 128], F32, kind="ExternalInput")
    maskeo = dram("maskeo", [1, 256], F32, kind="ExternalInput")

    yT = dram("yT", [D, TPC], F32, kind="ExternalOutput")

    xnh_h = dram("xnh_h", [D, TPC], BF16)
    xnl_h = dram("xnl_h", [D, TPC], BF16)
    o1h_h = dram("o1h_h", [D, TPC], BF16)
    o1l_h = dram("o1l_h", [D, TPC], BF16)
    aout_h = dram("aout_h", [D, TPC], F32)
    hh_h = dram("hh_h", [DFF, TPC], BF16)
    hl_h = dram("hl_h", [DFF, TPC], BF16)
    gate_h = dram("gate_h", [HID, TPC], F32)
    hsh_h = dram("hsh_h", [HID, TPC], BF16)
    hsl_h = dram("hsl_h", [HID, TPC], BF16)
    ccin = dram("ccin", [1, 16], F32)
    ccout = dram("ccout", [1, 16], F32, addr_space="Shared")
    dbg_kv = dram("dbg_kv", [128, 32 * 65], F32)
    dbg_mx = dram("dbg_mx", [1, 16], F32)
    dbg_qf = dram("dbg_qf", [128, 4 * H * 65], F32)
    dbg_qft = dram("dbg_qft", [128, 512], F32)
    dbg_den = dram("dbg_den", [1, 2048], F32)
    dbg_lmx = dram("dbg_lmx", [1, 144], F32)
    dbg_aend = dram("dbg_aend", [128, 8448], F32)
    dbg_kd = dram("dbg_kd", [128, 148], F32)
    dbg_gmax = dram("dbg_gmax", [128, 16], F32)
    dbg_attn = dram("dbg_attn", [128, NCH * 512], F32)

    S = Sched(nc, ctx)
    block = ctx.enter_context(nc.Block())

    PS = [ctx.enter_context(nc.psum_tensor("ps%d" % i, [128, 512], F32))
          for i in range(8)]
    PB = [RW() for _ in range(8)]

    def sb(name, shape, dtype=F32, stack=ctx):
        return stack.enter_context(nc.sbuf_tensor(name, shape, dtype))

    eye_t = sb("eye_t", [128, 128])
    ones1_t = sb("ones1_t", [1, 128])
    avgc_t = sb("avgc_t", [128, 1])
    blockneg_t = sb("blockneg_t", [128, 128])
    maskeo_t = sb("maskeo_t", [1, 256])
    projbd_t = sb("projbd_t", [128, 128])
    kv_acc = sb("kv_acc", [128, 32 * 65])
    gmax = sb("gmax", [128, 16])
    maxscr = sb("maxscr", [128, 4])
    mxrow = sb("mxrow", [1, 144])
    mx_sb = sb("mx_sb", [1, 16])
    scale_col = sb("scale_col", [128, 1])
    stat_sb = sb("stat_sb", [1, 4 * 512])
    g1_t = sb("g1_t", [128, NCH])
    b1l_t = sb("b1l_t", [128, NCH])
    g2_t = sb("g2_t", [128, NCH])
    b2l_t = sb("b2l_t", [128, NCH])
    b2_t = sb("b2_t", [128, NCH])
    b1_t = sb("b1_t", [128, DFF // 128])
    nb1_t = sb("nb1_t", [128, DFF // 128])

    r_stat = RW()
    r_kv = RW()
    r_gmax = RW()
    r_maxscr = RW()
    r_scol = RW()
    r_mx = RW()
    r_mxrow = RW()

    tok_consts = S.do("sync", lambda eng: [
        eng.dma_start(out=eye_t[:], in_=eye[:]),
        eng.dma_start(out=ones1_t[:], in_=ones1r[:]),
        eng.dma_start(out=avgc_t[:], in_=avgc[:]),
        eng.dma_start(out=blockneg_t[:], in_=blockneg[:]),
        eng.dma_start(out=maskeo_t[:], in_=maskeo[:]),
        eng.dma_start(out=projbd_t[:], in_=projbd[:]),
        eng.dma_start(out=g1_t[:], in_=g1c[:]),
        eng.dma_start(out=b1l_t[:], in_=b1lc[:]),
        eng.dma_start(out=g2_t[:], in_=g2c[:]),
        eng.dma_start(out=b2l_t[:], in_=b2lc[:]),
        eng.dma_start(out=b2_t[:], in_=b2c[:]),
        eng.dma_start(out=b1_t[:], in_=b1c[:]),
    ], dma=True, n_dma=12, chan="c0")

    tok_init = S.do("vector", lambda eng: [
        eng.tensor_scalar(out=projbd_t[:], in0=projbd_t[:], scalar1=DN,
                          scalar2=None, op0=ALU.mult),
        eng.memset(kv_acc[:], 0.0),
        eng.memset(gmax[:], -3.0e38),
        eng.tensor_scalar(out=nb1_t[:], in0=b1_t[:], scalar1=-1.0,
                          scalar2=None, op0=ALU.mult),
    ], deps=[tok_consts], n_inst=4)
    r_kv.wrote(tok_init)
    r_gmax.wrote(tok_init)

    if "dbg_init" in debug:
        t = S.do("gpsimd", lambda eng: eng.dma_start(
            out=dbg_gmax[:], in_=gmax[:]), deps=[tok_init], dma=True,
            chan="c9")
        r_gmax.read(t)

    def ln_stats(tile_buf, tile_rw, sq_bufs, sq_rws, T, pmean_i, pmsq_i,
                 pmu_i, prs_i, chunk_toks, eps):
        mm_last = None
        for chh in range(NCH):
            sqi = chh % 2
            tok_sq = S.do("scalar", lambda eng, _c=chh, _s=sqi: eng.activation(
                out=sq_bufs[_s][:, 0:T],
                in_=tile_buf[:, _c * T:(_c + 1) * T], func=AF.Square),
                deps=[chunk_toks[chh]] + sq_rws[sqi].wdep())
            sq_rws[sqi].wrote(tok_sq)
            tile_rw.read(tok_sq)

            tok_mm = S.do("tensor", lambda eng, _c=chh, _s=sqi: [
                eng.matmul(PS[pmean_i][0:1, 0:T], avgc_t[:],
                           tile_buf[:, _c * T:(_c + 1) * T],
                           start=(_c == 0), stop=(_c == NCH - 1)),
                eng.matmul(PS[pmsq_i][0:1, 0:T], avgc_t[:],
                           sq_bufs[_s][:, 0:T], start=(_c == 0),
                           stop=(_c == NCH - 1)),
            ], deps=[chunk_toks[chh], tok_sq, tok_consts] +
                (PB[pmean_i].wdep() + PB[pmsq_i].wdep() if chh == 0 else []),
                n_inst=2)
            sq_rws[sqi].read(tok_mm)
            tile_rw.read(tok_mm)
            mm_last = tok_mm
        PB[pmean_i].wrote(mm_last)
        PB[pmsq_i].wrote(mm_last)

        t1 = S.do("vector", lambda eng: [
            eng.tensor_copy(out=stat_sb[0:1, 0:T], in_=PS[pmean_i][0:1, 0:T]),
            eng.tensor_copy(out=stat_sb[0:1, 512:512 + T],
                            in_=PS[pmsq_i][0:1, 0:T]),
        ], deps=[mm_last] + r_stat.wdep(), n_inst=2)
        r_stat.wrote(t1)
        PB[pmean_i].read(t1)
        PB[pmsq_i].read(t1)
        t2 = S.do("vector", lambda eng: eng.tensor_mul(
            stat_sb[0:1, 1024:1024 + T], stat_sb[0:1, 0:T],
            stat_sb[0:1, 0:T]), deps=[t1])
        r_stat.wrote(t2, reset=False)
        t3 = S.do("vector", lambda eng: eng.tensor_sub(
            stat_sb[0:1, 512:512 + T], stat_sb[0:1, 512:512 + T],
            stat_sb[0:1, 1024:1024 + T]), deps=[t2])
        r_stat.wrote(t3, reset=False)
        t4 = S.do("vector", lambda eng, _e=eps: eng.tensor_scalar(
            out=stat_sb[0:1, 512:512 + T], in0=stat_sb[0:1, 512:512 + T],
            scalar1=_e, scalar2=None, op0=ALU.add), deps=[t3])
        r_stat.wrote(t4, reset=False)
        t5 = S.do("scalar", lambda eng: eng.activation(
            out=stat_sb[0:1, 1024:1024 + T], in_=stat_sb[0:1, 512:512 + T],
            func=AF.Sqrt), deps=[t4])
        r_stat.wrote(t5, reset=False)
        t6 = S.do("vector", lambda eng: eng.reciprocal(
            out=stat_sb[0:1, 1536:1536 + T], in_=stat_sb[0:1, 1024:1024 + T]),
            deps=[t5])
        r_stat.wrote(t6, reset=False)

        tok_bmu = S.do("tensor", lambda eng: eng.matmul(
            PS[pmu_i][:, 0:T], ones1_t[:], stat_sb[0:1, 0:T],
            start=True, stop=True), deps=[t6] + PB[pmu_i].wdep())
        PB[pmu_i].wrote(tok_bmu)
        r_stat.read(tok_bmu)
        tok_brs = S.do("tensor", lambda eng: eng.matmul(
            PS[prs_i][:, 0:T], ones1_t[:], stat_sb[0:1, 1536:1536 + T],
            start=True, stop=True), deps=[t6] + PB[prs_i].wdep())
        PB[prs_i].wrote(tok_brs)
        r_stat.read(tok_brs)
        return tok_bmu, tok_brs

    def ln_norm_chunk(out_buf, out_rw, out_off, z_buf, z_rw, z_off, T,
                      pmu_i, prs_i, tok_bmu, tok_brs, gcol, bcol, first):
        n1 = S.do("vector", lambda eng: eng.tensor_sub(
            out_buf[:, out_off:out_off + T], z_buf[:, z_off:z_off + T],
            PS[pmu_i][:, 0:T]),
            deps=[tok_bmu, tok_brs] + z_rw.rdep() +
            (out_rw.wdep() if first else []))
        out_rw.wrote(n1, reset=first)
        z_rw.read(n1)
        PB[pmu_i].read(n1)
        n2 = S.do("vector", lambda eng: eng.tensor_mul(
            out_buf[:, out_off:out_off + T], out_buf[:, out_off:out_off + T],
            PS[prs_i][:, 0:T]), deps=[n1])
        out_rw.wrote(n2, reset=False)
        PB[prs_i].read(n2)
        last = n2
        if gcol is not None:
            n3 = S.do("scalar", lambda eng: eng.activation(
                out=out_buf[:, out_off:out_off + T],
                in_=out_buf[:, out_off:out_off + T],
                func=AF.Identity, bias=bcol, scale=gcol), deps=[n2])
            out_rw.wrote(n3, reset=False)
            last = n3
        return last

    def ld3(eng, sbuf_t, hbm_ap, t0, T):
        dst = sbuf_t[:].rearrange("p (c t) -> p c t", t=T)
        srcr = hbm_ap.rearrange("(c p) n -> p c n", p=128)[:, :, t0:t0 + T]
        return eng.dma_start(out=dst, in_=srcr)

    def st3(eng, hbm_ap, sbuf_t, t0, T):
        dstr = hbm_ap.rearrange("(c p) n -> p c n", p=128)[:, :, t0:t0 + T]
        srcs = sbuf_t[:].rearrange("p (c t) -> p c t", t=T)
        return eng.dma_start(out=dstr, in_=srcs)

    def split3s(srcap, hap, lap, tmpap, deps):
        t1 = S.do("scalar", lambda eng: eng.activation(
            out=hap, in_=srcap, func=AF.Copy), deps=deps)
        t2 = S.do("vector", lambda eng: eng.tensor_sub(tmpap, srcap, hap),
                  deps=[t1])
        t3 = S.do("vector", lambda eng: eng.tensor_copy(out=lap, in_=tmpap),
                  deps=[t2])
        return t3

    TA = 512
    NT = nt if nt is not None else TPC // TA

    phA = ExitStack()
    wkh = sb("wkh", [128, NCH * D], BF16, stack=phA)
    wkl = sb("wkl", [128, NCH * D], BF16, stack=phA)
    wvh = sb("wvh", [128, NCH * D], BF16, stack=phA)
    wvl = sb("wvl", [128, NCH * D], BF16, stack=phA)
    wstg = [sb("wstg%d" % i, [128, D // 2], stack=phA) for i in range(2)]
    wtmp = [sb("wtmp%d" % i, [128, D // 2], stack=phA) for i in range(2)]
    xb = [sb("xb0", [128, NCH * TA], stack=phA)]
    xnh = [sb("xnh%d" % i, [128, NCH * TA], BF16, stack=phA) for i in range(2)]
    xnl = [sb("xnl%d" % i, [128, NCH * TA], BF16, stack=phA) for i in range(2)]
    xtc = [sb("xtc%d" % i, [128, TA], stack=phA) for i in range(2)]
    kfm = sb("kfm", [128, NCH * TA], stack=phA)
    sqb = [sb("sqb%d" % i, [128, TA], stack=phA) for i in range(2)]
    kdneg = sb("kdneg", [16, TA], stack=phA)
    kdT = sb("kdT", [128, 4 * 16], stack=phA)
    ek_ext = sb("ek_ext", [128, 4 * H * 65], stack=phA)
    v_ext = sb("v_ext", [128, 4 * H * 65], stack=phA)
    kdtm = [sb("kdtm%d" % i, [128, 128], stack=phA) for i in range(2)]

    r_xb = [RW()]
    r_xnh = [RW(), RW()]
    r_xnl = [RW(), RW()]
    r_xtc = [RW(), RW()]
    r_wstg = [RW(), RW()]
    r_wtmp = [RW(), RW()]
    r_wk = RW()
    r_wv = RW()
    r_kfm = RW()
    r_sq = [RW(), RW()]
    r_kdneg = RW()
    r_kdT = RW()
    r_ek = RW()
    r_v = RW()
    r_kdtm = [RW(), RW()]

    def ld_split_w(srct, dsth, dstl, chans, rw):
        last = None
        half_toks = [None, None]
        HDW = D // 2
        for i in range(2 * NCH):
            dd, hv = i % NCH, i // NCH
            sg = i % 2
            tok_l = S.do("sync", lambda eng, _d=dd, _h=hv, _s=sg:
                         eng.dma_start(
                out=wstg[_s][:],
                in_=srct[_d * 128:(_d + 1) * 128, _h * HDW:(_h + 1) * HDW]),
                deps=r_wstg[sg].wdep(), dma=True, chan=chans[sg])
            r_wstg[sg].wrote(tok_l)
            tok_s = split3s(
                wstg[sg][:],
                dsth[:, dd * D + hv * HDW:dd * D + (hv + 1) * HDW],
                dstl[:, dd * D + hv * HDW:dd * D + (hv + 1) * HDW],
                wtmp[sg][:],
                [tok_l] + r_wtmp[sg].wdep() + (rw.wdep() if i == 0 else []))
            r_wstg[sg].read(tok_s)
            r_wtmp[sg].wrote(tok_s)
            rw.wrote(tok_s, reset=(i == 0))
            half_toks[hv] = tok_s
            last = tok_s
        return half_toks

    tok_wk = ld_split_w(wk, wkh, wkl, ("c1", "c2"), r_wk)
    tok_wv = ld_split_w(wv, wvh, wvl, ("c10", "c11"), r_wv)

    tok_ones = S.do("vector", lambda eng: [
        eng.memset(ek_ext.rearrange("p (a c) -> p a c", c=65)[:, :, 64:65], 1.0),
        eng.memset(v_ext.rearrange("p (a c) -> p a c", c=65)[:, :, 64:65], 1.0),
    ], n_inst=2)
    r_ek.wrote(tok_ones)
    r_v.wrote(tok_ones)

    for it in range(NT):
        bf = it % 2
        t0 = it * TA
        slot0 = (it // 4) * 16

        tok_x = S.do("sync", lambda eng, _t0=t0: ld3(
            eng, xb[0], xT, _t0, TA), deps=r_xb[0].wdep(), dma=True, n_dma=1,
            chan="c3" if bf == 0 else "c4")
        r_xb[0].wrote(tok_x)
        xtoks = [tok_x] * NCH

        tok_bmu, tok_brs = ln_stats(xb[0], r_xb[0], sqb, r_sq, TA,
                                    6, 7, 4, 5, xtoks, EPS_OUT)

        xn_toks = []
        for chh in range(NCH):
            tb = chh % 2
            t = ln_norm_chunk(xtc[tb], r_xtc[tb], 0, xb[0], r_xb[0],
                              chh * TA, TA, 4, 5, tok_bmu, tok_brs,
                              None, None, first=True)
            tsp = split3s(xtc[tb][:, 0:TA],
                          xnh[bf][:, chh * TA:(chh + 1) * TA],
                          xnl[bf][:, chh * TA:(chh + 1) * TA],
                          sqb[tb][:, 0:TA],
                          [t] + r_sq[tb].wdep() +
                          (r_xnh[bf].wdep() + r_xnl[bf].wdep()
                           if chh == 0 else []))
            r_xtc[tb].read(tsp)
            r_sq[tb].wrote(tsp)
            r_xnh[bf].wrote(tsp, reset=(chh == 0))
            r_xnl[bf].wrote(tsp, reset=(chh == 0))
            xn_toks.append(tsp)

        tok_sp = S.do("gpsimd", lambda eng, _t0=t0, _b=bf: [
            st3(eng, xnh_h, xnh[_b], _t0, TA),
            st3(eng, xnl_h, xnl[_b], _t0, TA)],
            deps=xn_toks, dma=True, n_dma=2,
            chan="c5" if bf == 0 else "c6")
        r_xnh[bf].read(tok_sp)
        r_xnl[bf].read(tok_sp)

        kev_toks = []
        kd_last = None
        for kch in (range(NCH) if "k" in parts else []):
            pi = kch % 2
            tok_k = S.do("tensor", lambda eng, _k=kch, _b=bf: [
                eng.matmul(PS[_k % 2][:, 0:TA],
                           wop[:, dd * D + _k * 128: dd * D + (_k + 1) * 128],
                           xop[:, dd * TA:(dd + 1) * TA],
                           start=(pp == 0 and dd == 0),
                           stop=(pp == 2 and dd == NCH - 1))
                for pp, (wop, xop) in enumerate(
                    ((wkh, xnh[_b]), (wkh, xnl[_b]), (wkl, xnh[_b])))
                for dd in range(NCH)],
                deps=[tok_wk[kch // 4]] + xn_toks + PB[pi].wdep(),
                n_inst=24)
            PB[pi].wrote(tok_k)
            r_xnh[bf].read(tok_k)
            r_xnl[bf].read(tok_k)
            r_wk.read(tok_k)

            tok_kev = S.do("vector", lambda eng, _k=kch: eng.tensor_copy(
                out=kfm[:, _k * TA:(_k + 1) * TA], in_=PS[_k % 2][:, 0:TA]),
                deps=[tok_k] + (r_kfm.wdep() if kch == 0 else []))
            PB[pi].read(tok_kev)
            kev_toks.append(tok_kev)

            if "q" not in parts:
                kd_last = tok_kev
                continue
            tok_ksq = S.do("scalar", lambda eng, _k=kch: eng.activation(
                out=sqb[_k % 2][:, 0:TA], in_=PS[_k % 2][:, 0:TA],
                func=AF.Square), deps=[tok_k, tok_kev] + r_sq[kch % 2].wdep())
            r_sq[kch % 2].wrote(tok_ksq)
            PB[pi].read(tok_ksq)

            if "w" not in parts:
                kd_last = tok_ksq
                continue
            tok_kdmm = S.do("tensor", lambda eng, _k=kch: eng.matmul(
                PS[6][0:16, 0:TA], blockneg_t[:, _k * 16:(_k + 1) * 16],
                sqb[_k % 2][:, 0:TA], start=(_k == 0), stop=(_k == NCH - 1)),
                deps=[tok_ksq] + (PB[6].wdep() if kch == 0 else []))
            r_sq[kch % 2].read(tok_kdmm)
            kd_last = tok_kdmm
        if "k" not in parts:
            continue
        r_kfm.wrote(kev_toks[0])
        for t in kev_toks[1:]:
            r_kfm.wrote(t, reset=False)
        if "q" not in parts or "w" not in parts:
            continue
        PB[6].wrote(kd_last)

        tok_kdev = S.do("vector", lambda eng: eng.tensor_copy(
            out=kdneg[:], in_=PS[6][0:16, 0:TA]),
            deps=[kd_last] + r_kdneg.wdep())
        r_kdneg.wrote(tok_kdev)
        PB[6].read(tok_kdev)

        if "t" not in parts:
            continue
        tok_ktr = S.do("tensor", lambda eng: [
            eng.transpose(PS[7][:, s * 16:(s + 1) * 16],
                          kdneg[:, s * 128:(s + 1) * 128], eye_t[0:16, 0:16])
            for s in range(4)], deps=[tok_kdev, tok_consts] + PB[7].wdep(),
            n_inst=4)
        PB[7].wrote(tok_ktr)
        r_kdneg.read(tok_ktr)
        tok_ktev = S.do("vector", lambda eng: eng.tensor_copy(
            out=kdT[:], in_=PS[7][:, 0:64]), deps=[tok_ktr] + r_kdT.wdep())
        r_kdT.wrote(tok_ktev)
        PB[7].read(tok_ktev)

        ek_toks = {}
        first_ek = True
        for j in (range(8) if "d" in parts else []):
            for s in range(4):
                pi = 2 + ((j * 4 + s) % 2)
                tok_kd = S.do("tensor", lambda eng, _j=j, _s=s, _p=pi: eng.matmul(
                    PS[_p][:, 0:128],
                    kfm[:, _j * TA + _s * 128: _j * TA + (_s + 1) * 128],
                    projbd_t[:], start=True, stop=True),
                    deps=[kev_toks[j], tok_init] + PB[pi].wdep())
                PB[pi].wrote(tok_kd)
                r_kfm.read(tok_kd)

                ti = (j * 4 + s) % 2
                tok_cp = S.do("vector", lambda eng, _p=pi, _t=ti:
                              eng.tensor_copy(out=kdtm[_t][:],
                                              in_=PS[_p][:, 0:128]),
                    deps=[tok_kd] + r_kdtm[ti].wdep())
                r_kdtm[ti].wrote(tok_cp)
                PB[pi].read(tok_cp)

                tok_red = S.do("vector", lambda eng, _t=ti: [
                    eng.reduce_max(out=maxscr[:, 0:1], in_=kdtm[_t][:, 0:64],
                                   axis=AX.X),
                    eng.reduce_max(out=maxscr[:, 1:2], in_=kdtm[_t][:, 64:128],
                                   axis=AX.X),
                ], deps=[tok_cp] + r_maxscr.wdep(), n_inst=2)
                r_maxscr.wrote(tok_red)
                r_kdtm[ti].read(tok_red)

                tok_gm = S.do("vector", lambda eng, _j=j: [
                    eng.tensor_max(gmax[:, 2 * _j:2 * _j + 1],
                                   gmax[:, 2 * _j:2 * _j + 1],
                                   maxscr[:, 0:1]),
                    eng.tensor_max(gmax[:, 2 * _j + 1:2 * _j + 2],
                                   gmax[:, 2 * _j + 1:2 * _j + 2],
                                   maxscr[:, 1:2]),
                ], deps=[tok_red] + r_gmax.wdep(), n_inst=2)
                r_gmax.wrote(tok_gm, reset=False)
                r_maxscr.read(tok_gm)


                tok_ek = S.do("scalar", lambda eng, _j=j, _s=s, _t=ti: [
                    eng.activation(
                        out=ek_ext[:, _s * (H * 65) + (2 * _j + hh) * 65:
                                   _s * (H * 65) + (2 * _j + hh) * 65 + 64],
                        in_=kdtm[_t][:, hh * 64:(hh + 1) * 64],
                        func=AF.Exp,
                        bias=kdT[:, _s * 16 + 2 * _j + hh:
                                 _s * 16 + 2 * _j + hh + 1],
                        scale=1.0)
                    for hh in range(2)],
                    deps=[tok_cp, tok_ktev] +
                    (r_ek.wdep() if first_ek else []), n_inst=2)
                first_ek = False
                r_kdT.read(tok_ek)
                r_kdtm[ti].read(tok_ek)
                ek_toks[(j, s)] = tok_ek
        if "d" in parts:
            r_ek.wrote(ek_toks[(0, 0)])
            for key, t in ek_toks.items():
                if key != (0, 0):
                    r_ek.wrote(t, reset=False)

        v_toks = []
        firstv = True
        for s in (range(4) if "v" in parts else []):
            for half in range(2):
                pi = 4 + ((s * 2 + half) % 2)
                tok_v = S.do("tensor", lambda eng, _s=s, _h=half, _b=bf,
                             _p=pi: [
                    eng.matmul(PS[_p][:, 0:512],
                               xop[:, dd * TA + _s * 128:
                                   dd * TA + (_s + 1) * 128],
                               wop[:, dd * D + _h * 512:
                                    dd * D + (_h + 1) * 512],
                               start=(pp == 0 and dd == 0),
                               stop=(pp == 2 and dd == NCH - 1))
                    for pp, (xop, wop) in enumerate(
                        ((xnh[_b], wvh), (xnh[_b], wvl), (xnl[_b], wvh)))
                    for dd in range(NCH)],
                    deps=[tok_wv[kch // 4]] + xn_toks + PB[pi].wdep(),
                    n_inst=24)
                PB[pi].wrote(tok_v)
                r_xnh[bf].read(tok_v)
                r_xnl[bf].read(tok_v)
                r_wv.read(tok_v)

                tok_vev = S.do("vector", lambda eng, _s=s, _h=half, _p=pi:
                               eng.tensor_copy(
                    out=v_ext.rearrange("p (s h c) -> p s h c", s=4, c=65)[
                        :, _s, _h * 8:(_h + 1) * 8, 0:64],
                    in_=PS[_p][:, 0:512].rearrange("p (h c) -> p h c", c=64)),
                    deps=[tok_v] + (r_v.wdep() if firstv else []))
                firstv = False
                PB[pi].read(tok_vev)
                v_toks.append(tok_vev)
        if "v" in parts:
            r_v.wrote(v_toks[0])
            for t in v_toks[1:]:
                r_v.wrote(t, reset=False)

        for h in (range(H) if "z" in parts else []):
            pi = 6 + (h % 2)
            tok_kv = S.do("tensor", lambda eng, _h=h, _p=pi: [
                eng.matmul(PS[_p][0:65, 0:65],
                           ek_ext[:, s * (H * 65) + _h * 65:
                                  s * (H * 65) + _h * 65 + 65],
                           v_ext[:, s * (H * 65) + _h * 65:
                                 s * (H * 65) + _h * 65 + 65],
                           start=(s == 0), stop=(s == 3))
                for s in range(4)],
                deps=[ek_toks[(h // 2, sx)] for sx in range(4)] + v_toks +
                PB[pi].wdep() + [tok_ones], n_inst=4)
            PB[pi].wrote(tok_kv)
            r_ek.read(tok_kv)
            r_v.read(tok_kv)

            tok_acc = S.do("vector", lambda eng, _h=h, _p=pi, _sl=slot0:
                           eng.tensor_add(
                kv_acc[0:65, (_sl + _h) * 65:(_sl + _h) * 65 + 65],
                kv_acc[0:65, (_sl + _h) * 65:(_sl + _h) * 65 + 65],
                PS[_p][0:65, 0:65]), deps=[tok_kv] + r_kv.wdep())
            r_kv.wrote(tok_acc, reset=False)
            PB[pi].read(tok_acc)

    if upto == "A0":
        S.wait_only("gpsimd", S.barrier_tokens())
        S.flush(block)
        phA.close()
        ctx.close()
        return nc

    tok_gred = S.do("vector", lambda eng: eng.reduce_max(
        out=maxscr[:, 2:3], in_=gmax[:], axis=AX.X),
        deps=r_gmax.wdep() + r_maxscr.wdep())
    r_maxscr.wrote(tok_gred)
    r_gmax.read(tok_gred)

    tok_gtr = S.do("tensor", lambda eng: eng.transpose(
        PS[7][0:1, 0:128], maxscr[:, 2:3], eye_t[:]),
        deps=[tok_gred] + PB[7].wdep())
    PB[7].wrote(tok_gtr)
    r_maxscr.read(tok_gtr)

    g1tok = S.do("vector", lambda eng: eng.tensor_copy(
        out=mxrow[0:1, 0:128], in_=PS[7][0:1, 0:128]),
        deps=[tok_gtr] + r_mxrow.wdep())
    r_mxrow.wrote(g1tok)
    PB[7].read(g1tok)
    g2tok = S.do("vector", lambda eng: eng.reduce_max(
        out=mxrow[0:1, 128:129], in_=mxrow[0:1, 0:128], axis=AX.X),
        deps=[g1tok])
    r_mxrow.wrote(g2tok, reset=False)
    g3tok = S.do("vector", lambda eng: eng.memset(mx_sb[0:1, 0:16], 1.0),
                 deps=r_mx.wdep())
    r_mx.wrote(g3tok)
    g4tok = S.do("vector", lambda eng: eng.tensor_scalar(
        out=mx_sb[0:1, 0:16], in0=mx_sb[0:1, 0:16],
        scalar1=mxrow[0:1, 128:129], scalar2=None, op0=ALU.mult),
        deps=[g2tok, g3tok])
    r_mx.wrote(g4tok, reset=False)
    r_mxrow.read(g4tok)

    tok_ccs = S.do("gpsimd", lambda eng: eng.dma_start(
        out=ccin[:], in_=mx_sb[0:1, 0:16]), deps=[g4tok], dma=True,
        chan="c7")
    r_mx.read(tok_ccs)

    if use_cc:
        tok_cc = S.do("gpsimd", lambda eng: eng.collective_compute(
            "AllReduce", ALU.max, replica_groups=[[0, 1, 2, 3, 4, 5, 6, 7]],
            ins=[ccin[:]], outs=[ccout[:]]), deps=[tok_ccs], inc_last=True)
    else:
        tok_cc = tok_ccs

    tok_mxl = S.do("sync", lambda eng: eng.dma_start(
        out=mx_sb[0:1, 0:16], in_=(ccout[:] if use_cc else ccin[:])),
        deps=[tok_cc] + r_mx.wdep(), dma=True, chan="c8")
    r_mx.wrote(tok_mxl)

    if "dbg_mx" in debug:
        t = S.do("gpsimd", lambda eng: eng.dma_start(
            out=dbg_mx[:], in_=mx_sb[0:1, 0:16]), deps=[tok_mxl], dma=True,
            chan="c9")
        r_mx.read(t)
    if "dbg_aend" in debug:
        t = S.do("gpsimd", lambda eng: eng.dma_start(
            out=dbg_aend[:, 0:NCH * 512], in_=kfm[:]), deps=S.barrier_tokens()[:0] or [tok_gred], dma=True, chan="c9")
        r_kfm.read(t)
        t = S.do("gpsimd", lambda eng: eng.dma_start(
            out=dbg_aend[:, 4096:4096 + 128], in_=projbd_t[:]), deps=[tok_gred], dma=True, chan="c9")
        t = S.do("gpsimd", lambda eng: eng.dma_start(
            out=dbg_aend[:, 4224:4224 + 64], in_=kdT[:]), deps=[tok_gred], dma=True, chan="c9")
        r_kdT.read(t)
        t = S.do("gpsimd", lambda eng: eng.dma_start(
            out=dbg_aend[:, 4288:4288 + 4160], in_=ek_ext[:]), deps=[tok_gred], dma=True, chan="c9")
        r_ek.read(t)
    if "dbg_kdtm" in debug:
        t = S.do("gpsimd", lambda eng: eng.dma_start(
            out=dbg_kd[:, 0:128], in_=kdtm[1][:]), deps=[tok_gred], dma=True,
            chan="c9")
        r_kdtm[1].read(t)
        t = S.do("gpsimd", lambda eng: eng.dma_start(
            out=dbg_kd[:, 128:132], in_=maxscr[:]), deps=[tok_gred], dma=True,
            chan="c9")
        r_maxscr.read(t)
    if "dbg_lmx" in debug:
        t = S.do("gpsimd", lambda eng: eng.dma_start(
            out=dbg_lmx[:], in_=mxrow[0:1, 0:144]), deps=[g4tok], dma=True,
            chan="c9")
        r_mxrow.read(t)
        t = S.do("gpsimd", lambda eng: eng.dma_start(
            out=dbg_gmax[:], in_=gmax[:]), deps=[tok_gred], dma=True,
            chan="c9")
        r_gmax.read(t)

    tok_neg = S.do("vector", lambda eng: eng.tensor_scalar(
        out=mx_sb[0:1, 1:2], in0=mx_sb[0:1, 0:1], scalar1=-1.0, scalar2=None,
        op0=ALU.mult), deps=[tok_mxl])
    r_mx.wrote(tok_neg, reset=False)
    tok_c = S.do("scalar", lambda eng: eng.activation(
        out=mx_sb[0:1, 2:3], in_=mx_sb[0:1, 1:2], func=AF.Exp),
        deps=[tok_neg])
    r_mx.wrote(tok_c, reset=False)

    tok_bcc = S.do("tensor", lambda eng: eng.matmul(
        PS[7][:, 0:1], ones1_t[:], mx_sb[0:1, 2:3], start=True, stop=True),
        deps=[tok_c] + PB[7].wdep())
    PB[7].wrote(tok_bcc)
    r_mx.read(tok_bcc)

    sc1 = S.do("vector", lambda eng: eng.tensor_copy(
        out=scale_col[:], in_=PS[7][:, 0:1]), deps=[tok_bcc] + r_scol.wdep())
    r_scol.wrote(sc1)
    PB[7].read(sc1)
    sc2 = S.do("vector", lambda eng: eng.memset(scale_col[64:65, 0:1], EPS_K),
               deps=[sc1])
    r_scol.wrote(sc2, reset=False)

    tok_mp = S.do("vector", lambda eng: [
        eng.tensor_scalar(out=kv_acc[0:65, sl * 65:sl * 65 + 65],
                          in0=kv_acc[0:65, sl * 65:sl * 65 + 65],
                          scalar1=scale_col[0:65, 0:1], scalar2=None,
                          op0=ALU.mult)
        for sl in range(32)], deps=[sc2] + r_kv.wdep(), n_inst=32)
    r_kv.wrote(tok_mp)
    r_scol.read(tok_mp)

    if "dbg_kv" in debug:
        t = S.do("gpsimd", lambda eng: eng.dma_start(
            out=dbg_kv[:], in_=kv_acc[:]), deps=[tok_mp], dma=True, chan="c9")
        r_kv.read(t)

    S.flush(block)
    phA.close()
    S.phase_start()

    if upto == "A":
        S.wait_only("gpsimd", S.barrier_tokens())
        S.flush(block)
        ctx.close()
        return nc

    phB1 = ExitStack()
    wqh = sb("wqh", [128, NCH * D], BF16, stack=phB1)
    wql = sb("wql", [128, NCH * D], BF16, stack=phB1)
    woh = sb("woh", [128, NCH * D], BF16, stack=phB1)
    wol = sb("wol", [128, NCH * D], BF16, stack=phB1)
    wstg1 = [sb("wstg1_0", [128, D // 2], stack=phB1)]
    wtmp1 = [sb("wtmp1_0", [128, D // 2], stack=phB1)]
    xn1h = sb("xn1h", [128, NCH * TA], BF16, stack=phB1)
    xn1l = sb("xn1l", [128, NCH * TA], BF16, stack=phB1)
    attnh = sb("attnh", [128, NCH * TA], BF16, stack=phB1)
    attnl = sb("attnl", [128, NCH * TA], BF16, stack=phB1)
    tmpb1 = sb("tmpb1", [128, TA], stack=phB1)
    qfm = sb("qfm", [128, NCH * TA], stack=phB1)
    sqb1 = [sb("sqb1_%d" % i, [128, TA], stack=phB1) for i in range(2)]
    qdneg = sb("qdneg", [16, TA], stack=phB1)
    qdT = sb("qdT", [128, 4 * 16], stack=phB1)
    qf_ext = sb("qf_ext", [128, 4 * H * 65], stack=phB1)
    qfTe = sb("qfTe", [128, 512], stack=phB1)
    qfTo = sb("qfTo", [128, 512], stack=phB1)
    den_sb = sb("den_sb", [1, 2048], stack=phB1)
    biascol = sb("biascol", [128, 4], stack=phB1)
    attn_t = sb("attn_t", [128, NCH * TA], stack=phB1)
    z1 = sb("z1", [128, NCH * TA], stack=phB1)
    qdtm = [sb("qdtm%d" % i, [128, 128], stack=phB1) for i in range(2)]

    r_xn1h = RW()
    r_xn1l = RW()
    r_attnh = RW()
    r_attnl = RW()
    r_tmpb1 = RW()
    r_wstg1 = [RW(), RW()]
    r_wtmp1 = [RW(), RW()]
    r_wq = RW()
    r_wo = RW()
    r_qfm = RW()
    r_sq1 = [RW(), RW()]
    r_qdneg = RW()
    r_qdT = RW()
    r_qf = RW()
    r_qfTe = RW()
    r_qfTo = RW()
    r_den = RW()
    r_bias = RW()
    r_attn = RW()
    r_z1 = RW()
    r_qdtm = [RW(), RW()]

    def ld_split_w1(srct, dsth, dstl, chan, rw):
        last = None
        half_toks = [None, None]
        HD = D // 2
        for i in range(2 * NCH):
            dd, hv = i % NCH, i // NCH
            sg = 0
            tok_l = S.do("sync", lambda eng, _d=dd, _h=hv, _s=sg:
                         eng.dma_start(
                out=wstg1[_s][:],
                in_=srct[_d * 128:(_d + 1) * 128, _h * HD:(_h + 1) * HD]),
                deps=r_wstg1[sg].wdep(), dma=True, chan=chan)
            r_wstg1[sg].wrote(tok_l)
            tok_s = split3s(
                wstg1[sg][:],
                dsth[:, dd * D + hv * HD:dd * D + (hv + 1) * HD],
                dstl[:, dd * D + hv * HD:dd * D + (hv + 1) * HD],
                wtmp1[sg][:],
                [tok_l] + r_wtmp1[sg].wdep() +
                (rw.wdep() if i == 0 else []))
            r_wstg1[sg].read(tok_s)
            r_wtmp1[sg].wrote(tok_s)
            rw.wrote(tok_s, reset=(i == 0))
            half_toks[hv] = tok_s
            last = tok_s
        return half_toks

    tok_wq = ld_split_w1(wq, wqh, wql, "c1", r_wq)
    tok_wo = ld_split_w1(wo, woh, wol, "c2", r_wo)

    for it in range(NT):
        bf = it % 2
        t0 = it * TA
        slot0 = (it // 4) * 16

        tok_xn = S.do("sync", lambda eng, _t0=t0: [
            ld3(eng, xn1h, xnh_h, _t0, TA),
            ld3(eng, xn1l, xnl_h, _t0, TA)],
            deps=r_xn1h.wdep() + r_xn1l.wdep(), dma=True, n_dma=2,
            chan="c3")
        r_xn1h.wrote(tok_xn)
        r_xn1l.wrote(tok_xn)

        zpre_toks = []
        for chh in range(NCH):
            zp1 = S.do("vector", lambda eng, _c=chh: eng.tensor_add(
                z1[:, _c * TA:(_c + 1) * TA],
                xn1h[:, _c * TA:(_c + 1) * TA],
                xn1l[:, _c * TA:(_c + 1) * TA]),
                deps=[tok_xn] + (r_z1.wdep() if chh == 0 else []))
            r_z1.wrote(zp1, reset=(chh == 0))
            r_xn1h.read(zp1)
            r_xn1l.read(zp1)
            zpre_toks.append(zp1)

        qev_toks = []
        qd_last = None
        for kch in range(NCH):
            pi = kch % 2
            tok_q = S.do("tensor", lambda eng, _k=kch: [
                eng.matmul(PS[_k % 2][:, 0:TA],
                           wop[:, dd * D + _k * 128: dd * D + (_k + 1) * 128],
                           xop[:, dd * TA:(dd + 1) * TA],
                           start=(pp == 0 and dd == 0),
                           stop=(pp == 2 and dd == NCH - 1))
                for pp, (wop, xop) in enumerate(
                    ((wqh, xn1h), (wqh, xn1l), (wql, xn1h)))
                for dd in range(NCH)],
                deps=[tok_wq[kch // 4], tok_xn] + PB[pi].wdep(),
                n_inst=24)
            PB[pi].wrote(tok_q)
            r_xn1h.read(tok_q)
            r_xn1l.read(tok_q)
            r_wq.read(tok_q)

            tok_qev = S.do("vector", lambda eng, _k=kch: eng.tensor_copy(
                out=qfm[:, _k * TA:(_k + 1) * TA], in_=PS[_k % 2][:, 0:TA]),
                deps=[tok_q] + (r_qfm.wdep() if kch == 0 else []))
            PB[pi].read(tok_qev)
            qev_toks.append(tok_qev)

            tok_qsq = S.do("scalar", lambda eng, _k=kch: eng.activation(
                out=sqb1[_k % 2][:, 0:TA], in_=PS[_k % 2][:, 0:TA],
                func=AF.Square), deps=[tok_q, tok_qev] + r_sq1[kch % 2].wdep())
            r_sq1[kch % 2].wrote(tok_qsq)
            PB[pi].read(tok_qsq)

            tok_qdmm = S.do("tensor", lambda eng, _k=kch: eng.matmul(
                PS[6][0:16, 0:TA], blockneg_t[:, _k * 16:(_k + 1) * 16],
                sqb1[_k % 2][:, 0:TA], start=(_k == 0), stop=(_k == NCH - 1)),
                deps=[tok_qsq] + (PB[6].wdep() if kch == 0 else []))
            r_sq1[kch % 2].read(tok_qdmm)
            qd_last = tok_qdmm
        PB[6].wrote(qd_last)
        r_qfm.wrote(qev_toks[0])
        for t in qev_toks[1:]:
            r_qfm.wrote(t, reset=False)

        tok_qdev = S.do("vector", lambda eng: eng.tensor_copy(
            out=qdneg[:], in_=PS[6][0:16, 0:TA]),
            deps=[qd_last] + r_qdneg.wdep())
        r_qdneg.wrote(tok_qdev)
        PB[6].read(tok_qdev)

        tok_qtr = S.do("tensor", lambda eng: [
            eng.transpose(PS[7][:, s * 16:(s + 1) * 16],
                          qdneg[:, s * 128:(s + 1) * 128], eye_t[0:16, 0:16])
            for s in range(4)], deps=[tok_qdev] + PB[7].wdep(), n_inst=4)
        PB[7].wrote(tok_qtr)
        r_qdneg.read(tok_qtr)
        tok_qtev = S.do("vector", lambda eng: eng.tensor_copy(
            out=qdT[:], in_=PS[7][:, 0:64]), deps=[tok_qtr] + r_qdT.wdep())
        r_qdT.wrote(tok_qtev)
        PB[7].read(tok_qtev)

        qf_toks = {}
        first_qf = True
        for j in range(8):
            for s in range(4):
                pi = 2 + ((j * 4 + s) % 2)
                tok_qd = S.do("tensor", lambda eng, _j=j, _s=s, _p=pi: eng.matmul(
                    PS[_p][:, 0:128],
                    qfm[:, _j * TA + _s * 128: _j * TA + (_s + 1) * 128],
                    projbd_t[:], start=True, stop=True),
                    deps=[qev_toks[j], tok_init] + PB[pi].wdep())
                PB[pi].wrote(tok_qd)
                r_qfm.read(tok_qd)

                ti = (j * 4 + s) % 2
                tok_cp = S.do("vector", lambda eng, _p=pi, _t=ti:
                              eng.tensor_copy(out=qdtm[_t][:],
                                              in_=PS[_p][:, 0:128]),
                    deps=[tok_qd] + r_qdtm[ti].wdep())
                r_qdtm[ti].wrote(tok_cp)
                PB[pi].read(tok_cp)

                tok_red = S.do("vector", lambda eng, _t=ti: [
                    eng.reduce_max(out=maxscr[:, 0:1], in_=qdtm[_t][:, 0:64],
                                   axis=AX.X),
                    eng.reduce_max(out=maxscr[:, 1:2], in_=qdtm[_t][:, 64:128],
                                   axis=AX.X),
                ], deps=[tok_cp] + r_maxscr.wdep(), n_inst=2)
                r_maxscr.wrote(tok_red)
                r_qdtm[ti].read(tok_red)

                tok_bias = S.do("vector", lambda eng, _j=j, _s=s: [
                    eng.tensor_sub(biascol[:, 0:1],
                                   qdT[:, _s * 16 + 2 * _j:
                                       _s * 16 + 2 * _j + 1],
                                   maxscr[:, 0:1]),
                    eng.tensor_sub(biascol[:, 1:2],
                                   qdT[:, _s * 16 + 2 * _j + 1:
                                       _s * 16 + 2 * _j + 2],
                                   maxscr[:, 1:2]),
                ], deps=[tok_red, tok_qtev] + r_bias.wdep(), n_inst=2)
                r_bias.wrote(tok_bias)
                r_maxscr.read(tok_bias)
                r_qdT.read(tok_bias)

                tok_qf = S.do("scalar", lambda eng, _j=j, _s=s, _t=ti: [
                    eng.activation(
                        out=qf_ext[:, _s * (H * 65) + (2 * _j + hh) * 65:
                                   _s * (H * 65) + (2 * _j + hh) * 65 + 64],
                        in_=qdtm[_t][:, hh * 64:(hh + 1) * 64],
                        func=AF.Exp, bias=biascol[:, hh:hh + 1], scale=1.0)
                    for hh in range(2)],
                    deps=[tok_cp, tok_bias] +
                    (r_qf.wdep() if first_qf else []), n_inst=2)
                first_qf = False
                r_bias.read(tok_qf)
                r_qdtm[ti].read(tok_qf)
                qf_toks[(j, s)] = tok_qf
        r_qf.wrote(qf_toks[(0, 0)])
        for key, t in qf_toks.items():
            if key != (0, 0):
                r_qf.wrote(t, reset=False)

        geps_toks = []
        for s in range(4):
            tok_eps = S.do("vector", lambda eng, _s=s: eng.tensor_scalar(
                out=qf_ext.rearrange("p (s h c) -> p s h c", s=4, c=65)[
                    :, _s, :, 0:64],
                in0=qf_ext.rearrange("p (s h c) -> p s h c", s=4, c=65)[
                    :, _s, :, 0:64],
                scalar1=EPS_K, scalar2=None, op0=ALU.add),
                deps=[qf_toks[(jj, s)] for jj in range(8)])
            r_qf.wrote(tok_eps, reset=False)
            tok_g = S.do("vector", lambda eng, _s=s: [
                eng.reduce_sum(
                    out=qf_ext[:, _s * (H * 65) + h * 65 + 64:
                               _s * (H * 65) + h * 65 + 65],
                    in_=qf_ext[:, _s * (H * 65) + h * 65:
                               _s * (H * 65) + h * 65 + 64],
                    axis=AX.X)
                for h in range(H)], deps=[tok_eps], n_inst=16)
            r_qf.wrote(tok_g, reset=False)
            geps_toks.append(tok_g)

        for j in range(8):
            he, ho = 2 * j, 2 * j + 1
            sle, slo = slot0 + he, slot0 + ho

            pT = j % 2
            tok_tre = S.do("tensor", lambda eng, _h=he, _p=pT: [
                eng.transpose(PS[_p][0:65, s * 128:(s + 1) * 128],
                              qf_ext[:, s * (H * 65) + _h * 65:
                                     s * (H * 65) + _h * 65 + 65], eye_t[:])
                for s in range(4)], deps=geps_toks + PB[pT].wdep(), n_inst=4)
            PB[pT].wrote(tok_tre)
            r_qf.read(tok_tre)
            tok_qfe = S.do("vector", lambda eng, _p=pT: eng.tensor_copy(
                out=qfTe[0:65, :], in_=PS[_p][0:65, 0:512]),
                deps=[tok_tre] + r_qfTe.wdep())
            r_qfTe.wrote(tok_qfe)
            PB[pT].read(tok_qfe)

            tok_tro = S.do("tensor", lambda eng, _h=ho, _p=pT: [
                eng.transpose(PS[5][0:65, s * 128:(s + 1) * 128],
                              qf_ext[:, s * (H * 65) + _h * 65:
                                     s * (H * 65) + _h * 65 + 65], eye_t[:])
                for s in range(4)], deps=geps_toks + PB[5].wdep(), n_inst=4)
            PB[5].wrote(tok_tro)
            r_qf.read(tok_tro)
            tok_qfo = S.do("vector", lambda eng: eng.tensor_copy(
                out=qfTo[0:65, :], in_=PS[5][0:65, 0:512]),
                deps=[tok_tro] + r_qfTo.wdep())
            r_qfTo.wrote(tok_qfo)
            PB[5].read(tok_qfo)

            tok_nume = S.do("tensor", lambda eng, _sl=sle: eng.matmul(
                PS[4][0:64, 0:512], kv_acc[0:65, _sl * 65:_sl * 65 + 64],
                qfTe[0:65, :], start=True, stop=True),
                deps=[tok_qfe, tok_mp] + PB[4].wdep())
            PB[4].wrote(tok_nume)
            r_qfTe.read(tok_nume)
            r_kv.read(tok_nume)
            tok_numo = S.do("tensor", lambda eng, _sl=slo: eng.matmul(
                PS[4][64:128, 0:512], kv_acc[0:65, _sl * 65:_sl * 65 + 64],
                qfTo[0:65, :], start=True, stop=True),
                deps=[tok_qfo, tok_mp] + PB[4].wdep())
            PB[4].wrote(tok_numo, reset=False)
            r_qfTo.read(tok_numo)
            r_kv.read(tok_numo)

            tok_dene = S.do("tensor", lambda eng, _sl=sle: eng.matmul(
                PS[2][0:1, 0:512], kv_acc[0:65, _sl * 65 + 64:_sl * 65 + 65],
                qfTe[0:65, :], start=True, stop=True),
                deps=[tok_qfe, tok_mp] + PB[2].wdep())
            PB[2].wrote(tok_dene)
            r_qfTe.read(tok_dene)
            tok_deno = S.do("tensor", lambda eng, _sl=slo: eng.matmul(
                PS[3][0:1, 0:512], kv_acc[0:65, _sl * 65 + 64:_sl * 65 + 65],
                qfTo[0:65, :], start=True, stop=True),
                deps=[tok_qfo, tok_mp] + PB[3].wdep())
            PB[3].wrote(tok_deno)
            r_qfTo.read(tok_deno)

            d1 = S.do("vector", lambda eng: [
                eng.tensor_copy(out=den_sb[0:1, 0:512], in_=PS[2][0:1, 0:512]),
                eng.tensor_copy(out=den_sb[0:1, 512:1024],
                                in_=PS[3][0:1, 0:512]),
            ], deps=[tok_dene, tok_deno] + r_den.wdep(), n_inst=2)
            r_den.wrote(d1)
            PB[2].read(d1)
            PB[3].read(d1)
            d2a = S.do("vector", lambda eng: eng.reciprocal(
                out=den_sb[0:1, 1024:1536], in_=den_sb[0:1, 0:512]),
                deps=[d1])
            r_den.wrote(d2a, reset=False)
            d2b = S.do("vector", lambda eng: eng.reciprocal(
                out=den_sb[0:1, 1536:2048], in_=den_sb[0:1, 512:1024]),
                deps=[d1])
            r_den.wrote(d2b, reset=False)

            tok_bden = S.do("tensor", lambda eng: [
                eng.matmul(PS[6][:, 0:512], maskeo_t[0:1, 0:128],
                           den_sb[0:1, 1024:1536], start=True, stop=False),
                eng.matmul(PS[6][:, 0:512], maskeo_t[0:1, 128:256],
                           den_sb[0:1, 1536:2048], start=False, stop=True),
            ], deps=[d2b, tok_consts] + PB[6].wdep(), n_inst=2)
            PB[6].wrote(tok_bden)
            r_den.read(tok_bden)

            a1 = S.do("vector", lambda eng, _j=j: eng.tensor_copy(
                out=attn_t[:, _j * TA:(_j + 1) * TA], in_=PS[4][:, 0:512]),
                deps=[tok_nume, tok_numo] +
                (r_attn.wdep() if j == 0 else []))
            r_attn.wrote(a1, reset=(j == 0))
            PB[4].read(a1)
            a2 = S.do("vector", lambda eng, _j=j: eng.tensor_mul(
                attn_t[:, _j * TA:(_j + 1) * TA],
                attn_t[:, _j * TA:(_j + 1) * TA], PS[6][:, 0:512]),
                deps=[a1, tok_bden])
            r_attn.wrote(a2, reset=False)
            PB[6].read(a2)
            a3 = split3s(attn_t[:, j * TA:(j + 1) * TA],
                         attnh[:, j * TA:(j + 1) * TA],
                         attnl[:, j * TA:(j + 1) * TA], tmpb1[:],
                         [a2] + r_tmpb1.wdep() +
                         (r_attnh.wdep() + r_attnl.wdep() if j == 0 else []))
            r_attn.read(a3)
            r_tmpb1.wrote(a3)
            r_attnh.wrote(a3, reset=(j == 0))
            r_attnl.wrote(a3, reset=(j == 0))

        attn_toks = list(r_attnh.w) + list(r_attnl.w)

        if it == 0 and "dbg_attn" in debug:
            t = S.do("gpsimd", lambda eng: eng.dma_start(
                out=dbg_attn[:], in_=attn_t[:]), deps=attn_toks, dma=True,
                chan="c9")
            r_attn.read(t)
            t = S.do("gpsimd", lambda eng: eng.dma_start(
                out=dbg_kv[:], in_=kv_acc[:]), deps=attn_toks, dma=True,
                chan="c9")
            r_kv.read(t)
            t = S.do("gpsimd", lambda eng: eng.dma_start(
                out=dbg_qf[:], in_=qf_ext[:]), deps=geps_toks, dma=True,
                chan="c9")
            r_qf.read(t)
            t = S.do("gpsimd", lambda eng: eng.dma_start(
                out=dbg_qft[:], in_=qfTo[:]), deps=r_qfTo.rdep(), dma=True,
                chan="c9")
            r_qfTo.read(t)
            t = S.do("gpsimd", lambda eng: eng.dma_start(
                out=dbg_den[:], in_=den_sb[:]), deps=r_den.rdep(), dma=True,
                chan="c9")
            r_den.read(t)

        z1_toks = []
        for o in range(NCH):
            pi = o % 2
            tok_ao = S.do("tensor", lambda eng, _o=o: [
                eng.matmul(PS[_o % 2][:, 0:TA],
                           wop[:, p * D + _o * 128: p * D + (_o + 1) * 128],
                           aop[:, p * TA:(p + 1) * TA],
                           start=(pp == 0 and p == 0),
                           stop=(pp == 2 and p == NCH - 1))
                for pp, (wop, aop) in enumerate(
                    ((woh, attnh), (woh, attnl), (wol, attnh)))
                for p in range(NCH)],
                deps=[tok_wo[kch // 4]] + attn_toks + PB[pi].wdep(),
                n_inst=24)
            PB[pi].wrote(tok_ao)
            r_attnh.read(tok_ao)
            r_attnl.read(tok_ao)
            r_wo.read(tok_ao)

            tok_z1 = S.do("vector", lambda eng, _o=o: eng.tensor_add(
                z1[:, _o * TA:(_o + 1) * TA], z1[:, _o * TA:(_o + 1) * TA],
                PS[_o % 2][:, 0:TA]),
                deps=[tok_ao, zpre_toks[o]])
            r_z1.wrote(tok_z1, reset=False)
            PB[pi].read(tok_z1)
            z1_toks.append(tok_z1)

        tok_bmu, tok_brs = ln_stats(z1, r_z1, sqb1, r_sq1, TA,
                                    6, 7, 2, 3, z1_toks, EPS_LN)

        o1_toks = []
        for chh in range(NCH):
            t = ln_norm_chunk(attn_t, r_attn, chh * TA, z1, r_z1, chh * TA,
                              TA, 2, 3, tok_bmu, tok_brs,
                              g1_t[:, chh:chh + 1], b1l_t[:, chh:chh + 1],
                              first=(chh == 0))
            ts = split3s(attn_t[:, chh * TA:(chh + 1) * TA],
                         attnh[:, chh * TA:(chh + 1) * TA],
                         attnl[:, chh * TA:(chh + 1) * TA], tmpb1[:],
                         [t] + r_tmpb1.wdep() +
                         (r_attnh.wdep() + r_attnl.wdep()
                          if chh == 0 else []))
            r_attn.read(ts)
            r_tmpb1.wrote(ts)
            r_attnh.wrote(ts, reset=(chh == 0))
            r_attnl.wrote(ts, reset=(chh == 0))
            o1_toks.append(ts)

        tok_o1sp = S.do("gpsimd", lambda eng, _t0=t0: [
            st3(eng, o1h_h, attnh, _t0, TA),
            st3(eng, o1l_h, attnl, _t0, TA)],
            deps=o1_toks, dma=True, n_dma=2, chan="c5")
        r_attnh.read(tok_o1sp)
        r_attnl.read(tok_o1sp)

    S.flush(block)
    phB1.close()
    S.phase_start()

    if upto == "B1":
        S.wait_only("gpsimd", S.barrier_tokens())
        S.flush(block)
        ctx.close()
        return nc

    TB = 256
    NTB = TPC // TB

    phB2 = ExitStack()
    w1h = sb("w1h", [128, NCH * DFF], BF16, stack=phB2)
    w1l = sb("w1l", [128, NCH * DFF], BF16, stack=phB2)
    wstg2 = [sb("wstg2_%d" % i, [128, DFF // 2], stack=phB2)
             for i in range(2)]
    wtmp2 = [sb("wtmp2_%d" % i, [128, DFF // 2], stack=phB2)
             for i in range(2)]
    o1th = [sb("o1th%d" % i, [128, NCH * TB], BF16, stack=phB2)
            for i in range(2)]
    o1tl = [sb("o1tl%d" % i, [128, NCH * TB], BF16, stack=phB2)
            for i in range(2)]
    hcb = [sb("hcb%d" % i, [128, TB], stack=phB2) for i in range(2)]
    hcbh = [sb("hcbh%d" % i, [128, TB], BF16, stack=phB2) for i in range(2)]
    hcbl = [sb("hcbl%d" % i, [128, TB], BF16, stack=phB2) for i in range(2)]
    tmp1 = [sb("tmp1_%d" % i, [128, TB], stack=phB2) for i in range(2)]
    tmp2 = [sb("tmp2_%d" % i, [128, TB], stack=phB2) for i in range(2)]

    r_o1t = [RW(), RW()]
    r_hcb = [RW(), RW()]
    r_hcbh = [RW(), RW()]
    r_hcbl = [RW(), RW()]
    r_t1 = [RW(), RW()]
    r_t2 = [RW(), RW()]
    r_wstg2 = [RW(), RW()]
    r_wtmp2 = [RW(), RW()]
    r_w1 = RW()

    tok_w1 = None
    HF2 = DFF // 2
    for i in range(2 * NCH):
        dd, hv = i // 2, i % 2
        sg = i % 2
        tok_l = S.do("sync", lambda eng, _d=dd, _h=hv, _s=sg: eng.dma_start(
            out=wstg2[_s][:],
            in_=w1[_d * 128:(_d + 1) * 128, _h * HF2:(_h + 1) * HF2]),
            deps=r_wstg2[sg].wdep(), dma=True, chan="c1")
        r_wstg2[sg].wrote(tok_l)
        tok_s = split3s(
            wstg2[sg][:],
            w1h[:, dd * DFF + hv * HF2:dd * DFF + (hv + 1) * HF2],
            w1l[:, dd * DFF + hv * HF2:dd * DFF + (hv + 1) * HF2],
            wtmp2[sg][:],
            [tok_l] + r_wtmp2[sg].wdep() + (r_w1.wdep() if i == 0 else []))
        r_wstg2[sg].read(tok_s)
        r_wtmp2[sg].wrote(tok_s)
        r_w1.wrote(tok_s, reset=(i == 0))
        tok_w1 = tok_s

    for it in range(NTB):
        bf = it % 2
        t0 = it * TB

        tok_o1 = S.do("sync", lambda eng, _t0=t0, _b=bf: [
            ld3(eng, o1th[_b], o1h_h, _t0, TB),
            ld3(eng, o1tl[_b], o1l_h, _t0, TB)],
            deps=r_o1t[bf].wdep(), dma=True, n_dma=2,
            chan="c3" if bf == 0 else "c4")
        r_o1t[bf].wrote(tok_o1)

        for f in range(DFF // 128):
            pi = f % 4
            fb = f % 2
            tok_mm = S.do("tensor", lambda eng, _f=f, _b=bf, _p=pi: [
                eng.matmul(PS[_p][:, 0:TB],
                           wop[:, dd * DFF + _f * 128:
                               dd * DFF + (_f + 1) * 128],
                           xop[:, dd * TB:(dd + 1) * TB],
                           start=(pp == 0 and dd == 0),
                           stop=(pp == 2 and dd == NCH - 1))
                for pp, (wop, xop) in enumerate(
                    ((w1h, o1th[_b]), (w1h, o1tl[_b]), (w1l, o1th[_b])))
                for dd in range(NCH)],
                deps=[tok_w1, tok_o1] + PB[pi].wdep(), n_inst=24)
            PB[pi].wrote(tok_mm)
            r_o1t[bf].read(tok_mm)
            r_w1.read(tok_mm)

            e1 = S.do("scalar", lambda eng, _f=f, _p=pi, _fb=fb: [
                eng.activation(out=hcb[_fb][:], in_=PS[_p][:, 0:TB],
                               func=AF.Relu, bias=b1_t[:, _f:_f + 1],
                               scale=1.0),
                eng.activation(out=tmp1[_fb][:], in_=PS[_p][:, 0:TB],
                               func=AF.Relu, bias=nb1_t[:, _f:_f + 1],
                               scale=-1.0),
            ], deps=[tok_mm, tok_init] + r_hcb[fb].wdep() + r_t1[fb].wdep(),
                n_inst=2)
            r_hcb[fb].wrote(e1)
            r_t1[fb].wrote(e1)
            PB[pi].read(e1)
            e2 = S.do("scalar", lambda eng, _fb=fb: eng.activation(
                out=tmp2[_fb][:], in_=tmp1[_fb][:], func=AF.Exp, scale=-1.0),
                deps=[e1] + r_t2[fb].wdep())
            r_t2[fb].wrote(e2)
            r_t1[fb].read(e2)
            e3 = S.do("vector", lambda eng, _fb=fb: eng.tensor_scalar(
                out=tmp2[_fb][:], in0=tmp2[_fb][:], scalar1=-1.0,
                scalar2=None, op0=ALU.add), deps=[e2])
            r_t2[fb].wrote(e3, reset=False)
            e4 = S.do("vector", lambda eng, _fb=fb: eng.tensor_add(
                hcb[_fb][:], hcb[_fb][:], tmp2[_fb][:]), deps=[e3, e1])
            r_hcb[fb].wrote(e4, reset=False)
            r_t2[fb].read(e4)
            e5 = split3s(hcb[fb][:], hcbh[fb][:], hcbl[fb][:], tmp1[fb][:],
                         [e4] + r_t1[fb].wdep() + r_hcbh[fb].wdep() +
                         r_hcbl[fb].wdep())
            r_hcb[fb].read(e5)
            r_t1[fb].wrote(e5)
            r_hcbh[fb].wrote(e5)
            r_hcbl[fb].wrote(e5)

            tok_hsp = S.do("gpsimd", lambda eng, _f=f, _t0=t0, _fb=fb: [
                eng.dma_start(
                    out=hh_h[_f * 128:(_f + 1) * 128, _t0:_t0 + TB],
                    in_=hcbh[_fb][:]),
                eng.dma_start(
                    out=hl_h[_f * 128:(_f + 1) * 128, _t0:_t0 + TB],
                    in_=hcbl[_fb][:]),
            ], deps=[e5], dma=True, n_dma=2,
                chan="c5" if fb == 0 else "c6")
            r_hcbh[fb].read(tok_hsp)
            r_hcbl[fb].read(tok_hsp)

    S.flush(block)
    phB2.close()
    S.phase_start()

    if upto == "B2":
        S.wait_only("gpsimd", S.barrier_tokens())
        S.flush(block)
        ctx.close()
        return nc

    phB3 = ExitStack()
    w2h = sb("w2h", [128, (DFF // 128) * D], BF16, stack=phB3)
    w2l = sb("w2l", [128, (DFF // 128) * D], BF16, stack=phB3)
    wstg3 = [sb("wstg3_%d" % i, [128, D // 2], stack=phB3) for i in range(2)]
    wtmp3 = [sb("wtmp3_%d" % i, [128, D // 2], stack=phB3) for i in range(2)]
    hth = sb("hth", [128, (DFF // 128) * TB], BF16, stack=phB3)
    htl = sb("htl", [128, (DFF // 128) * TB], BF16, stack=phB3)
    o1t3h = sb("o1t3h", [128, NCH * TB], BF16, stack=phB3)
    o1t3l = sb("o1t3l", [128, NCH * TB], BF16, stack=phB3)
    z2 = sb("z2", [128, NCH * TB], stack=phB3)
    sqb3 = [sb("sqb3_%d" % i, [128, TB], stack=phB3) for i in range(2)]
    aoutb = z2

    r_ht = RW()
    r_o1t3 = RW()
    r_z2 = RW()
    r_sq3 = [RW(), RW()]
    r_ao = RW()
    r_wstg3 = [RW(), RW()]
    r_wtmp3 = [RW(), RW()]
    r_w2 = RW()

    tok_w2 = None
    HD3 = D // 2
    for i in range(2 * (DFF // 128)):
        ff, hv = i // 2, i % 2
        sg = i % 2
        tok_l = S.do("sync", lambda eng, _f=ff, _h=hv, _s=sg: eng.dma_start(
            out=wstg3[_s][:],
            in_=w2[_f * 128:(_f + 1) * 128, _h * HD3:(_h + 1) * HD3]),
            deps=r_wstg3[sg].wdep(), dma=True, chan="c1")
        r_wstg3[sg].wrote(tok_l)
        tok_s = split3s(
            wstg3[sg][:],
            w2h[:, ff * D + hv * HD3:ff * D + (hv + 1) * HD3],
            w2l[:, ff * D + hv * HD3:ff * D + (hv + 1) * HD3], wtmp3[sg][:],
            [tok_l] + r_wtmp3[sg].wdep() + (r_w2.wdep() if i == 0 else []))
        r_wstg3[sg].read(tok_s)
        r_wtmp3[sg].wrote(tok_s)
        r_w2.wrote(tok_s, reset=(i == 0))
        tok_w2 = tok_s

    for it in range(NTB):
        t0 = it * TB

        tok_h = S.do("sync", lambda eng, _t0=t0: [
            ld3(eng, hth, hh_h, _t0, TB),
            ld3(eng, htl, hl_h, _t0, TB)],
            deps=r_ht.wdep(), dma=True, n_dma=2, chan="c2")
        r_ht.wrote(tok_h)

        tok_o1 = S.do("sync", lambda eng, _t0=t0: [
            ld3(eng, o1t3h, o1h_h, _t0, TB),
            ld3(eng, o1t3l, o1l_h, _t0, TB)],
            deps=r_o1t3.wdep(), dma=True, n_dma=2, chan="c3")
        r_o1t3.wrote(tok_o1)

        z2_toks = []
        for o in range(NCH):
            pi = o % 2
            tok_mm = S.do("tensor", lambda eng, _o=o, _p=pi: [
                eng.matmul(PS[_p][:, 0:TB],
                           wop[:, ff * D + _o * 128: ff * D + (_o + 1) * 128],
                           xop[:, ff * TB:(ff + 1) * TB],
                           start=(pp == 0 and ff == 0),
                           stop=(pp == 2 and ff == DFF // 128 - 1))
                for pp, (wop, xop) in enumerate(
                    ((w2h, hth), (w2h, htl), (w2l, hth)))
                for ff in range(DFF // 128)],
                deps=[tok_w2, tok_h] + PB[pi].wdep(), n_inst=96)
            PB[pi].wrote(tok_mm)
            r_ht.read(tok_mm)
            r_w2.read(tok_mm)

            tz1 = S.do("scalar", lambda eng, _o=o, _p=pi: eng.activation(
                out=z2[:, _o * TB:(_o + 1) * TB], in_=PS[_p][:, 0:TB],
                func=AF.Identity, bias=b2_t[:, _o:_o + 1], scale=1.0),
                deps=[tok_mm] + (r_z2.wdep() if o == 0 else []))
            r_z2.wrote(tz1, reset=(o == 0))
            PB[pi].read(tz1)
            tz2a = S.do("vector", lambda eng, _o=o: eng.tensor_add(
                z2[:, _o * TB:(_o + 1) * TB], z2[:, _o * TB:(_o + 1) * TB],
                o1t3h[:, _o * TB:(_o + 1) * TB]), deps=[tz1, tok_o1])
            r_z2.wrote(tz2a, reset=False)
            r_o1t3.read(tz2a)
            tz2 = S.do("vector", lambda eng, _o=o: eng.tensor_add(
                z2[:, _o * TB:(_o + 1) * TB], z2[:, _o * TB:(_o + 1) * TB],
                o1t3l[:, _o * TB:(_o + 1) * TB]), deps=[tz2a])
            r_z2.wrote(tz2, reset=False)
            r_o1t3.read(tz2)
            z2_toks.append(tz2)

        tok_bmu, tok_brs = ln_stats(z2, r_z2, sqb3, r_sq3, TB,
                                    6, 7, 2, 3, z2_toks, EPS_LN)

        ao_toks = []
        for chh in range(NCH):
            t = ln_norm_chunk(z2, r_z2, chh * TB, z2, r_z2, chh * TB,
                              TB, 2, 3, tok_bmu, tok_brs,
                              g2_t[:, chh:chh + 1], b2l_t[:, chh:chh + 1],
                              first=False)
            ao_toks.append(t)

        tok_aosp = S.do("gpsimd", lambda eng, _t0=t0: st3(
            eng, aout_h, z2, _t0, TB),
            deps=ao_toks, dma=True, n_dma=1, chan="c5")
        r_z2.read(tok_aosp)

    S.flush(block)
    phB3.close()
    S.phase_start()

    if upto == "B3":
        S.wait_only("gpsimd", S.barrier_tokens())
        S.flush(block)
        ctx.close()
        return nc

    TC = 512
    NTC = TPC // TC

    def swiglu_proj_phase(wsrc, out_hbm, silu_mul):
        ph = ExitStack()
        wf_t = sb("wf_t_%d" % silu_mul, [128, NCH * HID], BF16, stack=ph)
        wi_t = [sb("wi_t%d_%d" % (i, silu_mul), [128, HID], I32, stack=ph)
                for i in range(2)]
        xnch = [sb("xnch%d_%d" % (i, silu_mul), [128, NCH * TC], BF16,
                   stack=ph) for i in range(2)]
        xncl = [sb("xncl%d_%d" % (i, silu_mul), [128, NCH * TC], BF16,
                   stack=ph) for i in range(2)]
        gc = [sb("gc%d_%d" % (i, silu_mul), [128, TC], stack=ph)
              for i in range(2)]
        gch = [sb("gch%d_%d" % (i, silu_mul), [128, TC], BF16, stack=ph)
               for i in range(2)]
        gcl = [sb("gcl%d_%d" % (i, silu_mul), [128, TC], BF16, stack=ph)
               for i in range(2)]
        gtmp = [sb("gtmp%d_%d" % (i, silu_mul), [128, TC], stack=ph)
                for i in range(2)]
        sgc = [sb("sgc%d_%d" % (i, silu_mul), [128, TC], stack=ph)
               for i in range(2)]
        gst = None
        if silu_mul:
            gst = [sb("gst0", [128, NHC * TC], stack=ph)]
        r_wf = RW()
        r_wi = [RW(), RW()]
        r_xnc = [RW(), RW()]
        r_gc = [RW(), RW()]
        r_gch = [RW(), RW()]
        r_gcl = [RW(), RW()]
        r_gtmp = [RW(), RW()]
        r_sgc = [RW(), RW()]
        r_gst = [RW(), RW()]

        cvt_toks = []
        for dd in range(NCH):
            sg = dd % 2
            tok_l = S.do("sync", lambda eng, _d=dd, _s=sg: eng.dma_start(
                out=wi_t[_s][:], in_=wsrc[_d * 128:(_d + 1) * 128, :]),
                deps=r_wi[sg].wdep(), dma=True, chan="c1")
            r_wi[sg].wrote(tok_l)
            tok_cv = S.do("vector", lambda eng, _d=dd, _s=sg: eng.tensor_copy(
                out=wf_t[:, _d * HID:(_d + 1) * HID], in_=wi_t[_s][:]),
                deps=[tok_l])
            r_wi[sg].read(tok_cv)
            cvt_toks.append(tok_cv)
            r_wf.wrote(tok_cv, reset=(dd == 0))

        for it in range(NTC):
            t0 = it * TC
            cb = it % 2

            tok_xn = S.do("sync", lambda eng, _t0=t0, _c=cb: [
                ld3(eng, xnch[_c], xnh_h, _t0, TC),
                ld3(eng, xncl[_c], xnl_h, _t0, TC)],
                deps=r_xnc[cb].wdep(), dma=True, n_dma=2, chan="c3")
            r_xnc[cb].wrote(tok_xn)

            tok_g = None
            if silu_mul:
                tok_g = S.do("sync", lambda eng, _t0=t0: ld3(
                    eng, gst[0], gate_h, _t0, TC), deps=r_gst[0].wdep(),
                    dma=True, n_dma=1, chan="c2")
                r_gst[0].wrote(tok_g)

            for hc in range(NHC):
                pi = hc % 4
                fb = hc % 2
                tok_mm = S.do("tensor", lambda eng, _h=hc, _p=pi, _c=cb: [
                    eng.matmul(PS[_p][:, 0:TC],
                               wf_t[:, dd * HID + _h * 128:
                                    dd * HID + (_h + 1) * 128],
                               xop[:, dd * TC:(dd + 1) * TC],
                               start=(pp == 0 and dd == 0),
                               stop=(pp == 1 and dd == NCH - 1))
                    for pp, xop in enumerate((xnch[_c], xncl[_c]))
                    for dd in range(NCH)],
                    deps=cvt_toks + [tok_xn] + PB[pi].wdep(), n_inst=16)
                PB[pi].wrote(tok_mm)
                r_xnc[cb].read(tok_mm)
                r_wf.read(tok_mm)

                if not silu_mul:
                    tok_ev = S.do("scalar", lambda eng, _p=pi, _fb=fb:
                                  eng.activation(
                        out=gc[_fb][:], in_=PS[_p][:, 0:TC], func=AF.Identity),
                        deps=[tok_mm] + r_gc[fb].wdep())
                    r_gc[fb].wrote(tok_ev)
                    PB[pi].read(tok_ev)
                    tok_spg = S.do("gpsimd", lambda eng, _h=hc, _t0=t0,
                                   _fb=fb: eng.dma_start(
                        out=out_hbm[_h * 128:(_h + 1) * 128, _t0:_t0 + TC],
                        in_=gc[_fb][:]), deps=[tok_ev], dma=True,
                        chan="c5" if fb == 0 else "c6")
                    r_gc[fb].read(tok_spg)
                else:
                    tok_si = S.do("scalar", lambda eng, _h=hc, _fb=fb:
                                  eng.activation(
                        out=sgc[_fb][:], in_=gst[0][:, _h * TC:(_h + 1) * TC],
                        func=AF.Silu, scale=SCALE_Q),
                        deps=[tok_g] + r_sgc[fb].wdep())
                    r_sgc[fb].wrote(tok_si)
                    r_gst[0].read(tok_si)
                    tok_ev = S.do("vector", lambda eng, _p=pi, _fb=fb:
                                  eng.tensor_mul(gc[_fb][:], sgc[_fb][:],
                                                 PS[_p][:, 0:TC]),
                        deps=[tok_si, tok_mm] + r_gc[fb].wdep())
                    r_gc[fb].wrote(tok_ev)
                    r_sgc[fb].read(tok_ev)
                    PB[pi].read(tok_ev)
                    tok_sp2 = split3s(gc[fb][:], gch[fb][:], gcl[fb][:],
                                      gtmp[fb][:],
                                      [tok_ev] + r_gch[fb].wdep() +
                                      r_gcl[fb].wdep() + r_gtmp[fb].wdep())
                    r_gc[fb].read(tok_sp2)
                    r_gtmp[fb].wrote(tok_sp2)
                    r_gch[fb].wrote(tok_sp2)
                    r_gcl[fb].wrote(tok_sp2)
                    tok_spg = S.do("gpsimd", lambda eng, _h=hc, _t0=t0,
                                   _fb=fb: [
                        eng.dma_start(
                            out=hsh_h[_h * 128:(_h + 1) * 128, _t0:_t0 + TC],
                            in_=gch[_fb][:]),
                        eng.dma_start(
                            out=hsl_h[_h * 128:(_h + 1) * 128, _t0:_t0 + TC],
                            in_=gcl[_fb][:]),
                    ], deps=[tok_sp2], dma=True, n_dma=2,
                        chan="c5" if fb == 0 else "c6")
                    r_gch[fb].read(tok_spg)
                    r_gcl[fb].read(tok_spg)

        S.flush(block)
        ph.close()
        S.phase_start()

    swiglu_proj_phase(wgT, gate_h, False)
    if upto == "C1":
        S.wait_only("gpsimd", S.barrier_tokens())
        S.flush(block)
        ctx.close()
        return nc
    swiglu_proj_phase(wuT, None, True)
    if upto == "C1b":
        S.wait_only("gpsimd", S.barrier_tokens())
        S.flush(block)
        ctx.close()
        return nc

    phC2 = ExitStack()
    wdf_t = sb("wdf_t", [128, NHC * D], BF16, stack=phC2)
    wdi_t = [sb("wdi_t%d" % i, [128, D], I32, stack=phC2) for i in range(2)]
    hsth = [sb("hsth%d" % i, [128, NHC * TC], BF16, stack=phC2)
            for i in range(2)]
    hstl = [sb("hstl%d" % i, [128, NHC * TC], BF16, stack=phC2)
            for i in range(2)]
    xt2 = sb("xt2", [128, NCH * TC], stack=phC2)
    aot = [sb("aot%d" % i, [128, TC], stack=phC2) for i in range(2)]
    yb = [sb("yb%d" % i, [128, TC], stack=phC2) for i in range(2)]

    r_wdf = RW()
    r_wdi = [RW(), RW()]
    r_hst = [RW(), RW()]
    r_xt2 = RW()
    r_aot = [RW(), RW()]
    r_yb = [RW(), RW()]

    cvt_toks = []
    for hcg in range(NHC):
        sg = hcg % 2
        tok_l = S.do("sync", lambda eng, _h=hcg, _s=sg: eng.dma_start(
            out=wdi_t[_s][:], in_=wdT[_h * 128:(_h + 1) * 128, :]),
            deps=r_wdi[sg].wdep(), dma=True, chan="c1")
        r_wdi[sg].wrote(tok_l)
        tok_cv = S.do("vector", lambda eng, _h=hcg, _s=sg: eng.tensor_copy(
            out=wdf_t[:, _h * D:(_h + 1) * D], in_=wdi_t[_s][:]),
            deps=[tok_l])
        r_wdi[sg].read(tok_cv)
        cvt_toks.append(tok_cv)
        r_wdf.wrote(tok_cv, reset=(hcg == 0))

    for it in range(NTC):
        t0 = it * TC
        cb = it % 2

        tok_hs = S.do("sync", lambda eng, _t0=t0, _c=cb: [
            ld3(eng, hsth[_c], hsh_h, _t0, TC),
            ld3(eng, hstl[_c], hsl_h, _t0, TC)],
            deps=r_hst[cb].wdep(), dma=True, n_dma=2, chan="c2")
        r_hst[cb].wrote(tok_hs)

        tok_x = S.do("sync", lambda eng, _t0=t0: ld3(
            eng, xt2, xT, _t0, TC), deps=r_xt2.wdep(), dma=True, n_dma=1,
            chan="c3")
        r_xt2.wrote(tok_x)



        for o in range(NCH):
            pi = o % 2
            ab = o % 2
            tok_ao = S.do("sync", lambda eng, _t0=t0, _o=o, _a=ab:
                          eng.dma_start(
                out=aot[_a][:],
                in_=aout_h[_o * 128:(_o + 1) * 128, _t0:_t0 + TC]),
                deps=r_aot[ab].wdep(), dma=True, n_dma=1, chan="c4")
            r_aot[ab].wrote(tok_ao)
            tok_mm = S.do("tensor", lambda eng, _o=o, _p=pi, _c=cb: [
                eng.matmul(PS[_p][:, 0:TC],
                           wdf_t[:, hc * D + _o * 128: hc * D + (_o + 1) * 128],
                           xop[:, hc * TC:(hc + 1) * TC],
                           start=(pp == 0 and hc == 0),
                           stop=(pp == 1 and hc == NHC - 1))
                for pp, xop in enumerate((hsth[_c], hstl[_c]))
                for hc in range(NHC)],
                deps=cvt_toks + [tok_hs] + PB[pi].wdep(), n_inst=48)
            PB[pi].wrote(tok_mm)
            r_hst[cb].read(tok_mm)
            r_wdf.read(tok_mm)

            ybi = o % 2
            y1 = S.do("vector", lambda eng, _p=pi, _y=ybi: eng.tensor_scalar(
                out=yb[_y][:], in0=PS[_p][:, 0:TC], scalar1=SCALE_Q2,
                scalar2=None, op0=ALU.mult),
                deps=[tok_mm] + r_yb[ybi].wdep())
            r_yb[ybi].wrote(y1)
            PB[pi].read(y1)
            y2 = S.do("vector", lambda eng, _o=o, _y=ybi: eng.tensor_add(
                yb[_y][:], yb[_y][:], xt2[:, _o * TC:(_o + 1) * TC]),
                deps=[y1, tok_x])
            r_yb[ybi].wrote(y2, reset=False)
            r_xt2.read(y2)
            y3 = S.do("vector", lambda eng, _o=o, _y=ybi, _a=ab:
                      eng.tensor_add(
                yb[_y][:], yb[_y][:], aot[_a][:]),
                deps=[y2, tok_ao])
            r_yb[ybi].wrote(y3, reset=False)
            r_aot[ab].read(y3)

            tok_st = S.do("gpsimd", lambda eng, _o=o, _t0=t0, _y=ybi:
                          eng.dma_start(
                out=yT[_o * 128:(_o + 1) * 128, _t0:_t0 + TC],
                in_=yb[_y][:]), deps=[y3], dma=True, chan="c5")
            r_yb[ybi].read(tok_st)

    S.wait_only("gpsimd", S.barrier_tokens())
    S.flush(block)
    phC2.close()

    ctx.close()
    return nc


def make_inputs(inputs):
    x = np.asarray(inputs["L0_attn_x"])
    Wq = np.asarray(inputs["L0_attn_Wq"]).reshape(D, D)
    Wk = np.asarray(inputs["L0_attn_Wk"]).reshape(D, D)
    Wv = np.asarray(inputs["L0_attn_Wv"]).reshape(D, D)
    Wo = np.asarray(inputs["L0_attn_Wo"]).reshape(D, D)
    proj = np.asarray(inputs["L0_attn_proj"])
    W1 = np.asarray(inputs["L0_attn_W1"])
    b1 = np.asarray(inputs["L0_attn_b1"])
    W2 = np.asarray(inputs["L0_attn_W2"])
    b2 = np.asarray(inputs["L0_attn_b2"])
    g1 = np.asarray(inputs["L0_attn_ln1_g"])
    b1l = np.asarray(inputs["L0_attn_ln1_b"])
    g2 = np.asarray(inputs["L0_attn_ln2_g"])
    b2l = np.asarray(inputs["L0_attn_ln2_b"])
    wg = np.asarray(inputs["L0_ffn_w_gate"])
    wu = np.asarray(inputs["L0_ffn_w_up"])
    wd = np.asarray(inputs["L0_ffn_w_down"])

    projbd = np.zeros((128, 128), np.float32)
    projbd[0:64, 0:64] = proj.T
    projbd[64:128, 64:128] = proj.T

    eye = np.eye(128, dtype=np.float32)
    ones1r = np.ones((1, 128), np.float32)
    avgc = np.full((128, 1), np.float32(1.0 / 1024.0), np.float32)
    blockneg = np.zeros((128, 128), np.float32)
    for chh in range(NCH):
        blockneg[0:64, chh * 16 + 2 * chh] = np.float32(-CDIAG)
        blockneg[64:128, chh * 16 + 2 * chh + 1] = np.float32(-CDIAG)
    maskeo = np.zeros((1, 256), np.float32)
    maskeo[0, 0:64] = 1.0
    maskeo[0, 192:256] = 1.0

    shared = dict(
        wq=np.ascontiguousarray(Wq), wk=np.ascontiguousarray(Wk),
        wv=np.ascontiguousarray(Wv), wo=np.ascontiguousarray(Wo),
        projbd=projbd, w1=np.ascontiguousarray(W1),
        w2=np.ascontiguousarray(W2),
        b1c=np.ascontiguousarray(b1.reshape(DFF // 128, 128).T),
        b2c=np.ascontiguousarray(b2.reshape(NCH, 128).T),
        g1c=np.ascontiguousarray(g1.reshape(NCH, 128).T),
        b1lc=np.ascontiguousarray(b1l.reshape(NCH, 128).T),
        g2c=np.ascontiguousarray(g2.reshape(NCH, 128).T),
        b2lc=np.ascontiguousarray(b2l.reshape(NCH, 128).T),
        wgT=np.ascontiguousarray(wg.T), wuT=np.ascontiguousarray(wu.T),
        wdT=np.ascontiguousarray(wd.T),
        eye=eye, ones1r=ones1r, avgc=avgc, blockneg=blockneg, maskeo=maskeo,
    )
    in_maps = []
    for c in range(NCORES):
        m = dict(shared)
        m["xT"] = np.ascontiguousarray(x[2 * c:2 * c + 2].reshape(TPC, D).T)
        in_maps.append(m)
    return in_maps


def kernel(**inputs):
    in_maps = make_inputs(inputs)
    nc = build_program()
    res = run_bass_kernel_spmd(nc, in_maps, core_ids=list(range(NCORES)))
    outs = []
    for c in range(NCORES):
        yc = np.asarray(res.results[c]["yT"])
        outs.append(np.ascontiguousarray(yc.T).reshape(2, N, D))
    return np.concatenate(outs, axis=0).astype(np.float32)


# version 23
# speedup vs baseline: 1.0418x; 1.0186x over previous
import numpy as np
from contextlib import ExitStack

import concourse.bass as bass
import concourse.mybir as mybir
from concourse.bass_utils import run_bass_kernel_spmd

F32 = mybir.dt.float32
BF16 = mybir.dt.bfloat16
I32 = mybir.dt.int32
AF = mybir.ActivationFunctionType
ALU = mybir.AluOpType
AX = mybir.AxisListType

NCORES = 8
B, N, D = 16, 2048, 1024
H, DH, M = 16, 64, 64
DFF, HID = 4096, 3072
TPC = B * N // NCORES
NCH = D // 128
NHC = HID // 128

EPS_K = 1e-6
EPS_LN = 1e-6
EPS_OUT = 1e-5

_DN32 = np.float32(1.0) / np.sqrt(np.sqrt(np.float32(64.0)))
DN = float(_DN32)
CDIAG = float((_DN32 * _DN32) / np.float32(2.0))
_S32 = np.float32(1.0 / 127.0)
SCALE_Q = float(_S32)
SCALE_Q2 = float(_S32 * _S32)

ENGS = ("sync", "gpsimd", "tensor", "vector", "scalar")
NCHAN = 10
CHANS = tuple("c%d" % i for i in range(NCHAN))


class Sched:

    def __init__(self, nc, ctx):
        self.nc = nc
        self.sem = {e: ctx.enter_context(nc.semaphore("sem_" + e)) for e in ENGS}
        for ch in CHANS:
            self.sem[ch] = ctx.enter_context(nc.semaphore("sem_" + ch))
        self.cnt = {k: 0 for k in self.sem}
        self.steps = {e: [] for e in ENGS}
        self.pending = {e: [] for e in ENGS}

    def do(self, eng, emit, deps=(), dma=False, n_dma=1, chan=None,
           n_inst=1, inc_last=False):
        waits = {}
        for d in list(deps) + self.pending[eng]:
            if d is None:
                continue
            de, dv = d
            assert not (de == eng and not dma and dv > self.cnt[eng])
            if waits.get(de, 0) < dv:
                waits[de] = dv
        self.pending[eng] = []
        if dma:
            assert chan in CHANS, chan
            self.cnt[chan] += 16 * n_dma
            tok = (chan, self.cnt[chan])
        else:
            self.cnt[eng] += 1 if inc_last else n_inst
            tok = (eng, self.cnt[eng])
        self.steps[eng].append((waits, emit, dma, n_dma, chan, n_inst,
                                inc_last))
        return tok

    def wait_only(self, eng, deps):
        waits = {}
        for d in deps:
            if d is None:
                continue
            de, dv = d
            if waits.get(de, 0) < dv:
                waits[de] = dv
        self.steps[eng].append((waits, None, False, 0, None, 0, False))

    def barrier_tokens(self):
        return [(k, self.cnt[k]) for k in self.sem]

    def phase_start(self):
        toks = self.barrier_tokens()
        for e in ENGS:
            self.pending[e] = list(toks)

    def flush(self, block):
        for e in ENGS:
            steps = self.steps[e]
            if not steps:
                continue
            sems = self.sem

            def run(eng, _steps=steps, _e=e):
                for waits, emit, dma, n_dma, chan, n_inst, inc_last in _steps:
                    for we, wv in waits.items():
                        eng.wait_ge(sems[we], wv)
                    if emit is None:
                        continue
                    insts = emit(eng)
                    if not isinstance(insts, (list, tuple)):
                        insts = [insts]
                    if dma:
                        assert len(insts) == n_dma, (len(insts), n_dma)
                        for i in insts:
                            i.then_inc(sems[chan], 16)
                    elif inc_last:
                        insts[-1].then_inc(sems[_e], 1)
                    else:
                        assert len(insts) == n_inst, (len(insts), n_inst)
                        for i in insts:
                            i.then_inc(sems[_e], 1)

            getattr(block, e)(run)
            self.steps[e] = []


class RW:

    def __init__(self):
        self.w = []
        self.r = []

    def wdep(self):
        return self.r + self.w

    def rdep(self):
        return list(self.w)

    def wrote(self, tok, reset=True):
        if reset:
            self.w = [tok]
            self.r = []
        else:
            self.w.append(tok)

    def read(self, tok):
        self.r.append(tok)


def build_program(debug=(), upto="C2", parts="lkdvzqwt", nt=None, use_cc=True):
    nc = bass.Bass("TRN2", target_bir_lowering=False, debug=False)
    ctx = ExitStack()

    def dram(name, shape, dtype, kind="Internal", **kw):
        if name in debug and kind == "Internal":
            kind = "ExternalOutput"
        return nc.dram_tensor(name, shape, dtype, kind=kind, **kw).ap()

    xT = dram("xT", [D, TPC], F32, kind="ExternalInput")
    wq = dram("wq", [D, D], F32, kind="ExternalInput")
    wk = dram("wk", [D, D], F32, kind="ExternalInput")
    wv = dram("wv", [D, D], F32, kind="ExternalInput")
    wo = dram("wo", [D, D], F32, kind="ExternalInput")
    projbd = dram("projbd", [128, 128], F32, kind="ExternalInput")
    w1 = dram("w1", [D, DFF], F32, kind="ExternalInput")
    w2 = dram("w2", [DFF, D], F32, kind="ExternalInput")
    b1c = dram("b1c", [128, DFF // 128], F32, kind="ExternalInput")
    b2c = dram("b2c", [128, NCH], F32, kind="ExternalInput")
    g1c = dram("g1c", [128, NCH], F32, kind="ExternalInput")
    b1lc = dram("b1lc", [128, NCH], F32, kind="ExternalInput")
    g2c = dram("g2c", [128, NCH], F32, kind="ExternalInput")
    b2lc = dram("b2lc", [128, NCH], F32, kind="ExternalInput")
    wgT = dram("wgT", [D, HID], I32, kind="ExternalInput")
    wuT = dram("wuT", [D, HID], I32, kind="ExternalInput")
    wdT = dram("wdT", [HID, D], I32, kind="ExternalInput")
    eye = dram("eye", [128, 128], F32, kind="ExternalInput")
    ones1r = dram("ones1r", [1, 128], F32, kind="ExternalInput")
    avgc = dram("avgc", [128, 1], F32, kind="ExternalInput")
    blockneg = dram("blockneg", [128, 128], F32, kind="ExternalInput")
    maskeo = dram("maskeo", [1, 256], F32, kind="ExternalInput")

    yT = dram("yT", [D, TPC], F32, kind="ExternalOutput")

    xnh_h = dram("xnh_h", [D, TPC], BF16)
    xnl_h = dram("xnl_h", [D, TPC], BF16)
    o1h_h = dram("o1h_h", [D, TPC], BF16)
    o1l_h = dram("o1l_h", [D, TPC], BF16)
    aout_h = dram("aout_h", [D, TPC], F32)
    hh_h = dram("hh_h", [DFF, TPC], BF16)
    hl_h = dram("hl_h", [DFF, TPC], BF16)
    gate_h = dram("gate_h", [HID, TPC], F32)
    hsh_h = dram("hsh_h", [HID, TPC], BF16)
    hsl_h = dram("hsl_h", [HID, TPC], BF16)
    ccin = dram("ccin", [1, 16], F32)
    ccout = dram("ccout", [1, 16], F32, addr_space="Shared")
    dbg_kv = dram("dbg_kv", [128, 32 * 65], F32)
    dbg_mx = dram("dbg_mx", [1, 16], F32)
    dbg_qf = dram("dbg_qf", [128, 4 * H * 65], F32)
    dbg_qft = dram("dbg_qft", [128, 512], F32)
    dbg_den = dram("dbg_den", [1, 2048], F32)
    dbg_lmx = dram("dbg_lmx", [1, 144], F32)
    dbg_aend = dram("dbg_aend", [128, 8448], F32)
    dbg_kd = dram("dbg_kd", [128, 148], F32)
    dbg_gmax = dram("dbg_gmax", [128, 16], F32)
    dbg_attn = dram("dbg_attn", [128, NCH * 512], F32)

    S = Sched(nc, ctx)
    block = ctx.enter_context(nc.Block())

    PS = [ctx.enter_context(nc.psum_tensor("ps%d" % i, [128, 512], F32))
          for i in range(8)]
    PB = [RW() for _ in range(8)]

    def sb(name, shape, dtype=F32, stack=ctx):
        return stack.enter_context(nc.sbuf_tensor(name, shape, dtype))

    eye_t = sb("eye_t", [128, 128])
    ones1_t = sb("ones1_t", [1, 128])
    avgc_t = sb("avgc_t", [128, 1])
    blockneg_t = sb("blockneg_t", [128, 128])
    maskeo_t = sb("maskeo_t", [1, 256])
    projbd_t = sb("projbd_t", [128, 128])
    kv_acc = sb("kv_acc", [128, 32 * 65])
    gmax = sb("gmax", [128, 16])
    maxscr = sb("maxscr", [128, 4])
    mxrow = sb("mxrow", [1, 144])
    mx_sb = sb("mx_sb", [1, 16])
    scale_col = sb("scale_col", [128, 1])
    stat_sb = sb("stat_sb", [1, 4 * 512])
    g1_t = sb("g1_t", [128, NCH])
    b1l_t = sb("b1l_t", [128, NCH])
    g2_t = sb("g2_t", [128, NCH])
    b2l_t = sb("b2l_t", [128, NCH])
    b2_t = sb("b2_t", [128, NCH])
    b1_t = sb("b1_t", [128, DFF // 128])
    nb1_t = sb("nb1_t", [128, DFF // 128])

    r_stat = RW()
    r_kv = RW()
    r_gmax = RW()
    r_maxscr = RW()
    r_scol = RW()
    r_mx = RW()
    r_mxrow = RW()

    tok_consts = S.do("sync", lambda eng: [
        eng.dma_start(out=eye_t[:], in_=eye[:]),
        eng.dma_start(out=ones1_t[:], in_=ones1r[:]),
        eng.dma_start(out=avgc_t[:], in_=avgc[:]),
        eng.dma_start(out=blockneg_t[:], in_=blockneg[:]),
        eng.dma_start(out=maskeo_t[:], in_=maskeo[:]),
        eng.dma_start(out=projbd_t[:], in_=projbd[:]),
        eng.dma_start(out=g1_t[:], in_=g1c[:]),
        eng.dma_start(out=b1l_t[:], in_=b1lc[:]),
        eng.dma_start(out=g2_t[:], in_=g2c[:]),
        eng.dma_start(out=b2l_t[:], in_=b2lc[:]),
        eng.dma_start(out=b2_t[:], in_=b2c[:]),
        eng.dma_start(out=b1_t[:], in_=b1c[:]),
    ], dma=True, n_dma=12, chan="c0")

    tok_init = S.do("vector", lambda eng: [
        eng.tensor_scalar(out=projbd_t[:], in0=projbd_t[:], scalar1=DN,
                          scalar2=None, op0=ALU.mult),
        eng.memset(kv_acc[:], 0.0),
        eng.memset(gmax[:], -3.0e38),
        eng.tensor_scalar(out=nb1_t[:], in0=b1_t[:], scalar1=-1.0,
                          scalar2=None, op0=ALU.mult),
    ], deps=[tok_consts], n_inst=4)
    r_kv.wrote(tok_init)
    r_gmax.wrote(tok_init)

    if "dbg_init" in debug:
        t = S.do("gpsimd", lambda eng: eng.dma_start(
            out=dbg_gmax[:], in_=gmax[:]), deps=[tok_init], dma=True,
            chan="c9")
        r_gmax.read(t)

    def ln_stats(tile_buf, tile_rw, sq_bufs, sq_rws, T, pmean_i, pmsq_i,
                 pmu_i, prs_i, chunk_toks, eps):
        mm_last = None
        for chh in range(NCH):
            sqi = chh % 2
            tok_sq = S.do("scalar", lambda eng, _c=chh, _s=sqi: eng.activation(
                out=sq_bufs[_s][:, 0:T],
                in_=tile_buf[:, _c * T:(_c + 1) * T], func=AF.Square),
                deps=[chunk_toks[chh]] + sq_rws[sqi].wdep())
            sq_rws[sqi].wrote(tok_sq)
            tile_rw.read(tok_sq)

            tok_mm = S.do("tensor", lambda eng, _c=chh, _s=sqi: [
                eng.matmul(PS[pmean_i][0:1, 0:T], avgc_t[:],
                           tile_buf[:, _c * T:(_c + 1) * T],
                           start=(_c == 0), stop=(_c == NCH - 1)),
                eng.matmul(PS[pmsq_i][0:1, 0:T], avgc_t[:],
                           sq_bufs[_s][:, 0:T], start=(_c == 0),
                           stop=(_c == NCH - 1)),
            ], deps=[chunk_toks[chh], tok_sq, tok_consts] +
                (PB[pmean_i].wdep() + PB[pmsq_i].wdep() if chh == 0 else []),
                n_inst=2)
            sq_rws[sqi].read(tok_mm)
            tile_rw.read(tok_mm)
            mm_last = tok_mm
        PB[pmean_i].wrote(mm_last)
        PB[pmsq_i].wrote(mm_last)

        t1 = S.do("vector", lambda eng: [
            eng.tensor_copy(out=stat_sb[0:1, 0:T], in_=PS[pmean_i][0:1, 0:T]),
            eng.tensor_copy(out=stat_sb[0:1, 512:512 + T],
                            in_=PS[pmsq_i][0:1, 0:T]),
        ], deps=[mm_last] + r_stat.wdep(), n_inst=2)
        r_stat.wrote(t1)
        PB[pmean_i].read(t1)
        PB[pmsq_i].read(t1)
        t2 = S.do("vector", lambda eng: eng.tensor_mul(
            stat_sb[0:1, 1024:1024 + T], stat_sb[0:1, 0:T],
            stat_sb[0:1, 0:T]), deps=[t1])
        r_stat.wrote(t2, reset=False)
        t3 = S.do("vector", lambda eng: eng.tensor_sub(
            stat_sb[0:1, 512:512 + T], stat_sb[0:1, 512:512 + T],
            stat_sb[0:1, 1024:1024 + T]), deps=[t2])
        r_stat.wrote(t3, reset=False)
        t4 = S.do("vector", lambda eng, _e=eps: eng.tensor_scalar(
            out=stat_sb[0:1, 512:512 + T], in0=stat_sb[0:1, 512:512 + T],
            scalar1=_e, scalar2=None, op0=ALU.add), deps=[t3])
        r_stat.wrote(t4, reset=False)
        t5 = S.do("scalar", lambda eng: eng.activation(
            out=stat_sb[0:1, 1024:1024 + T], in_=stat_sb[0:1, 512:512 + T],
            func=AF.Sqrt), deps=[t4])
        r_stat.wrote(t5, reset=False)
        t6 = S.do("vector", lambda eng: eng.reciprocal(
            out=stat_sb[0:1, 1536:1536 + T], in_=stat_sb[0:1, 1024:1024 + T]),
            deps=[t5])
        r_stat.wrote(t6, reset=False)

        tok_bmu = S.do("tensor", lambda eng: eng.matmul(
            PS[pmu_i][:, 0:T], ones1_t[:], stat_sb[0:1, 0:T],
            start=True, stop=True), deps=[t6] + PB[pmu_i].wdep())
        PB[pmu_i].wrote(tok_bmu)
        r_stat.read(tok_bmu)
        tok_brs = S.do("tensor", lambda eng: eng.matmul(
            PS[prs_i][:, 0:T], ones1_t[:], stat_sb[0:1, 1536:1536 + T],
            start=True, stop=True), deps=[t6] + PB[prs_i].wdep())
        PB[prs_i].wrote(tok_brs)
        r_stat.read(tok_brs)
        return tok_bmu, tok_brs

    def ln_norm_chunk(out_buf, out_rw, out_off, z_buf, z_rw, z_off, T,
                      pmu_i, prs_i, tok_bmu, tok_brs, gcol, bcol, first):
        n1 = S.do("vector", lambda eng: eng.tensor_sub(
            out_buf[:, out_off:out_off + T], z_buf[:, z_off:z_off + T],
            PS[pmu_i][:, 0:T]),
            deps=[tok_bmu, tok_brs] + z_rw.rdep() +
            (out_rw.wdep() if first else []))
        out_rw.wrote(n1, reset=first)
        z_rw.read(n1)
        PB[pmu_i].read(n1)
        n2 = S.do("vector", lambda eng: eng.tensor_mul(
            out_buf[:, out_off:out_off + T], out_buf[:, out_off:out_off + T],
            PS[prs_i][:, 0:T]), deps=[n1])
        out_rw.wrote(n2, reset=False)
        PB[prs_i].read(n2)
        last = n2
        if gcol is not None:
            n3 = S.do("scalar", lambda eng: eng.activation(
                out=out_buf[:, out_off:out_off + T],
                in_=out_buf[:, out_off:out_off + T],
                func=AF.Identity, bias=bcol, scale=gcol), deps=[n2])
            out_rw.wrote(n3, reset=False)
            last = n3
        return last

    def ld3(eng, sbuf_t, hbm_ap, t0, T):
        dst = sbuf_t[:].rearrange("p (c t) -> p c t", t=T)
        srcr = hbm_ap.rearrange("(c p) n -> p c n", p=128)[:, :, t0:t0 + T]
        return eng.dma_start(out=dst, in_=srcr)

    def st3(eng, hbm_ap, sbuf_t, t0, T):
        dstr = hbm_ap.rearrange("(c p) n -> p c n", p=128)[:, :, t0:t0 + T]
        srcs = sbuf_t[:].rearrange("p (c t) -> p c t", t=T)
        return eng.dma_start(out=dstr, in_=srcs)

    def split3s(srcap, hap, lap, tmpap, deps):
        t1 = S.do("scalar", lambda eng: eng.activation(
            out=hap, in_=srcap, func=AF.Copy), deps=deps)
        t2 = S.do("vector", lambda eng: eng.tensor_sub(tmpap, srcap, hap),
                  deps=[t1])
        t3 = S.do("vector", lambda eng: eng.tensor_copy(out=lap, in_=tmpap),
                  deps=[t2])
        return t3

    TA = 512
    NT = nt if nt is not None else TPC // TA

    phA = ExitStack()
    wkh = sb("wkh", [128, NCH * D], BF16, stack=phA)
    wkl = sb("wkl", [128, NCH * D], BF16, stack=phA)
    wvh = sb("wvh", [128, NCH * D], BF16, stack=phA)
    wvl = sb("wvl", [128, NCH * D], BF16, stack=phA)
    wstg = [sb("wstg%d" % i, [128, D // 2], stack=phA) for i in range(2)]
    wtmp = [sb("wtmp%d" % i, [128, D // 2], stack=phA) for i in range(2)]
    xb = [sb("xb0", [128, NCH * TA], stack=phA)]
    xnh = [sb("xnh%d" % i, [128, NCH * TA], BF16, stack=phA) for i in range(2)]
    xnl = [sb("xnl%d" % i, [128, NCH * TA], BF16, stack=phA) for i in range(2)]
    xtc = [sb("xtc%d" % i, [128, TA], stack=phA) for i in range(2)]
    kfm = sb("kfm", [128, NCH * TA], stack=phA)
    sqb = [sb("sqb%d" % i, [128, TA], stack=phA) for i in range(2)]
    kdneg = sb("kdneg", [16, TA], stack=phA)
    kdT = sb("kdT", [128, 4 * 16], stack=phA)
    ek_ext = sb("ek_ext", [128, 4 * H * 65], stack=phA)
    v_ext = sb("v_ext", [128, 4 * H * 65], stack=phA)
    kdtm = [sb("kdtm%d" % i, [128, 128], stack=phA) for i in range(2)]

    r_xb = [RW()]
    r_xnh = [RW(), RW()]
    r_xnl = [RW(), RW()]
    r_xtc = [RW(), RW()]
    r_wstg = [RW(), RW()]
    r_wtmp = [RW(), RW()]
    r_wk = RW()
    r_wv = RW()
    r_kfm = RW()
    r_sq = [RW(), RW()]
    r_kdneg = RW()
    r_kdT = RW()
    r_ek = RW()
    r_v = RW()
    r_kdtm = [RW(), RW()]

    def ld_split_w(srct, dsth, dstl, chan, rw):
        last = None
        half_toks = [None, None]
        HDW = D // 2
        for i in range(2 * NCH):
            dd, hv = i % NCH, i // NCH
            sg = i % 2
            tok_l = S.do("sync", lambda eng, _d=dd, _h=hv, _s=sg:
                         eng.dma_start(
                out=wstg[_s][:],
                in_=srct[_d * 128:(_d + 1) * 128, _h * HDW:(_h + 1) * HDW]),
                deps=r_wstg[sg].wdep(), dma=True, chan=chan)
            r_wstg[sg].wrote(tok_l)
            tok_s = split3s(
                wstg[sg][:],
                dsth[:, dd * D + hv * HDW:dd * D + (hv + 1) * HDW],
                dstl[:, dd * D + hv * HDW:dd * D + (hv + 1) * HDW],
                wtmp[sg][:],
                [tok_l] + r_wtmp[sg].wdep() + (rw.wdep() if i == 0 else []))
            r_wstg[sg].read(tok_s)
            r_wtmp[sg].wrote(tok_s)
            rw.wrote(tok_s, reset=(i == 0))
            half_toks[hv] = tok_s
            last = tok_s
        return half_toks

    tok_wk = ld_split_w(wk, wkh, wkl, "c1", r_wk)
    tok_wv = ld_split_w(wv, wvh, wvl, "c2", r_wv)

    tok_ones = S.do("vector", lambda eng: [
        eng.memset(ek_ext.rearrange("p (a c) -> p a c", c=65)[:, :, 64:65], 1.0),
        eng.memset(v_ext.rearrange("p (a c) -> p a c", c=65)[:, :, 64:65], 1.0),
    ], n_inst=2)
    r_ek.wrote(tok_ones)
    r_v.wrote(tok_ones)

    for it in range(NT):
        bf = it % 2
        t0 = it * TA
        slot0 = (it // 4) * 16

        tok_x = S.do("sync", lambda eng, _t0=t0: ld3(
            eng, xb[0], xT, _t0, TA), deps=r_xb[0].wdep(), dma=True, n_dma=1,
            chan="c3" if bf == 0 else "c4")
        r_xb[0].wrote(tok_x)
        xtoks = [tok_x] * NCH

        tok_bmu, tok_brs = ln_stats(xb[0], r_xb[0], sqb, r_sq, TA,
                                    6, 7, 4, 5, xtoks, EPS_OUT)

        xn_toks = []
        for chh in range(NCH):
            tb = chh % 2
            t = ln_norm_chunk(xtc[tb], r_xtc[tb], 0, xb[0], r_xb[0],
                              chh * TA, TA, 4, 5, tok_bmu, tok_brs,
                              None, None, first=True)
            tsp = split3s(xtc[tb][:, 0:TA],
                          xnh[bf][:, chh * TA:(chh + 1) * TA],
                          xnl[bf][:, chh * TA:(chh + 1) * TA],
                          sqb[tb][:, 0:TA],
                          [t] + r_sq[tb].wdep() +
                          (r_xnh[bf].wdep() + r_xnl[bf].wdep()
                           if chh == 0 else []))
            r_xtc[tb].read(tsp)
            r_sq[tb].wrote(tsp)
            r_xnh[bf].wrote(tsp, reset=(chh == 0))
            r_xnl[bf].wrote(tsp, reset=(chh == 0))
            xn_toks.append(tsp)

        tok_sp = S.do("gpsimd", lambda eng, _t0=t0, _b=bf: [
            st3(eng, xnh_h, xnh[_b], _t0, TA),
            st3(eng, xnl_h, xnl[_b], _t0, TA)],
            deps=xn_toks, dma=True, n_dma=2,
            chan="c5" if bf == 0 else "c6")
        r_xnh[bf].read(tok_sp)
        r_xnl[bf].read(tok_sp)

        kev_toks = []
        kd_last = None
        for kch in (range(NCH) if "k" in parts else []):
            pi = kch % 2
            tok_k = S.do("tensor", lambda eng, _k=kch, _b=bf: [
                eng.matmul(PS[_k % 2][:, 0:TA],
                           wop[:, dd * D + _k * 128: dd * D + (_k + 1) * 128],
                           xop[:, dd * TA:(dd + 1) * TA],
                           start=(pp == 0 and dd == 0),
                           stop=(pp == 2 and dd == NCH - 1))
                for pp, (wop, xop) in enumerate(
                    ((wkh, xnh[_b]), (wkh, xnl[_b]), (wkl, xnh[_b])))
                for dd in range(NCH)],
                deps=[tok_wk[kch // 4]] + xn_toks + PB[pi].wdep(),
                n_inst=24)
            PB[pi].wrote(tok_k)
            r_xnh[bf].read(tok_k)
            r_xnl[bf].read(tok_k)
            r_wk.read(tok_k)

            tok_kev = S.do("vector", lambda eng, _k=kch: eng.tensor_copy(
                out=kfm[:, _k * TA:(_k + 1) * TA], in_=PS[_k % 2][:, 0:TA]),
                deps=[tok_k] + (r_kfm.wdep() if kch == 0 else []))
            PB[pi].read(tok_kev)
            kev_toks.append(tok_kev)

            if "q" not in parts:
                kd_last = tok_kev
                continue
            tok_ksq = S.do("scalar", lambda eng, _k=kch: eng.activation(
                out=sqb[_k % 2][:, 0:TA], in_=PS[_k % 2][:, 0:TA],
                func=AF.Square), deps=[tok_k, tok_kev] + r_sq[kch % 2].wdep())
            r_sq[kch % 2].wrote(tok_ksq)
            PB[pi].read(tok_ksq)

            if "w" not in parts:
                kd_last = tok_ksq
                continue
            tok_kdmm = S.do("tensor", lambda eng, _k=kch: eng.matmul(
                PS[6][0:16, 0:TA], blockneg_t[:, _k * 16:(_k + 1) * 16],
                sqb[_k % 2][:, 0:TA], start=(_k == 0), stop=(_k == NCH - 1)),
                deps=[tok_ksq] + (PB[6].wdep() if kch == 0 else []))
            r_sq[kch % 2].read(tok_kdmm)
            kd_last = tok_kdmm
        if "k" not in parts:
            continue
        r_kfm.wrote(kev_toks[0])
        for t in kev_toks[1:]:
            r_kfm.wrote(t, reset=False)
        if "q" not in parts or "w" not in parts:
            continue
        PB[6].wrote(kd_last)

        tok_kdev = S.do("vector", lambda eng: eng.tensor_copy(
            out=kdneg[:], in_=PS[6][0:16, 0:TA]),
            deps=[kd_last] + r_kdneg.wdep())
        r_kdneg.wrote(tok_kdev)
        PB[6].read(tok_kdev)

        if "t" not in parts:
            continue
        tok_ktr = S.do("tensor", lambda eng: [
            eng.transpose(PS[7][:, s * 16:(s + 1) * 16],
                          kdneg[:, s * 128:(s + 1) * 128], eye_t[0:16, 0:16])
            for s in range(4)], deps=[tok_kdev, tok_consts] + PB[7].wdep(),
            n_inst=4)
        PB[7].wrote(tok_ktr)
        r_kdneg.read(tok_ktr)
        tok_ktev = S.do("vector", lambda eng: eng.tensor_copy(
            out=kdT[:], in_=PS[7][:, 0:64]), deps=[tok_ktr] + r_kdT.wdep())
        r_kdT.wrote(tok_ktev)
        PB[7].read(tok_ktev)

        ek_toks = {}
        first_ek = True
        for j in (range(8) if "d" in parts else []):
            for s in range(4):
                pi = 2 + ((j * 4 + s) % 2)
                tok_kd = S.do("tensor", lambda eng, _j=j, _s=s, _p=pi: eng.matmul(
                    PS[_p][:, 0:128],
                    kfm[:, _j * TA + _s * 128: _j * TA + (_s + 1) * 128],
                    projbd_t[:], start=True, stop=True),
                    deps=[kev_toks[j], tok_init] + PB[pi].wdep())
                PB[pi].wrote(tok_kd)
                r_kfm.read(tok_kd)

                ti = (j * 4 + s) % 2
                tok_cp = S.do("vector", lambda eng, _p=pi, _t=ti:
                              eng.tensor_copy(out=kdtm[_t][:],
                                              in_=PS[_p][:, 0:128]),
                    deps=[tok_kd] + r_kdtm[ti].wdep())
                r_kdtm[ti].wrote(tok_cp)
                PB[pi].read(tok_cp)

                tok_red = S.do("vector", lambda eng, _t=ti: [
                    eng.reduce_max(out=maxscr[:, 0:1], in_=kdtm[_t][:, 0:64],
                                   axis=AX.X),
                    eng.reduce_max(out=maxscr[:, 1:2], in_=kdtm[_t][:, 64:128],
                                   axis=AX.X),
                ], deps=[tok_cp] + r_maxscr.wdep(), n_inst=2)
                r_maxscr.wrote(tok_red)
                r_kdtm[ti].read(tok_red)

                tok_gm = S.do("vector", lambda eng, _j=j: [
                    eng.tensor_max(gmax[:, 2 * _j:2 * _j + 1],
                                   gmax[:, 2 * _j:2 * _j + 1],
                                   maxscr[:, 0:1]),
                    eng.tensor_max(gmax[:, 2 * _j + 1:2 * _j + 2],
                                   gmax[:, 2 * _j + 1:2 * _j + 2],
                                   maxscr[:, 1:2]),
                ], deps=[tok_red] + r_gmax.wdep(), n_inst=2)
                r_gmax.wrote(tok_gm, reset=False)
                r_maxscr.read(tok_gm)


                tok_ek = S.do("scalar", lambda eng, _j=j, _s=s, _t=ti: [
                    eng.activation(
                        out=ek_ext[:, _s * (H * 65) + (2 * _j + hh) * 65:
                                   _s * (H * 65) + (2 * _j + hh) * 65 + 64],
                        in_=kdtm[_t][:, hh * 64:(hh + 1) * 64],
                        func=AF.Exp,
                        bias=kdT[:, _s * 16 + 2 * _j + hh:
                                 _s * 16 + 2 * _j + hh + 1],
                        scale=1.0)
                    for hh in range(2)],
                    deps=[tok_cp, tok_ktev] +
                    (r_ek.wdep() if first_ek else []), n_inst=2)
                first_ek = False
                r_kdT.read(tok_ek)
                r_kdtm[ti].read(tok_ek)
                ek_toks[(j, s)] = tok_ek
        if "d" in parts:
            r_ek.wrote(ek_toks[(0, 0)])
            for key, t in ek_toks.items():
                if key != (0, 0):
                    r_ek.wrote(t, reset=False)

        v_toks = []
        firstv = True
        for s in (range(4) if "v" in parts else []):
            for half in range(2):
                pi = 4 + ((s * 2 + half) % 2)
                tok_v = S.do("tensor", lambda eng, _s=s, _h=half, _b=bf,
                             _p=pi: [
                    eng.matmul(PS[_p][:, 0:512],
                               xop[:, dd * TA + _s * 128:
                                   dd * TA + (_s + 1) * 128],
                               wop[:, dd * D + _h * 512:
                                    dd * D + (_h + 1) * 512],
                               start=(pp == 0 and dd == 0),
                               stop=(pp == 2 and dd == NCH - 1))
                    for pp, (xop, wop) in enumerate(
                        ((xnh[_b], wvh), (xnh[_b], wvl), (xnl[_b], wvh)))
                    for dd in range(NCH)],
                    deps=[tok_wv[kch // 4]] + xn_toks + PB[pi].wdep(),
                    n_inst=24)
                PB[pi].wrote(tok_v)
                r_xnh[bf].read(tok_v)
                r_xnl[bf].read(tok_v)
                r_wv.read(tok_v)

                tok_vev = S.do("vector", lambda eng, _s=s, _h=half, _p=pi:
                               eng.tensor_copy(
                    out=v_ext.rearrange("p (s h c) -> p s h c", s=4, c=65)[
                        :, _s, _h * 8:(_h + 1) * 8, 0:64],
                    in_=PS[_p][:, 0:512].rearrange("p (h c) -> p h c", c=64)),
                    deps=[tok_v] + (r_v.wdep() if firstv else []))
                firstv = False
                PB[pi].read(tok_vev)
                v_toks.append(tok_vev)
        if "v" in parts:
            r_v.wrote(v_toks[0])
            for t in v_toks[1:]:
                r_v.wrote(t, reset=False)

        for h in (range(H) if "z" in parts else []):
            pi = 6 + (h % 2)
            tok_kv = S.do("tensor", lambda eng, _h=h, _p=pi: [
                eng.matmul(PS[_p][0:65, 0:65],
                           ek_ext[:, s * (H * 65) + _h * 65:
                                  s * (H * 65) + _h * 65 + 65],
                           v_ext[:, s * (H * 65) + _h * 65:
                                 s * (H * 65) + _h * 65 + 65],
                           start=(s == 0), stop=(s == 3))
                for s in range(4)],
                deps=[ek_toks[(h // 2, sx)] for sx in range(4)] + v_toks +
                PB[pi].wdep() + [tok_ones], n_inst=4)
            PB[pi].wrote(tok_kv)
            r_ek.read(tok_kv)
            r_v.read(tok_kv)

            tok_acc = S.do("vector", lambda eng, _h=h, _p=pi, _sl=slot0:
                           eng.tensor_add(
                kv_acc[0:65, (_sl + _h) * 65:(_sl + _h) * 65 + 65],
                kv_acc[0:65, (_sl + _h) * 65:(_sl + _h) * 65 + 65],
                PS[_p][0:65, 0:65]), deps=[tok_kv] + r_kv.wdep())
            r_kv.wrote(tok_acc, reset=False)
            PB[pi].read(tok_acc)

    if upto == "A0":
        S.wait_only("gpsimd", S.barrier_tokens())
        S.flush(block)
        phA.close()
        ctx.close()
        return nc

    tok_gred = S.do("vector", lambda eng: eng.reduce_max(
        out=maxscr[:, 2:3], in_=gmax[:], axis=AX.X),
        deps=r_gmax.wdep() + r_maxscr.wdep())
    r_maxscr.wrote(tok_gred)
    r_gmax.read(tok_gred)

    tok_gtr = S.do("tensor", lambda eng: eng.transpose(
        PS[7][0:1, 0:128], maxscr[:, 2:3], eye_t[:]),
        deps=[tok_gred] + PB[7].wdep())
    PB[7].wrote(tok_gtr)
    r_maxscr.read(tok_gtr)

    g1tok = S.do("vector", lambda eng: eng.tensor_copy(
        out=mxrow[0:1, 0:128], in_=PS[7][0:1, 0:128]),
        deps=[tok_gtr] + r_mxrow.wdep())
    r_mxrow.wrote(g1tok)
    PB[7].read(g1tok)
    g2tok = S.do("vector", lambda eng: eng.reduce_max(
        out=mxrow[0:1, 128:129], in_=mxrow[0:1, 0:128], axis=AX.X),
        deps=[g1tok])
    r_mxrow.wrote(g2tok, reset=False)
    g3tok = S.do("vector", lambda eng: eng.memset(mx_sb[0:1, 0:16], 1.0),
                 deps=r_mx.wdep())
    r_mx.wrote(g3tok)
    g4tok = S.do("vector", lambda eng: eng.tensor_scalar(
        out=mx_sb[0:1, 0:16], in0=mx_sb[0:1, 0:16],
        scalar1=mxrow[0:1, 128:129], scalar2=None, op0=ALU.mult),
        deps=[g2tok, g3tok])
    r_mx.wrote(g4tok, reset=False)
    r_mxrow.read(g4tok)

    tok_ccs = S.do("gpsimd", lambda eng: eng.dma_start(
        out=ccin[:], in_=mx_sb[0:1, 0:16]), deps=[g4tok], dma=True,
        chan="c7")
    r_mx.read(tok_ccs)

    if use_cc:
        tok_cc = S.do("gpsimd", lambda eng: eng.collective_compute(
            "AllReduce", ALU.max, replica_groups=[[0, 1, 2, 3, 4, 5, 6, 7]],
            ins=[ccin[:]], outs=[ccout[:]]), deps=[tok_ccs], inc_last=True)
    else:
        tok_cc = tok_ccs

    tok_mxl = S.do("sync", lambda eng: eng.dma_start(
        out=mx_sb[0:1, 0:16], in_=(ccout[:] if use_cc else ccin[:])),
        deps=[tok_cc] + r_mx.wdep(), dma=True, chan="c8")
    r_mx.wrote(tok_mxl)

    if "dbg_mx" in debug:
        t = S.do("gpsimd", lambda eng: eng.dma_start(
            out=dbg_mx[:], in_=mx_sb[0:1, 0:16]), deps=[tok_mxl], dma=True,
            chan="c9")
        r_mx.read(t)
    if "dbg_aend" in debug:
        t = S.do("gpsimd", lambda eng: eng.dma_start(
            out=dbg_aend[:, 0:NCH * 512], in_=kfm[:]), deps=S.barrier_tokens()[:0] or [tok_gred], dma=True, chan="c9")
        r_kfm.read(t)
        t = S.do("gpsimd", lambda eng: eng.dma_start(
            out=dbg_aend[:, 4096:4096 + 128], in_=projbd_t[:]), deps=[tok_gred], dma=True, chan="c9")
        t = S.do("gpsimd", lambda eng: eng.dma_start(
            out=dbg_aend[:, 4224:4224 + 64], in_=kdT[:]), deps=[tok_gred], dma=True, chan="c9")
        r_kdT.read(t)
        t = S.do("gpsimd", lambda eng: eng.dma_start(
            out=dbg_aend[:, 4288:4288 + 4160], in_=ek_ext[:]), deps=[tok_gred], dma=True, chan="c9")
        r_ek.read(t)
    if "dbg_kdtm" in debug:
        t = S.do("gpsimd", lambda eng: eng.dma_start(
            out=dbg_kd[:, 0:128], in_=kdtm[1][:]), deps=[tok_gred], dma=True,
            chan="c9")
        r_kdtm[1].read(t)
        t = S.do("gpsimd", lambda eng: eng.dma_start(
            out=dbg_kd[:, 128:132], in_=maxscr[:]), deps=[tok_gred], dma=True,
            chan="c9")
        r_maxscr.read(t)
    if "dbg_lmx" in debug:
        t = S.do("gpsimd", lambda eng: eng.dma_start(
            out=dbg_lmx[:], in_=mxrow[0:1, 0:144]), deps=[g4tok], dma=True,
            chan="c9")
        r_mxrow.read(t)
        t = S.do("gpsimd", lambda eng: eng.dma_start(
            out=dbg_gmax[:], in_=gmax[:]), deps=[tok_gred], dma=True,
            chan="c9")
        r_gmax.read(t)

    tok_neg = S.do("vector", lambda eng: eng.tensor_scalar(
        out=mx_sb[0:1, 1:2], in0=mx_sb[0:1, 0:1], scalar1=-1.0, scalar2=None,
        op0=ALU.mult), deps=[tok_mxl])
    r_mx.wrote(tok_neg, reset=False)
    tok_c = S.do("scalar", lambda eng: eng.activation(
        out=mx_sb[0:1, 2:3], in_=mx_sb[0:1, 1:2], func=AF.Exp),
        deps=[tok_neg])
    r_mx.wrote(tok_c, reset=False)

    tok_bcc = S.do("tensor", lambda eng: eng.matmul(
        PS[7][:, 0:1], ones1_t[:], mx_sb[0:1, 2:3], start=True, stop=True),
        deps=[tok_c] + PB[7].wdep())
    PB[7].wrote(tok_bcc)
    r_mx.read(tok_bcc)

    sc1 = S.do("vector", lambda eng: eng.tensor_copy(
        out=scale_col[:], in_=PS[7][:, 0:1]), deps=[tok_bcc] + r_scol.wdep())
    r_scol.wrote(sc1)
    PB[7].read(sc1)
    sc2 = S.do("vector", lambda eng: eng.memset(scale_col[64:65, 0:1], EPS_K),
               deps=[sc1])
    r_scol.wrote(sc2, reset=False)

    tok_mp = S.do("vector", lambda eng: [
        eng.tensor_scalar(out=kv_acc[0:65, sl * 65:sl * 65 + 65],
                          in0=kv_acc[0:65, sl * 65:sl * 65 + 65],
                          scalar1=scale_col[0:65, 0:1], scalar2=None,
                          op0=ALU.mult)
        for sl in range(32)], deps=[sc2] + r_kv.wdep(), n_inst=32)
    r_kv.wrote(tok_mp)
    r_scol.read(tok_mp)

    if "dbg_kv" in debug:
        t = S.do("gpsimd", lambda eng: eng.dma_start(
            out=dbg_kv[:], in_=kv_acc[:]), deps=[tok_mp], dma=True, chan="c9")
        r_kv.read(t)

    S.flush(block)
    phA.close()
    S.phase_start()

    if upto == "A":
        S.wait_only("gpsimd", S.barrier_tokens())
        S.flush(block)
        ctx.close()
        return nc

    phB1 = ExitStack()
    wqh = sb("wqh", [128, NCH * D], BF16, stack=phB1)
    wql = sb("wql", [128, NCH * D], BF16, stack=phB1)
    woh = sb("woh", [128, NCH * D], BF16, stack=phB1)
    wol = sb("wol", [128, NCH * D], BF16, stack=phB1)
    wstg1 = [sb("wstg1_0", [128, D // 2], stack=phB1)]
    wtmp1 = [sb("wtmp1_0", [128, D // 2], stack=phB1)]
    xn1h = sb("xn1h", [128, NCH * TA], BF16, stack=phB1)
    xn1l = sb("xn1l", [128, NCH * TA], BF16, stack=phB1)
    attnh = sb("attnh", [128, NCH * TA], BF16, stack=phB1)
    attnl = sb("attnl", [128, NCH * TA], BF16, stack=phB1)
    tmpb1 = sb("tmpb1", [128, TA], stack=phB1)
    qfm = sb("qfm", [128, NCH * TA], stack=phB1)
    sqb1 = [sb("sqb1_%d" % i, [128, TA], stack=phB1) for i in range(2)]
    qdneg = sb("qdneg", [16, TA], stack=phB1)
    qdT = sb("qdT", [128, 4 * 16], stack=phB1)
    qf_ext = sb("qf_ext", [128, 4 * H * 65], stack=phB1)
    qfTe = sb("qfTe", [128, 512], stack=phB1)
    qfTo = sb("qfTo", [128, 512], stack=phB1)
    den_sb = sb("den_sb", [1, 2048], stack=phB1)
    biascol = sb("biascol", [128, 4], stack=phB1)
    attn_t = sb("attn_t", [128, NCH * TA], stack=phB1)
    z1 = sb("z1", [128, NCH * TA], stack=phB1)
    qdtm = [sb("qdtm%d" % i, [128, 128], stack=phB1) for i in range(2)]

    r_xn1h = RW()
    r_xn1l = RW()
    r_attnh = RW()
    r_attnl = RW()
    r_tmpb1 = RW()
    r_wstg1 = [RW(), RW()]
    r_wtmp1 = [RW(), RW()]
    r_wq = RW()
    r_wo = RW()
    r_qfm = RW()
    r_sq1 = [RW(), RW()]
    r_qdneg = RW()
    r_qdT = RW()
    r_qf = RW()
    r_qfTe = RW()
    r_qfTo = RW()
    r_den = RW()
    r_bias = RW()
    r_attn = RW()
    r_z1 = RW()
    r_qdtm = [RW(), RW()]

    def ld_split_w1(srct, dsth, dstl, chan, rw):
        last = None
        half_toks = [None, None]
        HD = D // 2
        for i in range(2 * NCH):
            dd, hv = i % NCH, i // NCH
            sg = 0
            tok_l = S.do("sync", lambda eng, _d=dd, _h=hv, _s=sg:
                         eng.dma_start(
                out=wstg1[_s][:],
                in_=srct[_d * 128:(_d + 1) * 128, _h * HD:(_h + 1) * HD]),
                deps=r_wstg1[sg].wdep(), dma=True, chan=chan)
            r_wstg1[sg].wrote(tok_l)
            tok_s = split3s(
                wstg1[sg][:],
                dsth[:, dd * D + hv * HD:dd * D + (hv + 1) * HD],
                dstl[:, dd * D + hv * HD:dd * D + (hv + 1) * HD],
                wtmp1[sg][:],
                [tok_l] + r_wtmp1[sg].wdep() +
                (rw.wdep() if i == 0 else []))
            r_wstg1[sg].read(tok_s)
            r_wtmp1[sg].wrote(tok_s)
            rw.wrote(tok_s, reset=(i == 0))
            half_toks[hv] = tok_s
            last = tok_s
        return half_toks

    tok_wq = ld_split_w1(wq, wqh, wql, "c1", r_wq)
    tok_wo = ld_split_w1(wo, woh, wol, "c2", r_wo)

    for it in range(NT):
        bf = it % 2
        t0 = it * TA
        slot0 = (it // 4) * 16

        tok_xn = S.do("sync", lambda eng, _t0=t0: [
            ld3(eng, xn1h, xnh_h, _t0, TA),
            ld3(eng, xn1l, xnl_h, _t0, TA)],
            deps=r_xn1h.wdep() + r_xn1l.wdep(), dma=True, n_dma=2,
            chan="c3")
        r_xn1h.wrote(tok_xn)
        r_xn1l.wrote(tok_xn)

        zpre_toks = []
        for chh in range(NCH):
            zp1 = S.do("vector", lambda eng, _c=chh: eng.tensor_add(
                z1[:, _c * TA:(_c + 1) * TA],
                xn1h[:, _c * TA:(_c + 1) * TA],
                xn1l[:, _c * TA:(_c + 1) * TA]),
                deps=[tok_xn] + (r_z1.wdep() if chh == 0 else []))
            r_z1.wrote(zp1, reset=(chh == 0))
            r_xn1h.read(zp1)
            r_xn1l.read(zp1)
            zpre_toks.append(zp1)

        qev_toks = []
        qd_last = None
        for kch in range(NCH):
            pi = kch % 2
            tok_q = S.do("tensor", lambda eng, _k=kch: [
                eng.matmul(PS[_k % 2][:, 0:TA],
                           wop[:, dd * D + _k * 128: dd * D + (_k + 1) * 128],
                           xop[:, dd * TA:(dd + 1) * TA],
                           start=(pp == 0 and dd == 0),
                           stop=(pp == 2 and dd == NCH - 1))
                for pp, (wop, xop) in enumerate(
                    ((wqh, xn1h), (wqh, xn1l), (wql, xn1h)))
                for dd in range(NCH)],
                deps=[tok_wq[kch // 4], tok_xn] + PB[pi].wdep(),
                n_inst=24)
            PB[pi].wrote(tok_q)
            r_xn1h.read(tok_q)
            r_xn1l.read(tok_q)
            r_wq.read(tok_q)

            tok_qev = S.do("vector", lambda eng, _k=kch: eng.tensor_copy(
                out=qfm[:, _k * TA:(_k + 1) * TA], in_=PS[_k % 2][:, 0:TA]),
                deps=[tok_q] + (r_qfm.wdep() if kch == 0 else []))
            PB[pi].read(tok_qev)
            qev_toks.append(tok_qev)

            tok_qsq = S.do("scalar", lambda eng, _k=kch: eng.activation(
                out=sqb1[_k % 2][:, 0:TA], in_=PS[_k % 2][:, 0:TA],
                func=AF.Square), deps=[tok_q, tok_qev] + r_sq1[kch % 2].wdep())
            r_sq1[kch % 2].wrote(tok_qsq)
            PB[pi].read(tok_qsq)

            tok_qdmm = S.do("tensor", lambda eng, _k=kch: eng.matmul(
                PS[6][0:16, 0:TA], blockneg_t[:, _k * 16:(_k + 1) * 16],
                sqb1[_k % 2][:, 0:TA], start=(_k == 0), stop=(_k == NCH - 1)),
                deps=[tok_qsq] + (PB[6].wdep() if kch == 0 else []))
            r_sq1[kch % 2].read(tok_qdmm)
            qd_last = tok_qdmm
        PB[6].wrote(qd_last)
        r_qfm.wrote(qev_toks[0])
        for t in qev_toks[1:]:
            r_qfm.wrote(t, reset=False)

        tok_qdev = S.do("vector", lambda eng: eng.tensor_copy(
            out=qdneg[:], in_=PS[6][0:16, 0:TA]),
            deps=[qd_last] + r_qdneg.wdep())
        r_qdneg.wrote(tok_qdev)
        PB[6].read(tok_qdev)

        tok_qtr = S.do("tensor", lambda eng: [
            eng.transpose(PS[7][:, s * 16:(s + 1) * 16],
                          qdneg[:, s * 128:(s + 1) * 128], eye_t[0:16, 0:16])
            for s in range(4)], deps=[tok_qdev] + PB[7].wdep(), n_inst=4)
        PB[7].wrote(tok_qtr)
        r_qdneg.read(tok_qtr)
        tok_qtev = S.do("vector", lambda eng: eng.tensor_copy(
            out=qdT[:], in_=PS[7][:, 0:64]), deps=[tok_qtr] + r_qdT.wdep())
        r_qdT.wrote(tok_qtev)
        PB[7].read(tok_qtev)

        qf_toks = {}
        first_qf = True
        for j in range(8):
            for s in range(4):
                pi = 2 + ((j * 4 + s) % 2)
                tok_qd = S.do("tensor", lambda eng, _j=j, _s=s, _p=pi: eng.matmul(
                    PS[_p][:, 0:128],
                    qfm[:, _j * TA + _s * 128: _j * TA + (_s + 1) * 128],
                    projbd_t[:], start=True, stop=True),
                    deps=[qev_toks[j], tok_init] + PB[pi].wdep())
                PB[pi].wrote(tok_qd)
                r_qfm.read(tok_qd)

                ti = (j * 4 + s) % 2
                tok_cp = S.do("vector", lambda eng, _p=pi, _t=ti:
                              eng.tensor_copy(out=qdtm[_t][:],
                                              in_=PS[_p][:, 0:128]),
                    deps=[tok_qd] + r_qdtm[ti].wdep())
                r_qdtm[ti].wrote(tok_cp)
                PB[pi].read(tok_cp)

                tok_red = S.do("vector", lambda eng, _t=ti: [
                    eng.reduce_max(out=maxscr[:, 0:1], in_=qdtm[_t][:, 0:64],
                                   axis=AX.X),
                    eng.reduce_max(out=maxscr[:, 1:2], in_=qdtm[_t][:, 64:128],
                                   axis=AX.X),
                ], deps=[tok_cp] + r_maxscr.wdep(), n_inst=2)
                r_maxscr.wrote(tok_red)
                r_qdtm[ti].read(tok_red)

                tok_bias = S.do("vector", lambda eng, _j=j, _s=s: [
                    eng.tensor_sub(biascol[:, 0:1],
                                   qdT[:, _s * 16 + 2 * _j:
                                       _s * 16 + 2 * _j + 1],
                                   maxscr[:, 0:1]),
                    eng.tensor_sub(biascol[:, 1:2],
                                   qdT[:, _s * 16 + 2 * _j + 1:
                                       _s * 16 + 2 * _j + 2],
                                   maxscr[:, 1:2]),
                ], deps=[tok_red, tok_qtev] + r_bias.wdep(), n_inst=2)
                r_bias.wrote(tok_bias)
                r_maxscr.read(tok_bias)
                r_qdT.read(tok_bias)

                tok_qf = S.do("scalar", lambda eng, _j=j, _s=s, _t=ti: [
                    eng.activation(
                        out=qf_ext[:, _s * (H * 65) + (2 * _j + hh) * 65:
                                   _s * (H * 65) + (2 * _j + hh) * 65 + 64],
                        in_=qdtm[_t][:, hh * 64:(hh + 1) * 64],
                        func=AF.Exp, bias=biascol[:, hh:hh + 1], scale=1.0)
                    for hh in range(2)],
                    deps=[tok_cp, tok_bias] +
                    (r_qf.wdep() if first_qf else []), n_inst=2)
                first_qf = False
                r_bias.read(tok_qf)
                r_qdtm[ti].read(tok_qf)
                qf_toks[(j, s)] = tok_qf
        r_qf.wrote(qf_toks[(0, 0)])
        for key, t in qf_toks.items():
            if key != (0, 0):
                r_qf.wrote(t, reset=False)

        geps_toks = []
        for s in range(4):
            tok_eps = S.do("vector", lambda eng, _s=s: eng.tensor_scalar(
                out=qf_ext.rearrange("p (s h c) -> p s h c", s=4, c=65)[
                    :, _s, :, 0:64],
                in0=qf_ext.rearrange("p (s h c) -> p s h c", s=4, c=65)[
                    :, _s, :, 0:64],
                scalar1=EPS_K, scalar2=None, op0=ALU.add),
                deps=[qf_toks[(jj, s)] for jj in range(8)])
            r_qf.wrote(tok_eps, reset=False)
            tok_g = S.do("vector", lambda eng, _s=s: [
                eng.reduce_sum(
                    out=qf_ext[:, _s * (H * 65) + h * 65 + 64:
                               _s * (H * 65) + h * 65 + 65],
                    in_=qf_ext[:, _s * (H * 65) + h * 65:
                               _s * (H * 65) + h * 65 + 64],
                    axis=AX.X)
                for h in range(H)], deps=[tok_eps], n_inst=16)
            r_qf.wrote(tok_g, reset=False)
            geps_toks.append(tok_g)

        for j in range(8):
            he, ho = 2 * j, 2 * j + 1
            sle, slo = slot0 + he, slot0 + ho

            pT = j % 2
            tok_tre = S.do("tensor", lambda eng, _h=he, _p=pT: [
                eng.transpose(PS[_p][0:65, s * 128:(s + 1) * 128],
                              qf_ext[:, s * (H * 65) + _h * 65:
                                     s * (H * 65) + _h * 65 + 65], eye_t[:])
                for s in range(4)], deps=geps_toks + PB[pT].wdep(), n_inst=4)
            PB[pT].wrote(tok_tre)
            r_qf.read(tok_tre)
            tok_qfe = S.do("vector", lambda eng, _p=pT: eng.tensor_copy(
                out=qfTe[0:65, :], in_=PS[_p][0:65, 0:512]),
                deps=[tok_tre] + r_qfTe.wdep())
            r_qfTe.wrote(tok_qfe)
            PB[pT].read(tok_qfe)

            tok_tro = S.do("tensor", lambda eng, _h=ho, _p=pT: [
                eng.transpose(PS[5][0:65, s * 128:(s + 1) * 128],
                              qf_ext[:, s * (H * 65) + _h * 65:
                                     s * (H * 65) + _h * 65 + 65], eye_t[:])
                for s in range(4)], deps=geps_toks + PB[5].wdep(), n_inst=4)
            PB[5].wrote(tok_tro)
            r_qf.read(tok_tro)
            tok_qfo = S.do("vector", lambda eng: eng.tensor_copy(
                out=qfTo[0:65, :], in_=PS[5][0:65, 0:512]),
                deps=[tok_tro] + r_qfTo.wdep())
            r_qfTo.wrote(tok_qfo)
            PB[5].read(tok_qfo)

            tok_nume = S.do("tensor", lambda eng, _sl=sle: eng.matmul(
                PS[4][0:64, 0:512], kv_acc[0:65, _sl * 65:_sl * 65 + 64],
                qfTe[0:65, :], start=True, stop=True),
                deps=[tok_qfe, tok_mp] + PB[4].wdep())
            PB[4].wrote(tok_nume)
            r_qfTe.read(tok_nume)
            r_kv.read(tok_nume)
            tok_numo = S.do("tensor", lambda eng, _sl=slo: eng.matmul(
                PS[4][64:128, 0:512], kv_acc[0:65, _sl * 65:_sl * 65 + 64],
                qfTo[0:65, :], start=True, stop=True),
                deps=[tok_qfo, tok_mp] + PB[4].wdep())
            PB[4].wrote(tok_numo, reset=False)
            r_qfTo.read(tok_numo)
            r_kv.read(tok_numo)

            tok_dene = S.do("tensor", lambda eng, _sl=sle: eng.matmul(
                PS[2][0:1, 0:512], kv_acc[0:65, _sl * 65 + 64:_sl * 65 + 65],
                qfTe[0:65, :], start=True, stop=True),
                deps=[tok_qfe, tok_mp] + PB[2].wdep())
            PB[2].wrote(tok_dene)
            r_qfTe.read(tok_dene)
            tok_deno = S.do("tensor", lambda eng, _sl=slo: eng.matmul(
                PS[3][0:1, 0:512], kv_acc[0:65, _sl * 65 + 64:_sl * 65 + 65],
                qfTo[0:65, :], start=True, stop=True),
                deps=[tok_qfo, tok_mp] + PB[3].wdep())
            PB[3].wrote(tok_deno)
            r_qfTo.read(tok_deno)

            d1 = S.do("vector", lambda eng: [
                eng.tensor_copy(out=den_sb[0:1, 0:512], in_=PS[2][0:1, 0:512]),
                eng.tensor_copy(out=den_sb[0:1, 512:1024],
                                in_=PS[3][0:1, 0:512]),
            ], deps=[tok_dene, tok_deno] + r_den.wdep(), n_inst=2)
            r_den.wrote(d1)
            PB[2].read(d1)
            PB[3].read(d1)
            d2a = S.do("vector", lambda eng: eng.reciprocal(
                out=den_sb[0:1, 1024:1536], in_=den_sb[0:1, 0:512]),
                deps=[d1])
            r_den.wrote(d2a, reset=False)
            d2b = S.do("vector", lambda eng: eng.reciprocal(
                out=den_sb[0:1, 1536:2048], in_=den_sb[0:1, 512:1024]),
                deps=[d1])
            r_den.wrote(d2b, reset=False)

            tok_bden = S.do("tensor", lambda eng: [
                eng.matmul(PS[6][:, 0:512], maskeo_t[0:1, 0:128],
                           den_sb[0:1, 1024:1536], start=True, stop=False),
                eng.matmul(PS[6][:, 0:512], maskeo_t[0:1, 128:256],
                           den_sb[0:1, 1536:2048], start=False, stop=True),
            ], deps=[d2b, tok_consts] + PB[6].wdep(), n_inst=2)
            PB[6].wrote(tok_bden)
            r_den.read(tok_bden)

            a1 = S.do("vector", lambda eng, _j=j: eng.tensor_copy(
                out=attn_t[:, _j * TA:(_j + 1) * TA], in_=PS[4][:, 0:512]),
                deps=[tok_nume, tok_numo] +
                (r_attn.wdep() if j == 0 else []))
            r_attn.wrote(a1, reset=(j == 0))
            PB[4].read(a1)
            a2 = S.do("vector", lambda eng, _j=j: eng.tensor_mul(
                attn_t[:, _j * TA:(_j + 1) * TA],
                attn_t[:, _j * TA:(_j + 1) * TA], PS[6][:, 0:512]),
                deps=[a1, tok_bden])
            r_attn.wrote(a2, reset=False)
            PB[6].read(a2)
            a3 = split3s(attn_t[:, j * TA:(j + 1) * TA],
                         attnh[:, j * TA:(j + 1) * TA],
                         attnl[:, j * TA:(j + 1) * TA], tmpb1[:],
                         [a2] + r_tmpb1.wdep() +
                         (r_attnh.wdep() + r_attnl.wdep() if j == 0 else []))
            r_attn.read(a3)
            r_tmpb1.wrote(a3)
            r_attnh.wrote(a3, reset=(j == 0))
            r_attnl.wrote(a3, reset=(j == 0))

        attn_toks = list(r_attnh.w) + list(r_attnl.w)

        if it == 0 and "dbg_attn" in debug:
            t = S.do("gpsimd", lambda eng: eng.dma_start(
                out=dbg_attn[:], in_=attn_t[:]), deps=attn_toks, dma=True,
                chan="c9")
            r_attn.read(t)
            t = S.do("gpsimd", lambda eng: eng.dma_start(
                out=dbg_kv[:], in_=kv_acc[:]), deps=attn_toks, dma=True,
                chan="c9")
            r_kv.read(t)
            t = S.do("gpsimd", lambda eng: eng.dma_start(
                out=dbg_qf[:], in_=qf_ext[:]), deps=geps_toks, dma=True,
                chan="c9")
            r_qf.read(t)
            t = S.do("gpsimd", lambda eng: eng.dma_start(
                out=dbg_qft[:], in_=qfTo[:]), deps=r_qfTo.rdep(), dma=True,
                chan="c9")
            r_qfTo.read(t)
            t = S.do("gpsimd", lambda eng: eng.dma_start(
                out=dbg_den[:], in_=den_sb[:]), deps=r_den.rdep(), dma=True,
                chan="c9")
            r_den.read(t)

        z1_toks = []
        for o in range(NCH):
            pi = o % 2
            tok_ao = S.do("tensor", lambda eng, _o=o: [
                eng.matmul(PS[_o % 2][:, 0:TA],
                           wop[:, p * D + _o * 128: p * D + (_o + 1) * 128],
                           aop[:, p * TA:(p + 1) * TA],
                           start=(pp == 0 and p == 0),
                           stop=(pp == 2 and p == NCH - 1))
                for pp, (wop, aop) in enumerate(
                    ((woh, attnh), (woh, attnl), (wol, attnh)))
                for p in range(NCH)],
                deps=[tok_wo[kch // 4]] + attn_toks + PB[pi].wdep(),
                n_inst=24)
            PB[pi].wrote(tok_ao)
            r_attnh.read(tok_ao)
            r_attnl.read(tok_ao)
            r_wo.read(tok_ao)

            tok_z1 = S.do("vector", lambda eng, _o=o: eng.tensor_add(
                z1[:, _o * TA:(_o + 1) * TA], z1[:, _o * TA:(_o + 1) * TA],
                PS[_o % 2][:, 0:TA]),
                deps=[tok_ao, zpre_toks[o]])
            r_z1.wrote(tok_z1, reset=False)
            PB[pi].read(tok_z1)
            z1_toks.append(tok_z1)

        tok_bmu, tok_brs = ln_stats(z1, r_z1, sqb1, r_sq1, TA,
                                    6, 7, 2, 3, z1_toks, EPS_LN)

        o1_toks = []
        for chh in range(NCH):
            t = ln_norm_chunk(attn_t, r_attn, chh * TA, z1, r_z1, chh * TA,
                              TA, 2, 3, tok_bmu, tok_brs,
                              g1_t[:, chh:chh + 1], b1l_t[:, chh:chh + 1],
                              first=(chh == 0))
            ts = split3s(attn_t[:, chh * TA:(chh + 1) * TA],
                         attnh[:, chh * TA:(chh + 1) * TA],
                         attnl[:, chh * TA:(chh + 1) * TA], tmpb1[:],
                         [t] + r_tmpb1.wdep() +
                         (r_attnh.wdep() + r_attnl.wdep()
                          if chh == 0 else []))
            r_attn.read(ts)
            r_tmpb1.wrote(ts)
            r_attnh.wrote(ts, reset=(chh == 0))
            r_attnl.wrote(ts, reset=(chh == 0))
            o1_toks.append(ts)

        tok_o1sp = S.do("gpsimd", lambda eng, _t0=t0: [
            st3(eng, o1h_h, attnh, _t0, TA),
            st3(eng, o1l_h, attnl, _t0, TA)],
            deps=o1_toks, dma=True, n_dma=2, chan="c5")
        r_attnh.read(tok_o1sp)
        r_attnl.read(tok_o1sp)

    S.flush(block)
    phB1.close()
    S.phase_start()

    if upto == "B1":
        S.wait_only("gpsimd", S.barrier_tokens())
        S.flush(block)
        ctx.close()
        return nc

    TB = 256
    NTB = TPC // TB

    phB2 = ExitStack()
    w1h = sb("w1h", [128, NCH * DFF], BF16, stack=phB2)
    w1l = sb("w1l", [128, NCH * DFF], BF16, stack=phB2)
    wstg2 = [sb("wstg2_%d" % i, [128, DFF // 2], stack=phB2)
             for i in range(2)]
    wtmp2 = [sb("wtmp2_%d" % i, [128, DFF // 2], stack=phB2)
             for i in range(2)]
    o1th = [sb("o1th%d" % i, [128, NCH * TB], BF16, stack=phB2)
            for i in range(2)]
    o1tl = [sb("o1tl%d" % i, [128, NCH * TB], BF16, stack=phB2)
            for i in range(2)]
    hcb = [sb("hcb%d" % i, [128, TB], stack=phB2) for i in range(2)]
    hcbh = [sb("hcbh%d" % i, [128, TB], BF16, stack=phB2) for i in range(2)]
    hcbl = [sb("hcbl%d" % i, [128, TB], BF16, stack=phB2) for i in range(2)]
    tmp1 = [sb("tmp1_%d" % i, [128, TB], stack=phB2) for i in range(2)]
    tmp2 = [sb("tmp2_%d" % i, [128, TB], stack=phB2) for i in range(2)]

    r_o1t = [RW(), RW()]
    r_hcb = [RW(), RW()]
    r_hcbh = [RW(), RW()]
    r_hcbl = [RW(), RW()]
    r_t1 = [RW(), RW()]
    r_t2 = [RW(), RW()]
    r_wstg2 = [RW(), RW()]
    r_wtmp2 = [RW(), RW()]
    r_w1 = RW()

    tok_w1 = None
    HF2 = DFF // 2
    for i in range(2 * NCH):
        dd, hv = i // 2, i % 2
        sg = i % 2
        tok_l = S.do("sync", lambda eng, _d=dd, _h=hv, _s=sg: eng.dma_start(
            out=wstg2[_s][:],
            in_=w1[_d * 128:(_d + 1) * 128, _h * HF2:(_h + 1) * HF2]),
            deps=r_wstg2[sg].wdep(), dma=True, chan="c1")
        r_wstg2[sg].wrote(tok_l)
        tok_s = split3s(
            wstg2[sg][:],
            w1h[:, dd * DFF + hv * HF2:dd * DFF + (hv + 1) * HF2],
            w1l[:, dd * DFF + hv * HF2:dd * DFF + (hv + 1) * HF2],
            wtmp2[sg][:],
            [tok_l] + r_wtmp2[sg].wdep() + (r_w1.wdep() if i == 0 else []))
        r_wstg2[sg].read(tok_s)
        r_wtmp2[sg].wrote(tok_s)
        r_w1.wrote(tok_s, reset=(i == 0))
        tok_w1 = tok_s

    for it in range(NTB):
        bf = it % 2
        t0 = it * TB

        tok_o1 = S.do("sync", lambda eng, _t0=t0, _b=bf: [
            ld3(eng, o1th[_b], o1h_h, _t0, TB),
            ld3(eng, o1tl[_b], o1l_h, _t0, TB)],
            deps=r_o1t[bf].wdep(), dma=True, n_dma=2,
            chan="c3" if bf == 0 else "c4")
        r_o1t[bf].wrote(tok_o1)

        for f in range(DFF // 128):
            pi = f % 4
            fb = f % 2
            tok_mm = S.do("tensor", lambda eng, _f=f, _b=bf, _p=pi: [
                eng.matmul(PS[_p][:, 0:TB],
                           wop[:, dd * DFF + _f * 128:
                               dd * DFF + (_f + 1) * 128],
                           xop[:, dd * TB:(dd + 1) * TB],
                           start=(pp == 0 and dd == 0),
                           stop=(pp == 2 and dd == NCH - 1))
                for pp, (wop, xop) in enumerate(
                    ((w1h, o1th[_b]), (w1h, o1tl[_b]), (w1l, o1th[_b])))
                for dd in range(NCH)],
                deps=[tok_w1, tok_o1] + PB[pi].wdep(), n_inst=24)
            PB[pi].wrote(tok_mm)
            r_o1t[bf].read(tok_mm)
            r_w1.read(tok_mm)

            e1 = S.do("scalar", lambda eng, _f=f, _p=pi, _fb=fb: [
                eng.activation(out=hcb[_fb][:], in_=PS[_p][:, 0:TB],
                               func=AF.Relu, bias=b1_t[:, _f:_f + 1],
                               scale=1.0),
                eng.activation(out=tmp1[_fb][:], in_=PS[_p][:, 0:TB],
                               func=AF.Relu, bias=nb1_t[:, _f:_f + 1],
                               scale=-1.0),
            ], deps=[tok_mm, tok_init] + r_hcb[fb].wdep() + r_t1[fb].wdep(),
                n_inst=2)
            r_hcb[fb].wrote(e1)
            r_t1[fb].wrote(e1)
            PB[pi].read(e1)
            e2 = S.do("scalar", lambda eng, _fb=fb: eng.activation(
                out=tmp2[_fb][:], in_=tmp1[_fb][:], func=AF.Exp, scale=-1.0),
                deps=[e1] + r_t2[fb].wdep())
            r_t2[fb].wrote(e2)
            r_t1[fb].read(e2)
            e3 = S.do("vector", lambda eng, _fb=fb: eng.tensor_scalar(
                out=tmp2[_fb][:], in0=tmp2[_fb][:], scalar1=-1.0,
                scalar2=None, op0=ALU.add), deps=[e2])
            r_t2[fb].wrote(e3, reset=False)
            e4 = S.do("vector", lambda eng, _fb=fb: eng.tensor_add(
                hcb[_fb][:], hcb[_fb][:], tmp2[_fb][:]), deps=[e3, e1])
            r_hcb[fb].wrote(e4, reset=False)
            r_t2[fb].read(e4)
            e5 = split3s(hcb[fb][:], hcbh[fb][:], hcbl[fb][:], tmp1[fb][:],
                         [e4] + r_t1[fb].wdep() + r_hcbh[fb].wdep() +
                         r_hcbl[fb].wdep())
            r_hcb[fb].read(e5)
            r_t1[fb].wrote(e5)
            r_hcbh[fb].wrote(e5)
            r_hcbl[fb].wrote(e5)

            tok_hsp = S.do("gpsimd", lambda eng, _f=f, _t0=t0, _fb=fb: [
                eng.dma_start(
                    out=hh_h[_f * 128:(_f + 1) * 128, _t0:_t0 + TB],
                    in_=hcbh[_fb][:]),
                eng.dma_start(
                    out=hl_h[_f * 128:(_f + 1) * 128, _t0:_t0 + TB],
                    in_=hcbl[_fb][:]),
            ], deps=[e5], dma=True, n_dma=2,
                chan="c5" if fb == 0 else "c6")
            r_hcbh[fb].read(tok_hsp)
            r_hcbl[fb].read(tok_hsp)

    S.flush(block)
    phB2.close()
    S.phase_start()

    if upto == "B2":
        S.wait_only("gpsimd", S.barrier_tokens())
        S.flush(block)
        ctx.close()
        return nc

    phB3 = ExitStack()
    w2h = sb("w2h", [128, (DFF // 128) * D], BF16, stack=phB3)
    w2l = sb("w2l", [128, (DFF // 128) * D], BF16, stack=phB3)
    wstg3 = [sb("wstg3_%d" % i, [128, D // 2], stack=phB3) for i in range(2)]
    wtmp3 = [sb("wtmp3_%d" % i, [128, D // 2], stack=phB3) for i in range(2)]
    hth = sb("hth", [128, (DFF // 128) * TB], BF16, stack=phB3)
    htl = sb("htl", [128, (DFF // 128) * TB], BF16, stack=phB3)
    o1t3h = sb("o1t3h", [128, NCH * TB], BF16, stack=phB3)
    o1t3l = sb("o1t3l", [128, NCH * TB], BF16, stack=phB3)
    z2 = sb("z2", [128, NCH * TB], stack=phB3)
    sqb3 = [sb("sqb3_%d" % i, [128, TB], stack=phB3) for i in range(2)]
    aoutb = z2

    r_ht = RW()
    r_o1t3 = RW()
    r_z2 = RW()
    r_sq3 = [RW(), RW()]
    r_ao = RW()
    r_wstg3 = [RW(), RW()]
    r_wtmp3 = [RW(), RW()]
    r_w2 = RW()

    tok_w2 = None
    HD3 = D // 2
    for i in range(2 * (DFF // 128)):
        ff, hv = i // 2, i % 2
        sg = i % 2
        tok_l = S.do("sync", lambda eng, _f=ff, _h=hv, _s=sg: eng.dma_start(
            out=wstg3[_s][:],
            in_=w2[_f * 128:(_f + 1) * 128, _h * HD3:(_h + 1) * HD3]),
            deps=r_wstg3[sg].wdep(), dma=True, chan="c1")
        r_wstg3[sg].wrote(tok_l)
        tok_s = split3s(
            wstg3[sg][:],
            w2h[:, ff * D + hv * HD3:ff * D + (hv + 1) * HD3],
            w2l[:, ff * D + hv * HD3:ff * D + (hv + 1) * HD3], wtmp3[sg][:],
            [tok_l] + r_wtmp3[sg].wdep() + (r_w2.wdep() if i == 0 else []))
        r_wstg3[sg].read(tok_s)
        r_wtmp3[sg].wrote(tok_s)
        r_w2.wrote(tok_s, reset=(i == 0))
        tok_w2 = tok_s

    for it in range(NTB):
        t0 = it * TB

        tok_h = S.do("sync", lambda eng, _t0=t0: [
            ld3(eng, hth, hh_h, _t0, TB),
            ld3(eng, htl, hl_h, _t0, TB)],
            deps=r_ht.wdep(), dma=True, n_dma=2, chan="c2")
        r_ht.wrote(tok_h)

        tok_o1 = S.do("sync", lambda eng, _t0=t0: [
            ld3(eng, o1t3h, o1h_h, _t0, TB),
            ld3(eng, o1t3l, o1l_h, _t0, TB)],
            deps=r_o1t3.wdep(), dma=True, n_dma=2, chan="c3")
        r_o1t3.wrote(tok_o1)

        z2_toks = []
        for o in range(NCH):
            pi = o % 2
            tok_mm = S.do("tensor", lambda eng, _o=o, _p=pi: [
                eng.matmul(PS[_p][:, 0:TB],
                           wop[:, ff * D + _o * 128: ff * D + (_o + 1) * 128],
                           xop[:, ff * TB:(ff + 1) * TB],
                           start=(pp == 0 and ff == 0),
                           stop=(pp == 2 and ff == DFF // 128 - 1))
                for pp, (wop, xop) in enumerate(
                    ((w2h, hth), (w2h, htl), (w2l, hth)))
                for ff in range(DFF // 128)],
                deps=[tok_w2, tok_h] + PB[pi].wdep(), n_inst=96)
            PB[pi].wrote(tok_mm)
            r_ht.read(tok_mm)
            r_w2.read(tok_mm)

            tz1 = S.do("scalar", lambda eng, _o=o, _p=pi: eng.activation(
                out=z2[:, _o * TB:(_o + 1) * TB], in_=PS[_p][:, 0:TB],
                func=AF.Identity, bias=b2_t[:, _o:_o + 1], scale=1.0),
                deps=[tok_mm] + (r_z2.wdep() if o == 0 else []))
            r_z2.wrote(tz1, reset=(o == 0))
            PB[pi].read(tz1)
            tz2a = S.do("vector", lambda eng, _o=o: eng.tensor_add(
                z2[:, _o * TB:(_o + 1) * TB], z2[:, _o * TB:(_o + 1) * TB],
                o1t3h[:, _o * TB:(_o + 1) * TB]), deps=[tz1, tok_o1])
            r_z2.wrote(tz2a, reset=False)
            r_o1t3.read(tz2a)
            tz2 = S.do("vector", lambda eng, _o=o: eng.tensor_add(
                z2[:, _o * TB:(_o + 1) * TB], z2[:, _o * TB:(_o + 1) * TB],
                o1t3l[:, _o * TB:(_o + 1) * TB]), deps=[tz2a])
            r_z2.wrote(tz2, reset=False)
            r_o1t3.read(tz2)
            z2_toks.append(tz2)

        tok_bmu, tok_brs = ln_stats(z2, r_z2, sqb3, r_sq3, TB,
                                    6, 7, 2, 3, z2_toks, EPS_LN)

        ao_toks = []
        for chh in range(NCH):
            t = ln_norm_chunk(z2, r_z2, chh * TB, z2, r_z2, chh * TB,
                              TB, 2, 3, tok_bmu, tok_brs,
                              g2_t[:, chh:chh + 1], b2l_t[:, chh:chh + 1],
                              first=False)
            ao_toks.append(t)

        tok_aosp = S.do("gpsimd", lambda eng, _t0=t0: st3(
            eng, aout_h, z2, _t0, TB),
            deps=ao_toks, dma=True, n_dma=1, chan="c5")
        r_z2.read(tok_aosp)

    S.flush(block)
    phB3.close()
    S.phase_start()

    if upto == "B3":
        S.wait_only("gpsimd", S.barrier_tokens())
        S.flush(block)
        ctx.close()
        return nc

    TC = 512
    NTC = TPC // TC

    def swiglu_proj_phase(wsrc, out_hbm, silu_mul):
        ph = ExitStack()
        wf_t = sb("wf_t_%d" % silu_mul, [128, NCH * HID], BF16, stack=ph)
        wi_t = [sb("wi_t%d_%d" % (i, silu_mul), [128, HID], I32, stack=ph)
                for i in range(2)]
        xnch = [sb("xnch%d_%d" % (i, silu_mul), [128, NCH * TC], BF16,
                   stack=ph) for i in range(2)]
        xncl = [sb("xncl%d_%d" % (i, silu_mul), [128, NCH * TC], BF16,
                   stack=ph) for i in range(2)]
        gc = [sb("gc%d_%d" % (i, silu_mul), [128, TC], stack=ph)
              for i in range(2)]
        gch = [sb("gch%d_%d" % (i, silu_mul), [128, TC], BF16, stack=ph)
               for i in range(2)]
        gcl = [sb("gcl%d_%d" % (i, silu_mul), [128, TC], BF16, stack=ph)
               for i in range(2)]
        gtmp = [sb("gtmp%d_%d" % (i, silu_mul), [128, TC], stack=ph)
                for i in range(2)]
        sgc = [sb("sgc%d_%d" % (i, silu_mul), [128, TC], stack=ph)
               for i in range(2)]
        gst = None
        if silu_mul:
            gst = [sb("gst0", [128, NHC * TC], stack=ph)]
        r_wf = RW()
        r_wi = [RW(), RW()]
        r_xnc = [RW(), RW()]
        r_gc = [RW(), RW()]
        r_gch = [RW(), RW()]
        r_gcl = [RW(), RW()]
        r_gtmp = [RW(), RW()]
        r_sgc = [RW(), RW()]
        r_gst = [RW(), RW()]

        cvt_toks = []
        for dd in range(NCH):
            sg = dd % 2
            tok_l = S.do("sync", lambda eng, _d=dd, _s=sg: eng.dma_start(
                out=wi_t[_s][:], in_=wsrc[_d * 128:(_d + 1) * 128, :]),
                deps=r_wi[sg].wdep(), dma=True, chan="c1")
            r_wi[sg].wrote(tok_l)
            tok_cv = S.do("vector", lambda eng, _d=dd, _s=sg: eng.tensor_copy(
                out=wf_t[:, _d * HID:(_d + 1) * HID], in_=wi_t[_s][:]),
                deps=[tok_l])
            r_wi[sg].read(tok_cv)
            cvt_toks.append(tok_cv)
            r_wf.wrote(tok_cv, reset=(dd == 0))

        for it in range(NTC):
            t0 = it * TC
            cb = it % 2

            tok_xn = S.do("sync", lambda eng, _t0=t0, _c=cb: [
                ld3(eng, xnch[_c], xnh_h, _t0, TC),
                ld3(eng, xncl[_c], xnl_h, _t0, TC)],
                deps=r_xnc[cb].wdep(), dma=True, n_dma=2, chan="c3")
            r_xnc[cb].wrote(tok_xn)

            tok_g = None
            if silu_mul:
                tok_g = S.do("sync", lambda eng, _t0=t0: ld3(
                    eng, gst[0], gate_h, _t0, TC), deps=r_gst[0].wdep(),
                    dma=True, n_dma=1, chan="c2")
                r_gst[0].wrote(tok_g)

            for hc in range(NHC):
                pi = hc % 4
                fb = hc % 2
                tok_mm = S.do("tensor", lambda eng, _h=hc, _p=pi, _c=cb: [
                    eng.matmul(PS[_p][:, 0:TC],
                               wf_t[:, dd * HID + _h * 128:
                                    dd * HID + (_h + 1) * 128],
                               xop[:, dd * TC:(dd + 1) * TC],
                               start=(pp == 0 and dd == 0),
                               stop=(pp == 1 and dd == NCH - 1))
                    for pp, xop in enumerate((xnch[_c], xncl[_c]))
                    for dd in range(NCH)],
                    deps=cvt_toks + [tok_xn] + PB[pi].wdep(), n_inst=16)
                PB[pi].wrote(tok_mm)
                r_xnc[cb].read(tok_mm)
                r_wf.read(tok_mm)

                if not silu_mul:
                    tok_ev = S.do("scalar", lambda eng, _p=pi, _fb=fb:
                                  eng.activation(
                        out=gc[_fb][:], in_=PS[_p][:, 0:TC], func=AF.Identity),
                        deps=[tok_mm] + r_gc[fb].wdep())
                    r_gc[fb].wrote(tok_ev)
                    PB[pi].read(tok_ev)
                    tok_spg = S.do("gpsimd", lambda eng, _h=hc, _t0=t0,
                                   _fb=fb: eng.dma_start(
                        out=out_hbm[_h * 128:(_h + 1) * 128, _t0:_t0 + TC],
                        in_=gc[_fb][:]), deps=[tok_ev], dma=True,
                        chan="c5" if fb == 0 else "c6")
                    r_gc[fb].read(tok_spg)
                else:
                    tok_si = S.do("scalar", lambda eng, _h=hc, _fb=fb:
                                  eng.activation(
                        out=sgc[_fb][:], in_=gst[0][:, _h * TC:(_h + 1) * TC],
                        func=AF.Silu, scale=SCALE_Q),
                        deps=[tok_g] + r_sgc[fb].wdep())
                    r_sgc[fb].wrote(tok_si)
                    r_gst[0].read(tok_si)
                    tok_ev = S.do("vector", lambda eng, _p=pi, _fb=fb:
                                  eng.tensor_mul(gc[_fb][:], sgc[_fb][:],
                                                 PS[_p][:, 0:TC]),
                        deps=[tok_si, tok_mm] + r_gc[fb].wdep())
                    r_gc[fb].wrote(tok_ev)
                    r_sgc[fb].read(tok_ev)
                    PB[pi].read(tok_ev)
                    tok_sp2 = split3s(gc[fb][:], gch[fb][:], gcl[fb][:],
                                      gtmp[fb][:],
                                      [tok_ev] + r_gch[fb].wdep() +
                                      r_gcl[fb].wdep() + r_gtmp[fb].wdep())
                    r_gc[fb].read(tok_sp2)
                    r_gtmp[fb].wrote(tok_sp2)
                    r_gch[fb].wrote(tok_sp2)
                    r_gcl[fb].wrote(tok_sp2)
                    tok_spg = S.do("gpsimd", lambda eng, _h=hc, _t0=t0,
                                   _fb=fb: [
                        eng.dma_start(
                            out=hsh_h[_h * 128:(_h + 1) * 128, _t0:_t0 + TC],
                            in_=gch[_fb][:]),
                        eng.dma_start(
                            out=hsl_h[_h * 128:(_h + 1) * 128, _t0:_t0 + TC],
                            in_=gcl[_fb][:]),
                    ], deps=[tok_sp2], dma=True, n_dma=2,
                        chan="c5" if fb == 0 else "c6")
                    r_gch[fb].read(tok_spg)
                    r_gcl[fb].read(tok_spg)

        S.flush(block)
        ph.close()
        S.phase_start()

    swiglu_proj_phase(wgT, gate_h, False)
    if upto == "C1":
        S.wait_only("gpsimd", S.barrier_tokens())
        S.flush(block)
        ctx.close()
        return nc
    swiglu_proj_phase(wuT, None, True)
    if upto == "C1b":
        S.wait_only("gpsimd", S.barrier_tokens())
        S.flush(block)
        ctx.close()
        return nc

    phC2 = ExitStack()
    wdf_t = sb("wdf_t", [128, NHC * D], BF16, stack=phC2)
    wdi_t = [sb("wdi_t%d" % i, [128, D], I32, stack=phC2) for i in range(2)]
    hsth = [sb("hsth%d" % i, [128, NHC * TC], BF16, stack=phC2)
            for i in range(2)]
    hstl = [sb("hstl%d" % i, [128, NHC * TC], BF16, stack=phC2)
            for i in range(2)]
    xt2 = sb("xt2", [128, NCH * TC], stack=phC2)
    aot = [sb("aot%d" % i, [128, TC], stack=phC2) for i in range(2)]
    yb = [sb("yb%d" % i, [128, TC], stack=phC2) for i in range(2)]

    r_wdf = RW()
    r_wdi = [RW(), RW()]
    r_hst = [RW(), RW()]
    r_xt2 = RW()
    r_aot = [RW(), RW()]
    r_yb = [RW(), RW()]

    cvt_toks = []
    for hcg in range(NHC):
        sg = hcg % 2
        tok_l = S.do("sync", lambda eng, _h=hcg, _s=sg: eng.dma_start(
            out=wdi_t[_s][:], in_=wdT[_h * 128:(_h + 1) * 128, :]),
            deps=r_wdi[sg].wdep(), dma=True, chan="c1")
        r_wdi[sg].wrote(tok_l)
        tok_cv = S.do("vector", lambda eng, _h=hcg, _s=sg: eng.tensor_copy(
            out=wdf_t[:, _h * D:(_h + 1) * D], in_=wdi_t[_s][:]),
            deps=[tok_l])
        r_wdi[sg].read(tok_cv)
        cvt_toks.append(tok_cv)
        r_wdf.wrote(tok_cv, reset=(hcg == 0))

    for it in range(NTC):
        t0 = it * TC
        cb = it % 2

        tok_hs = S.do("sync", lambda eng, _t0=t0, _c=cb: [
            ld3(eng, hsth[_c], hsh_h, _t0, TC),
            ld3(eng, hstl[_c], hsl_h, _t0, TC)],
            deps=r_hst[cb].wdep(), dma=True, n_dma=2, chan="c2")
        r_hst[cb].wrote(tok_hs)

        tok_x = S.do("sync", lambda eng, _t0=t0: ld3(
            eng, xt2, xT, _t0, TC), deps=r_xt2.wdep(), dma=True, n_dma=1,
            chan="c3")
        r_xt2.wrote(tok_x)



        for o in range(NCH):
            pi = o % 2
            ab = o % 2
            tok_ao = S.do("sync", lambda eng, _t0=t0, _o=o, _a=ab:
                          eng.dma_start(
                out=aot[_a][:],
                in_=aout_h[_o * 128:(_o + 1) * 128, _t0:_t0 + TC]),
                deps=r_aot[ab].wdep(), dma=True, n_dma=1, chan="c4")
            r_aot[ab].wrote(tok_ao)
            tok_mm = S.do("tensor", lambda eng, _o=o, _p=pi, _c=cb: [
                eng.matmul(PS[_p][:, 0:TC],
                           wdf_t[:, hc * D + _o * 128: hc * D + (_o + 1) * 128],
                           xop[:, hc * TC:(hc + 1) * TC],
                           start=(pp == 0 and hc == 0),
                           stop=(pp == 1 and hc == NHC - 1))
                for pp, xop in enumerate((hsth[_c], hstl[_c]))
                for hc in range(NHC)],
                deps=cvt_toks + [tok_hs] + PB[pi].wdep(), n_inst=48)
            PB[pi].wrote(tok_mm)
            r_hst[cb].read(tok_mm)
            r_wdf.read(tok_mm)

            ybi = o % 2
            y1 = S.do("vector", lambda eng, _p=pi, _y=ybi: eng.tensor_scalar(
                out=yb[_y][:], in0=PS[_p][:, 0:TC], scalar1=SCALE_Q2,
                scalar2=None, op0=ALU.mult),
                deps=[tok_mm] + r_yb[ybi].wdep())
            r_yb[ybi].wrote(y1)
            PB[pi].read(y1)
            y2 = S.do("vector", lambda eng, _o=o, _y=ybi: eng.tensor_add(
                yb[_y][:], yb[_y][:], xt2[:, _o * TC:(_o + 1) * TC]),
                deps=[y1, tok_x])
            r_yb[ybi].wrote(y2, reset=False)
            r_xt2.read(y2)
            y3 = S.do("vector", lambda eng, _o=o, _y=ybi, _a=ab:
                      eng.tensor_add(
                yb[_y][:], yb[_y][:], aot[_a][:]),
                deps=[y2, tok_ao])
            r_yb[ybi].wrote(y3, reset=False)
            r_aot[ab].read(y3)

            tok_st = S.do("gpsimd", lambda eng, _o=o, _t0=t0, _y=ybi:
                          eng.dma_start(
                out=yT[_o * 128:(_o + 1) * 128, _t0:_t0 + TC],
                in_=yb[_y][:]), deps=[y3], dma=True, chan="c5")
            r_yb[ybi].read(tok_st)

    S.wait_only("gpsimd", S.barrier_tokens())
    S.flush(block)
    phC2.close()

    ctx.close()
    return nc


def make_inputs(inputs):
    x = np.asarray(inputs["L0_attn_x"])
    Wq = np.asarray(inputs["L0_attn_Wq"]).reshape(D, D)
    Wk = np.asarray(inputs["L0_attn_Wk"]).reshape(D, D)
    Wv = np.asarray(inputs["L0_attn_Wv"]).reshape(D, D)
    Wo = np.asarray(inputs["L0_attn_Wo"]).reshape(D, D)
    proj = np.asarray(inputs["L0_attn_proj"])
    W1 = np.asarray(inputs["L0_attn_W1"])
    b1 = np.asarray(inputs["L0_attn_b1"])
    W2 = np.asarray(inputs["L0_attn_W2"])
    b2 = np.asarray(inputs["L0_attn_b2"])
    g1 = np.asarray(inputs["L0_attn_ln1_g"])
    b1l = np.asarray(inputs["L0_attn_ln1_b"])
    g2 = np.asarray(inputs["L0_attn_ln2_g"])
    b2l = np.asarray(inputs["L0_attn_ln2_b"])
    wg = np.asarray(inputs["L0_ffn_w_gate"])
    wu = np.asarray(inputs["L0_ffn_w_up"])
    wd = np.asarray(inputs["L0_ffn_w_down"])

    projbd = np.zeros((128, 128), np.float32)
    projbd[0:64, 0:64] = proj.T
    projbd[64:128, 64:128] = proj.T

    eye = np.eye(128, dtype=np.float32)
    ones1r = np.ones((1, 128), np.float32)
    avgc = np.full((128, 1), np.float32(1.0 / 1024.0), np.float32)
    blockneg = np.zeros((128, 128), np.float32)
    for chh in range(NCH):
        blockneg[0:64, chh * 16 + 2 * chh] = np.float32(-CDIAG)
        blockneg[64:128, chh * 16 + 2 * chh + 1] = np.float32(-CDIAG)
    maskeo = np.zeros((1, 256), np.float32)
    maskeo[0, 0:64] = 1.0
    maskeo[0, 192:256] = 1.0

    shared = dict(
        wq=np.ascontiguousarray(Wq), wk=np.ascontiguousarray(Wk),
        wv=np.ascontiguousarray(Wv), wo=np.ascontiguousarray(Wo),
        projbd=projbd, w1=np.ascontiguousarray(W1),
        w2=np.ascontiguousarray(W2),
        b1c=np.ascontiguousarray(b1.reshape(DFF // 128, 128).T),
        b2c=np.ascontiguousarray(b2.reshape(NCH, 128).T),
        g1c=np.ascontiguousarray(g1.reshape(NCH, 128).T),
        b1lc=np.ascontiguousarray(b1l.reshape(NCH, 128).T),
        g2c=np.ascontiguousarray(g2.reshape(NCH, 128).T),
        b2lc=np.ascontiguousarray(b2l.reshape(NCH, 128).T),
        wgT=np.ascontiguousarray(wg.T), wuT=np.ascontiguousarray(wu.T),
        wdT=np.ascontiguousarray(wd.T),
        eye=eye, ones1r=ones1r, avgc=avgc, blockneg=blockneg, maskeo=maskeo,
    )
    in_maps = []
    for c in range(NCORES):
        m = dict(shared)
        m["xT"] = np.ascontiguousarray(x[2 * c:2 * c + 2].reshape(TPC, D).T)
        in_maps.append(m)
    return in_maps


def kernel(**inputs):
    in_maps = make_inputs(inputs)
    nc = build_program()
    res = run_bass_kernel_spmd(nc, in_maps, core_ids=list(range(NCORES)))
    outs = []
    for c in range(NCORES):
        yc = np.asarray(res.results[c]["yT"])
        outs.append(np.ascontiguousarray(yc.T).reshape(2, N, D))
    return np.concatenate(outs, axis=0).astype(np.float32)
